# version 24
# speedup vs baseline: 1.2412x; 1.0280x over previous
import numpy as np
import ml_dtypes
from contextlib import ExitStack

import concourse.bass as bass
import concourse.mybir as mybir
from concourse.bass_utils import run_bass_kernel_spmd

F32 = mybir.dt.float32
BF16 = mybir.dt.bfloat16
AF = mybir.ActivationFunctionType
ALU = mybir.AluOpType
AX = mybir.AxisListType.X

NCORES = 8
D = 1024
H0, KD0, HD0 = 16, 64, 128
DV0 = 2048
DFF = 4096
H1, DH1 = 16, 64
E, HMOE = 8, 2048
EPS_LN = 1e-5

ENGINES = ("sync", "tensor", "vector", "scalar", "gpsimd")

RND = 12582912.0



def host_tables(T, B=2):
    NT = B * T
    CAP = NT // 2
    f64 = np.float64
    half = 1.0 / np.power(10000.0, np.linspace(0.0, 1.0, KD0 // 2))
    angle = np.repeat(half, 2)
    idx = np.arange(T, dtype=f64)
    cos0 = np.cos(idx[:, None] * angle[None, :])
    sin0 = np.sin(idx[:, None] * angle[None, :])
    decay = np.log(1.0 - 2.0 ** (-5.0 - np.arange(H0, dtype=f64)))
    ex = np.exp(idx[None, :] * decay[:, None])
    S = np.cumsum(ex, axis=1)
    emq = ex / np.sqrt(S)
    enk = np.exp(-idx[None, :] * decay[:, None]) * (KD0 ** -0.5)
    cosq = np.empty((D, T), f64); sinq = np.empty((D, T), f64)
    cosk = np.empty((D, T), f64); sink = np.empty((D, T), f64)
    for h in range(H0):
        r = slice(KD0 * h, KD0 * h + KD0)
        cosq[r] = cos0.T * emq[h][None, :]
        sinq[r] = sin0.T * emq[h][None, :]
        cosk[r] = cos0.T * enk[h][None, :]
        sink[r] = sin0.T * enk[h][None, :]
    inv_freq = 1.0 / 10000.0 ** (np.arange(0, DH1, 2, dtype=f64) / DH1)
    ang = idx[:, None] * inv_freq[None, :]
    cos1 = np.concatenate([np.cos(ang), np.cos(ang)], axis=1).T
    sin1 = np.concatenate([np.sin(ang), np.sin(ang)], axis=1).T
    cos1e = np.concatenate([cos1, cos1], axis=0)
    sin1e = np.concatenate([sin1, sin1], axis=0)
    R0T = np.zeros((128, 128), f64)
    for i in range(64):
        R0T[2 * i + 1, 2 * i] = -1.0
        R0T[2 * i, 2 * i + 1] = 1.0
    R1T = np.zeros((128, 128), f64)
    for b in (0, 64):
        for i in range(32):
            R1T[b + i + 32, b + i] = -1.0
            R1T[b + i, b + i + 32] = 1.0
    triu = np.triu(np.ones((128, 128), f64))
    bf = ml_dtypes.bfloat16
    return {
        "cosq": cosq.astype(np.float32), "sinq": sinq.astype(np.float32),
        "cosk": cosk.astype(np.float32), "sink": sink.astype(np.float32),
        "cos1": cos1e.astype(np.float32), "sin1": sin1e.astype(np.float32),
        "R0T": R0T.astype(bf), "R1T": R1T.astype(bf),
        "triu": triu.astype(np.float32),
        "identB": np.eye(128, dtype=bf),
        "identF": np.eye(128, dtype=np.float32),
        "onesB": np.ones((1, 128), dtype=bf),
        "eCm1": np.tile((np.arange(E) * CAP - 1.0).astype(np.float32),
                        (128, 1)),
        "iota_tb": (np.arange(128)[:, None] +
                    128 * np.arange(NT // 128)[None, :]).astype(np.int32),
    }


def host_inputs(core, full):
    T = 2048
    x = np.ascontiguousarray(
        full["L0_attn_x"].reshape(16 * T, D)[core * 2 * T:(core + 1) * 2 * T])
    tabs = host_tables(T)
    c = np.ascontiguousarray
    m = {
        "x": x,
        "L0_attn_q_w": full["L0_attn_q_w"], "L0_attn_k_w": full["L0_attn_k_w"],
        "L0_attn_v_w": full["L0_attn_v_w"], "L0_attn_g_w": full["L0_attn_g_w"],
        "L0_attn_o_w": full["L0_attn_o_w"],
        "L0_ffn_W1": full["L0_ffn_W1"], "L0_ffn_W2": full["L0_ffn_W2"],
        "L1_attn_W_qkv": full["L1_attn_W_qkv"],
        "L1_ffn_Wr": full["L1_ffn_Wr"], "L1_ffn_W1": full["L1_ffn_W1"],
        "L1_ffn_W2": full["L1_ffn_W2"],
        "qb0_pp": c(full["L0_attn_q_b"].reshape(8, 128).T),
        "kb0_pp": c(full["L0_attn_k_b"].reshape(8, 128).T),
        "b1ff_pp": c(full["L0_ffn_b1"].reshape(32, 128).T),
        "qkb1_pp": c(full["L1_attn_b_qkv"][:2048].reshape(16, 128).T),
        "b1moe_pp": c(np.stack([full["L1_ffn_b1"][e].reshape(16, 128).T
                                for e in range(E)], 1).reshape(128, E * 16)),
        "br_pp": c(full["L1_ffn_br"].reshape(E, 1)),
        "vb0_row": c(full["L0_attn_v_b"].reshape(1, DV0)),
        "gb0_row": c(full["L0_attn_g_b"].reshape(1, DV0)),
        "ob0_row": c(full["L0_attn_o_b"].reshape(1, D)),
        "b2ff_row": c(full["L0_ffn_b2"].reshape(1, D)),
        "vb1_row": c(full["L1_attn_b_qkv"][2048:].reshape(1, D)),
        "b2moe": full["L1_ffn_b2"],
    }
    m.update(tabs)
    return m



class Sem:
    __slots__ = ("h", "n")

    def __init__(self, h):
        self.h = h
        self.n = 0

    def tok(self):
        return (self, self.n)


class _PhaseCtx:
    def __init__(self, mk, name):
        self.mk = mk
        self.name = name
        self.stack = ExitStack()

    def __enter__(self):
        self.mk.streams = {e: [] for e in ENGINES}
        self.mk.op_seq = 0
        g = self.mk.current_group
        if g is not None:
            idx = g.next_idx
            g.next_idx += 1
            self.saved_ch = self.mk.ch
            self.mk.ch = {e: self.mk.sem(f"gch{idx}_{e}")
                          for e in ENGINES}
            return g.stack
        return self.stack.__enter__()

    def __exit__(self, *exc):
        g = self.mk.current_group
        if g is not None:
            if exc[0] is None:
                g.members.append(self.mk.streams)
            self.mk.ch = self.saved_ch
            self.mk.streams = None
            return False
        if exc[0] is None:
            self.mk.barrier()
            with self.mk.nc.named_scope(self.name):
                self.mk.replay()
        self.stack.__exit__(*exc)
        self.mk.streams = None
        return False


class _GroupCtx:
    def __init__(self, mk, name):
        self.mk = mk
        self.name = name
        self.stack = ExitStack()
        self.members = []
        self.next_idx = 0

    def __enter__(self):
        self.stack.__enter__()
        self.mk.current_group = self
        return self

    def __exit__(self, *exc):
        self.mk.current_group = None
        if exc[0] is None:
            totals = [max(op[0] for e in ENGINES for op in m[e])
                      for m in self.members]
            merged = {e: [] for e in ENGINES}
            for e in ENGINES:
                items = []
                for j, m in enumerate(self.members):
                    for seq, fn in m[e]:
                        items.append((seq / totals[j], j, seq, fn))
                items.sort(key=lambda t: (t[0], t[1], t[2]))
                merged[e] = [(t[2], t[3]) for t in items]
            self.mk.streams = merged
            self.mk.barrier()
            with self.mk.nc.named_scope(self.name):
                self.mk.replay()
            self.mk.streams = None
        self.stack.__exit__(*exc)
        return False


class MK:
    def __init__(self, B=2, T=2048, debug=False, upto=99):
        self.B, self.T = B, T
        self.NT = B * T
        self.NTB = self.NT // 128
        self.TB = T // 128
        self.debug = debug
        self.upto = upto
        self.nc = bass.Bass("TRN2", target_bir_lowering=False)
        self.ctx = ExitStack()
        self.sems = {}
        self.ch = {e: self.sem(f"ch_{e}") for e in ENGINES}
        self.streams = None
        self.current_group = None
        self.op_seq = 0

    def sem(self, name):
        if name not in self.sems:
            h = self.ctx.enter_context(self.nc.semaphore(name))
            self.sems[name] = Sem(h)
        return self.sems[name]

    def psem(self, i, swdge=False):
        g = self.current_group
        pid = (g.next_idx - 1) if g is not None else 0
        pre = "psg" if swdge else "ps"
        return self.sem(f"{pre}{pid}_{i}")

    def _rec(self, eng, method, args, kw, wait, inc):
        flat = []
        for t in wait:
            if t is None:
                continue
            if isinstance(t, list) or (isinstance(t, tuple) and t and
                                       isinstance(t[0], (tuple, list))):
                flat.extend(x for x in t if x is not None)
            else:
                flat.append(t)
        wait = tuple((t[0].h, t[1]) for t in flat)
        inc = tuple((s.h, v) for (s, v) in inc)
        import sys as _sys
        fr = _sys._getframe(2)
        tag = f"{fr.f_code.co_name}:{fr.f_lineno}"

        def fn(e, method=method, args=args, kw=kw, waits=wait, incs=inc,
               tag=tag):
            for h, v in waits:
                e.wait_ge(h, v)
            ins = getattr(e, method)(*args, **kw)
            ins.annotate(tag)
            for h, v in incs:
                ins.then_inc(h, v)
        self.op_seq += 1
        self.streams[eng].append((self.op_seq, fn))

    def op(self, eng, method, *args, wait=(), mark=False, **kw):
        inc = ()
        tok = None
        if eng in ("vector", "scalar", "gpsimd"):
            c = self.ch[eng]
            wait = tuple(wait) + ((c, c.n),)
            c.n += 1
            tok = (c, c.n)
            inc = ((c, 1),)
        elif mark:
            c = self.ch[eng]
            c.n += 1
            tok = (c, c.n)
            inc = ((c, 1),)
        self._rec(eng, method, args, kw, wait, inc)
        return tok

    def dma(self, eng, out, in_, sem, wait=(), indirect_out=None,
            indirect_in=None, **kw):
        sem.n += 16
        if indirect_out is not None or indirect_in is not None:
            oo = (bass.IndirectOffsetOnAxis(indirect_out, 0)
                  if indirect_out is not None else None)
            io = (bass.IndirectOffsetOnAxis(indirect_in, 0)
                  if indirect_in is not None else None)
            self._rec(eng, "indirect_dma_start", (),
                      dict(out=out, out_offset=oo, in_=in_, in_offset=io,
                           **kw), wait, ((sem, 16),))
        else:
            self._rec(eng, "dma_start", (), dict(out=out, in_=in_, **kw),
                      wait, ((sem, 16),))
        return (sem, sem.n)

    def phase(self):
        import sys as _sys
        name = _sys._getframe(1).f_code.co_name
        self.phase_count = getattr(self, "phase_count", 0) + 1
        return _PhaseCtx(self, f"{self.phase_count:02d}_{name}")

    def group(self, name):
        self.phase_count = getattr(self, "phase_count", 0) + 1
        return _GroupCtx(self, f"{self.phase_count:02d}_{name}")

    def barrier(self):
        dma_toks = tuple((s, s.n) for nm, s in self.sems.items()
                         if not nm.startswith("ch_") and s.n > 0
                         and nm != "s_zfill")
        for e in ENGINES:
            self._rec(e, "nop", (), {}, dma_toks, ())
        return

    def replay(self):
        streams = self.streams
        with self.nc.Block() as block:
            @block.sync
            def _(e):
                for _, fn in streams["sync"]:
                    fn(e)

            @block.tensor
            def _(e):
                for _, fn in streams["tensor"]:
                    fn(e)

            @block.vector
            def _(e):
                for _, fn in streams["vector"]:
                    fn(e)

            @block.scalar
            def _(e):
                for _, fn in streams["scalar"]:
                    fn(e)

            @block.gpsimd
            def _(e):
                for _, fn in streams["gpsimd"]:
                    fn(e)

    def scratch(self, name, shape, dtype):
        kind = "ExternalOutput" if self.debug else "Internal"
        return self.nc.dram_tensor(name, shape, dtype, kind=kind).ap()

    def build(self):
        nc = self.nc
        NT, T = self.NT, self.T
        inp = {}
        inp["x"] = nc.dram_tensor("x", [NT, D], F32, kind="ExternalInput").ap()
        for w, shape in [
            ("L0_attn_q_w", [D, D]), ("L0_attn_k_w", [D, D]),
            ("L0_attn_v_w", [D, DV0]), ("L0_attn_g_w", [D, DV0]),
            ("L0_attn_o_w", [DV0, D]),
            ("L0_ffn_W1", [DFF, D]), ("L0_ffn_W2", [D, DFF]),
            ("L1_attn_W_qkv", [3 * D, D]),
            ("L1_ffn_Wr", [D, E]), ("L1_ffn_W1", [E, D, HMOE]),
            ("L1_ffn_W2", [E, HMOE, D]),
            ("qb0_pp", [128, 8]), ("kb0_pp", [128, 8]),
            ("b1ff_pp", [128, 32]), ("qkb1_pp", [128, 16]),
            ("b1moe_pp", [128, E * 16]), ("br_pp", [E, 1]),
            ("vb0_row", [1, DV0]), ("gb0_row", [1, DV0]),
            ("ob0_row", [1, D]), ("b2ff_row", [1, D]),
            ("vb1_row", [1, D]), ("b2moe", [E, D]),
        ]:
            inp[w] = nc.dram_tensor(w, shape, F32, kind="ExternalInput").ap()
        for w, shape, dt in [
            ("cosq", [D, T], F32), ("sinq", [D, T], F32),
            ("cosk", [D, T], F32), ("sink", [D, T], F32),
            ("cos1", [128, T], F32), ("sin1", [128, T], F32),
            ("R0T", [128, 128], BF16), ("R1T", [128, 128], BF16),
            ("triu", [128, 128], F32),
            ("identB", [128, 128], BF16), ("identF", [128, 128], F32),
            ("onesB", [1, 128], BF16), ("eCm1", [128, E], F32),
        ]:
            inp[w] = nc.dram_tensor(w, shape, dt, kind="ExternalInput").ap()
        inp["iota_tb"] = nc.dram_tensor("iota_tb", [128, NT // 128],
                                        mybir.dt.int32,
                                        kind="ExternalInput").ap()
        self.inp = inp
        self.y_out = nc.dram_tensor("y", [NT, D], F32,
                                    kind="ExternalOutput").ap()

        self.qsc = self.scratch("qsc", [D, NT], BF16)
        self.ksc = self.scratch("ksc", [D, NT], BF16)
        self.ksctm = self.scratch("ksctm", [NT, D], BF16)
        self.vsc = self.scratch("vsc", [NT, DV0], BF16)
        self.gsc = self.scratch("gsc", [NT, DV0], BF16)
        self.w1qt = self.scratch("w1qt", [D, DFF], BF16)
        self.w2qt = self.scratch("w2qt", [DFF, D], BF16)
        self.ogf = self.scratch("ogf", [DV0, NT], BF16)
        self.g20 = self.scratch("g20", [NT, D], F32)
        self.x1s = self.scratch("x1s", [NT, D], F32)
        self.qr1 = self.scratch("qr1", [D, NT], BF16)
        self.kr1tm = self.scratch("kr1tm", [NT, D], BF16)
        self.kr1 = self.scratch("kr1", [D, NT], BF16)
        self.v1s = self.scratch("v1s", [NT, D], BF16)
        self.s1 = self.scratch("s1", [NT, D], F32)
        self.CAP = NT // 2
        self.xn1tm = self.scratch("xn1tm", [NT, D], BF16)
        self.xnsort = self.scratch("xnsort", [E * self.CAP, D], BF16)
        self.ysort = self.scratch("ysort", [E * self.CAP, D], F32)

        al = self.ctx.enter_context
        nc_ = nc
        self.xn_ctx = ExitStack()
        self.xn_fm = self.xn_ctx.enter_context(
            nc_.sbuf_tensor("xn_fm", [128, 8 * NT], BF16, side="right"))
        self.xnv = self.xn_fm[:, :].rearrange("p (c t) -> p c t", c=8)
        for nm, shape, dt in [
            ("identB", [128, 128], BF16), ("identF", [128, 128], F32),
            ("onesB", [1, 128], BF16), ("R0T", [128, 128], BF16),
            ("R1T", [128, 128], BF16), ("triu", [128, 128], F32),
            ("qb0", [128, 8], F32), ("kb0", [128, 8], F32),
            ("b1ff", [128, 32], F32), ("qkb1", [128, 16], F32),
            ("b1moe", [128, E * 16], F32), ("br", [E, 1], F32),
            ("ob0", [1, D], BF16), ("b2ff", [1, D], BF16),
            ("vb1", [1, D], BF16), ("b2moe", [E, D], BF16),
            ("mp", [128, self.NTB], F32), ("ind", [128, self.NTB * E], F32),
            ("eps", [128, 4], F32), ("rs", [128, 2], F32),
            ("eCm1", [128, E], F32),
            ("slots", [128, self.NTB], mybir.dt.int32),
            ("iota", [128, self.NTB], mybir.dt.int32),
            ("onesF", [128, 1], F32), ("onesFr", [1, 128], F32),
        ]:
            setattr(self, "c_" + nm, al(nc_.sbuf_tensor("c_" + nm, shape, dt)))

        self.p_const()
        if self.upto >= 1:
            self.p_ln(self.inp["x"], src_sem="p1x", dbg="xn0")
        if self.upto >= 2:
            self.p_proj_qk("q0", self.inp["L0_attn_q_w"], self.c_qb0[:, 0:8],
                           inp["cosq"], inp["sinq"], self.c_R0T, self.qsc)
            self.p_proj_qk("k0", self.inp["L0_attn_k_w"], self.c_kb0[:, 0:8],
                           inp["cosk"], inp["sink"], self.c_R0T, self.ksc,
                           tm_out=self.ksctm)
            with self.group("projtm_qgamma"):
                self.p_proj_tm("v0", self.inp["L0_attn_v_w"], DV0,
                               self.inp["vb0_row"], self.vsc, act=None)
                self.p_proj_tm("g0", self.inp["L0_attn_g_w"], DV0,
                               self.inp["gb0_row"], self.gsc, act=AF.Silu)
                if self.upto >= 3:
                    self.p_quant_gamma()
        if self.upto >= 5:
            with self.group("attn0_qapply"):
                self.p_attn0()
                if self.upto >= 3:
                    self.p_quant_apply()
        elif self.upto >= 3:
            self.p_quant_apply()
        if self.upto >= 4:
            self.p_ffn0()
        if self.upto >= 6:
            self.p_oproj_resid0()
        if self.upto >= 8 and self.upto >= 10:
            with self.group("qkv1_router"):
                self.p_qkv1()
                self.p_router()
        else:
            if self.upto >= 8:
                self.p_qkv1()
            if self.upto >= 10:
                self.p_router()
        self.xn_ctx.close()
        if self.upto >= 9 and self.upto >= 11:
            with self.group("moe_attn1"):
                self.p_attn1()
                self.p_moe()
        elif self.upto >= 9:
            self.p_attn1()
        if self.upto >= 12:
            self.p_resid1()
        self.finalize()
        self.ctx.close()
        return self.nc

    def p_const(self):
        inp = self.inp
        s_ld = self.sem("s_const")
        s_ldg = self.sem("s_constg")
        with self.phase():
            for src, dst in [
                ("identB", self.c_identB), ("identF", self.c_identF),
                ("onesB", self.c_onesB), ("R0T", self.c_R0T),
                ("R1T", self.c_R1T), ("triu", self.c_triu),
                ("qb0_pp", self.c_qb0), ("kb0_pp", self.c_kb0),
                ("b1ff_pp", self.c_b1ff), ("qkb1_pp", self.c_qkb1),
                ("b1moe_pp", self.c_b1moe), ("br_pp", self.c_br),
                ("eCm1", self.c_eCm1), ("iota_tb", self.c_iota),
            ]:
                self.dma("sync", dst[:, :], inp[src], s_ld)
            for src, dst in [
                ("ob0_row", self.c_ob0), ("b2ff_row", self.c_b2ff),
                ("vb1_row", self.c_vb1), ("b2moe", self.c_b2moe),
            ]:
                self.dma("gpsimd", dst[:, :], inp[src], s_ldg)
            self.op("vector", "memset", self.c_eps[:, 0:1], EPS_LN)
            self.op("vector", "memset", self.c_eps[:, 1:2], RND)
            self.op("vector", "memset", self.c_eps[:, 2:3], -RND)
            self.op("vector", "memset", self.c_eps[:, 3:4], 0.0)
            self.op("vector", "memset", self.c_onesF[:, :], 1.0)
            self.op("vector", "memset", self.c_onesFr[:, :], 1.0)

    def p_ln(self, x_ap, src_sem="p1x", dbg=None, resid=None):
        nc = self.nc
        NTB = self.NTB
        GRP = 4 if resid is None else 2
        dbg_ap = (self.scratch("dbg_" + dbg, [self.NT, D], BF16)
                  if (self.debug and dbg) else None)
        with self.phase() as ctx:
            al = ctx.enter_context
            pre = src_sem
            xt = [al(nc.sbuf_tensor(f"{pre}ln_x{i}", [128, GRP * D], F32))
                  for i in range(2)]
            if resid is not None:
                st_b = [al(nc.sbuf_tensor(f"{pre}ln_s{i}", [128, GRP * D], F32))
                        for i in range(2)]
                gt_b = [al(nc.sbuf_tensor(f"{pre}ln_g{i}", [128, GRP * D], F32))
                        for i in range(2)]
                x1t = al(nc.sbuf_tensor(pre + "ln_x1t", [128, D], F32))
                s_s = [self.psem(2), self.psem(3)]
                s_g = [self.psem(4), self.psem(5)]
                s_x1 = self.psem(6)
            xn_tm = al(nc.sbuf_tensor(pre + "ln_xn", [128, D], BF16))
            st = al(nc.sbuf_tensor(pre + "ln_st", [128, 8], F32))
            sq = al(nc.sbuf_tensor(pre + "ln_sq", [128, D], F32))
            pt = al(nc.psum_tensor(pre + "ln_pt", [128, 512], BF16))
            s_x = [self.psem(0), self.psem(1)]
            s_dbg = self.psem(7) if dbg_ap is not None else None
            src_ap = x_ap if resid is None else resid[0]
            x_r = src_ap.rearrange("(g a p) d -> g p a d", a=GRP, p=128)
            if resid is not None:
                s_r = resid[1].rearrange("(g a p) d -> g p a d", a=GRP, p=128)
                g_r = resid[2].rearrange("(g a p) d -> g p a d", a=GRP, p=128)
            free_tok = [(), ()]
            free_sg = [(), ()]
            xn_free = ()
            pt_free = None
            t_sq = None
            for g in range(NTB // GRP):
                buf = xt[g % 2]
                ld = self.dma("sync",
                              buf[:, :].rearrange("p (a d) -> p a d", a=GRP),
                              x_r[g], s_x[g % 2], wait=free_tok[g % 2])
                if resid is not None:
                    lds = self.dma("sync",
                                   st_b[g % 2][:, :].rearrange(
                                       "p (a d) -> p a d", a=GRP),
                                   s_r[g], s_s[g % 2], wait=free_sg[g % 2])
                    ldg = self.dma("sync",
                                   gt_b[g % 2][:, :].rearrange(
                                       "p (a d) -> p a d", a=GRP),
                                   g_r[g], s_g[g % 2])
                for a in range(GRP):
                    tb = g * GRP + a
                    xa = buf[:, a * D:(a + 1) * D]
                    if resid is not None:
                        sl = slice(a * D, (a + 1) * D)
                        tmu = self.op("vector", "tensor_mul", sq[:, :],
                                      st_b[g % 2][:, sl], gt_b[g % 2][:, sl],
                                      wait=(lds, ldg) if a == 0 else ())
                        tx1 = self.op(
                            "vector", "tensor_add", x1t[:, :], xa, sq[:, :],
                            wait=((ld,) if a == 0 else ()) +
                                 ((t_sq,) if t_sq else ()) +
                                 ((s_x1, s_x1.n),))
                        self.dma("sync", x_ap[tb * 128:(tb + 1) * 128, :],
                                 x1t[:, :], s_x1, wait=(tx1,))
                        free_sg[g % 2] = (tmu,)
                        xa = x1t[:, :]
                    self.op("vector", "reduce_sum", st[:, 0:1], xa, axis=AX,
                            wait=(ld,) if (a == 0 and resid is None) else ())
                    self.op("vector", "tensor_scalar_mul", st[:, 1:2],
                            st[:, 0:1], -1.0 / D)
                    t_nm = self.op("vector", "tensor_scalar_mul", st[:, 2:3],
                                   st[:, 0:1], 1.0 / D, mark=True)
                    t_sq = self.op("scalar", "activation", sq[:, :], xa,
                                   AF.Square, bias=st[:, 1:2], scale=1.0,
                                   wait=(t_nm,))
                    t_ss = self.op("vector", "reduce_sum", st[:, 3:4],
                                   sq[:, :], axis=AX, wait=(t_sq,))
                    t_sd = self.op("scalar", "activation", st[:, 4:5],
                                   st[:, 3:4], AF.Sqrt,
                                   bias=self.c_eps[:, 0:1],
                                   scale=1.0 / D, wait=(t_ss,))
                    t_rc = self.op("vector", "reciprocal", st[:, 5:6],
                                   st[:, 4:5], wait=(t_sd,), mark=True)
                    t_xn = self.op("vector", "tensor_scalar", xn_tm[:, :], xa,
                                   st[:, 2:3], st[:, 5:6], ALU.subtract,
                                   ALU.mult, wait=xn_free, mark=True)
                    xn_free = ()
                    if resid is not None:
                        xtk = self.dma("sync",
                                       self.xn1tm[tb * 128:(tb + 1) * 128, :],
                                       xn_tm[:, :], s_x1, wait=(t_xn,))
                        xn_free = xn_free + (xtk,)
                    if dbg_ap is not None:
                        dtk = self.dma("sync",
                                       dbg_ap[tb * 128:(tb + 1) * 128, :],
                                       xn_tm[:, :], s_dbg, wait=(t_xn,))
                        xn_free = xn_free + (dtk,)
                    for half in range(2):
                        toks = []
                        for j in range(4):
                            c = half * 4 + j
                            pw = ()
                            if j == 0:
                                pw = (t_xn,) if half == 0 else ()
                                if pt_free is not None:
                                    pw = pw + (pt_free,)
                            tt = self.op(
                                "tensor", "transpose",
                                pt[:, j * 128:(j + 1) * 128],
                                xn_tm[:, c * 128:(c + 1) * 128],
                                self.c_identB[:, :],
                                wait=pw, mark=(j == 3))
                            if tt:
                                toks.append(tt)
                        pt_free = self.op(
                            "vector", "tensor_copy",
                            self.xnv[:, half * 4:(half + 1) * 4,
                                     tb * 128:(tb + 1) * 128],
                            pt[:, :].rearrange("p (j t) -> p j t", j=4),
                            wait=tuple(toks), mark=True)
                    xn_free = xn_free + (tt,)
                free_tok[g % 2] = (t_sq, t_xn)

    def p_proj_qk(self, name, w_ap, bias_col8, cos_ap, sin_ap, rotT, out_ap,
                  tm_out=None):
        nc = self.nc
        NT, T = self.NT, self.T
        NTB = self.NTB
        NC = NT // 512
        with self.phase() as ctx:
            al = ctx.enter_context
            if tm_out is not None:
                ktm = al(nc.sbuf_tensor(name + "_ktm", [128, NT], BF16))
                psT = al(nc.psum_tensor(name + "_psT", [128, 512], BF16))
                s_tm = self.psem(4)
            w_sb = al(nc.sbuf_tensor(name + "_w", [128, 8 * D], BF16))
            wv = w_sb[:, :].rearrange("p (c f) -> p c f", c=8)
            tab = [al(nc.sbuf_tensor(f"{name}_tab{i}", [128, 2 * T], F32))
                   for i in range(2)]
            qtmp = al(nc.sbuf_tensor(name + "_tmp", [128, NT], BF16))
            t1 = al(nc.sbuf_tensor(name + "_t1", [128, 512], F32))
            t2 = al(nc.sbuf_tensor(name + "_t2", [128, 512], F32))
            qout = al(nc.sbuf_tensor(name + "_out", [128, NT], BF16))
            pq = [al(nc.psum_tensor(f"{name}_pq{i}", [128, 512], F32))
                  for i in range(2)]
            pr = [al(nc.psum_tensor(f"{name}_pr{i}", [128, 512], F32))
                  for i in range(2)]
            s_w = self.psem(0, swdge=True)
            s_tab = [self.psem(1), self.psem(2)]
            s_out = self.psem(3)
            ldw = self.dma("gpsimd",
                           w_sb[:, :].rearrange("p (c f) -> p c f", c=8),
                           w_ap.rearrange("(c p) f -> p c f", p=128), s_w)

            def load_tab(fb, wait=()):
                tbuf = tab[fb % 2]
                self.dma("sync", tbuf[:, 0:T],
                         cos_ap[fb * 128:(fb + 1) * 128, :], s_tab[fb % 2],
                         wait=wait)
                lt2 = self.dma("sync", tbuf[:, T:2 * T],
                               sin_ap[fb * 128:(fb + 1) * 128, :],
                               s_tab[fb % 2])
                return (lt2, lt2)

            tab_tok = {0: load_tab(0), 1: load_tab(1)}
            out_tok = None
            tm_tr = None
            tm_ev = None
            evac_toks = {}
            rot_toks = {}
            add_toks = {}
            last_add = None

            def flush(fbc, nc_, t_evc, tcolc, ltc, lt2c, tbufc):
                nonlocal last_add
                rbank = pr[nc_ % 2]
                rw = [t_evc]
                prev_t2 = add_toks.get((fbc, nc_ - 2) if nc_ >= 2
                                       else (fbc - 1, NC - 2 + nc_))
                if prev_t2:
                    rw.append(prev_t2)
                t_rot = self.op("tensor", "matmul", rbank[:, :],
                                rotT[:, :],
                                qtmp[:, nc_ * 512:(nc_ + 1) * 512],
                                start=True, stop=True,
                                wait=tuple(rw), mark=True)
                rot_toks[(fbc, nc_)] = t_rot
                w1 = [t_evc, (ltc if nc_ == 0 else None),
                      (lt2c if nc_ == 0 else None)]
                if out_tok is not None and nc_ == 0:
                    w1.append(out_tok)
                    if tm_out is not None and tm_tr is not None:
                        w1.append(tm_tr)
                self.op("vector", "tensor_mul", t1[:, :],
                        qtmp[:, nc_ * 512:(nc_ + 1) * 512],
                        tbufc[:, tcolc:tcolc + 512],
                        wait=tuple(t for t in w1 if t))
                self.op("vector", "tensor_mul", t2[:, :], rbank[:, :],
                        tbufc[:, T + tcolc:T + tcolc + 512],
                        wait=(t_rot,))
                last_add = self.op("vector", "tensor_add",
                                   qout[:, nc_ * 512:(nc_ + 1) * 512],
                                   t1[:, :], t2[:, :], mark=True)
                add_toks[(fbc, nc_)] = last_add

            for fb in range(8):
                lt, lt2 = tab_tok[fb]
                tbuf = tab[fb % 2]
                pend = None
                for n in range(NC):
                    s = (n * 512) // T
                    tcol = n * 512 - s * T
                    bank = pq[n % 2]
                    for c in range(8):
                        pw = []
                        if c == 0:
                            if fb == 0 and n < 2:
                                pw.append(ldw)
                            prev = evac_toks.get((fb, n - 2) if n >= 2
                                                 else (fb - 1, NC - 2 + n))
                            if prev:
                                pw.append(prev)
                        self.op("tensor", "matmul", bank[:, :],
                                wv[:, c, fb * 128:(fb + 1) * 128],
                                self.xnv[:, c, n * 512:(n + 1) * 512],
                                start=(c == 0), stop=(c == 7),
                                wait=tuple(pw), mark=(c == 7))
                    t_mm = (self.ch["tensor"], self.ch["tensor"].n)
                    ev_w = [t_mm]
                    prev_rot = rot_toks.get((fb - 1, n))
                    if prev_rot:
                        ev_w.append(prev_rot)
                    prev_t1 = add_toks.get((fb - 1, n))
                    if prev_t1:
                        ev_w.append(prev_t1)
                    t_ev = self.op("scalar", "add",
                                   qtmp[:, n * 512:(n + 1) * 512], bank[:, :],
                                   bias_col8[:, fb:fb + 1],
                                   wait=tuple(ev_w), mark=True)
                    evac_toks[(fb, n)] = t_ev
                    if pend is not None:
                        flush(*pend)
                    pend = (fb, n, t_ev, tcol, lt, lt2, tbuf)
                flush(*pend)
                if fb + 2 < 8:
                    tab_tok[fb + 2] = load_tab(fb + 2, wait=(last_add,))
                out_tok = self.dma("sync", out_ap[fb * 128:(fb + 1) * 128, :],
                                   qout[:, :], s_out, wait=(last_add,))
                if tm_out is not None:
                    ev = None
                    for grp in range(NTB // 4):
                        for j in range(4):
                            tbk = grp * 4 + j
                            pw = ()
                            if j == 0:
                                pw = (last_add,) if grp == 0 else ()
                                if tm_tr is not None:
                                    pw = pw + (tm_ev,)
                            tm_tr = self.op(
                                "tensor", "transpose",
                                psT[:, j * 128:(j + 1) * 128],
                                qout[:, tbk * 128:(tbk + 1) * 128],
                                self.c_identB[:, :], wait=pw, mark=(j == 3))
                        tm_ev = self.op(
                            "scalar", "copy",
                            ktm[:, grp * 512:(grp + 1) * 512], psT[:, :],
                            wait=(tm_tr,) +
                                 (((s_tm, s_tm.n),) if grp == 0 else ()))
                    self.dma("sync",
                             tm_out.rearrange("(i p) f -> p i f", p=128)
                             [:, :, fb * 128:(fb + 1) * 128],
                             ktm[:, :].rearrange("p (i f) -> p i f", f=128),
                             s_tm, wait=(tm_ev,))
                    out_tok = (out_tok[0], out_tok[1])

    def p_proj_tm(self, name, w_ap, FW, bias_ap, out_ap, act=None):
        nc = self.nc
        NTB = self.NTB
        NF = FW // 512
        with self.phase() as ctx:
            al = ctx.enter_context
            w_sb = al(nc.sbuf_tensor(name + "_w", [128, 8 * FW], BF16))
            wv = w_sb[:, :].rearrange("p (c f) -> p c f", c=8)
            bias_row = al(nc.sbuf_tensor(name + "_b", [1, FW], BF16))
            vout = al(nc.sbuf_tensor(name + "_out", [128, 2 * FW], BF16))
            pb = [al(nc.psum_tensor(f"{name}_p{i}", [128, 512], F32))
                  for i in range(2)]
            s_w = self.psem(0, swdge=True)
            s_out = self.psem(3)
            ldw = self.dma("gpsimd", wv, w_ap.rearrange("(c p) f -> p c f",
                                                        p=128), s_w)
            ldb = self.dma("gpsimd", bias_row[:, :], bias_ap, s_w)
            ldw = ldb
            ev_tok = {}
            out_dma = [None, None]
            k = 0
            for tb in range(NTB):
                obuf = vout[:, (tb % 2) * FW:(tb % 2 + 1) * FW]
                for fc in range(NF):
                    bank = pb[k % 2]
                    for c in range(8):
                        pw = []
                        if c == 0:
                            if k < 2:
                                pw.append(ldw)
                            prev = ev_tok.get(k - 2)
                            if prev:
                                pw.append(prev)
                        self.op("tensor", "matmul", bank[:, :],
                                self.xnv[:, c, tb * 128:(tb + 1) * 128],
                                wv[:, c, fc * 512:(fc + 1) * 512],
                                start=(c == 0), stop=False,
                                wait=tuple(pw))
                    self.op("tensor", "matmul", bank[:, :],
                            self.c_onesB[0:1, :],
                            bias_row[0:1, fc * 512:(fc + 1) * 512],
                            start=False, stop=True, mark=True)
                    t_mm = (self.ch["tensor"], self.ch["tensor"].n)
                    ew = [t_mm]
                    if fc == 0 and out_dma[tb % 2] is not None:
                        ew.append((s_out, s_out.n))
                    if act is None:
                        t_ev = self.op("vector", "tensor_copy",
                                       obuf[:, fc * 512:(fc + 1) * 512],
                                       bank[:, :], wait=tuple(ew), mark=True)
                    else:
                        t_ev = self.op("scalar", "activation",
                                       obuf[:, fc * 512:(fc + 1) * 512],
                                       bank[:, :], act, bias=0.0, scale=1.0,
                                       wait=tuple(ew), mark=True)
                    ev_tok[k] = t_ev
                    k += 1
                out_dma[tb % 2] = self.dma(
                    "sync", out_ap[tb * 128:(tb + 1) * 128, :], obuf,
                    s_out, wait=(ev_tok[k - 1], ev_tok[k - NF]))

    def p_quant_gamma(self):
        nc = self.nc
        W1, W2 = self.inp["L0_ffn_W1"], self.inp["L0_ffn_W2"]
        with self.phase() as ctx:
            al = ctx.enter_context
            wt = [al(nc.sbuf_tensor(f"qg_w{i}", [128, 2048], F32))
                  for i in range(2)]
            parts = al(nc.sbuf_tensor("qg_parts", [128, 48], F32))
            strow = al(nc.sbuf_tensor("qg_st", [1, 56], F32))
            ptot = al(nc.psum_tensor("qg_ptot", [1, 48], F32))
            pbc = al(nc.psum_tensor("qg_pbc", [128, 2], F32))
            s_w = [self.psem(0), self.psem(1)]
            k = 0
            free = [(), ()]
            col = 0
            for (w_ap, ntile, width) in ((W1, 32, D), (W2, 8, DFF)):
                for tb in range(ntile):
                    for hw in range(width // 2048 if width > 2048 else 1):
                        wslice = w_ap[tb * 128:(tb + 1) * 128,
                                      hw * 2048:hw * 2048 + min(2048, width)]
                        buf = wt[k % 2]
                        ld = self.dma("scalar", buf[:, 0:min(2048, width)],
                                      wslice, s_w[k % 2], wait=free[k % 2])
                        free[k % 2] = (self.op(
                            "vector", "reduce_sum", parts[:, col:col + 1],
                            buf[:, 0:min(2048, width)], axis=AX,
                            apply_absolute_value=True, wait=(ld,)),)
                        k += 1
                        col += 1
            assert col == 48
            t_mm = self.op("tensor", "matmul", ptot[:, 0:48],
                           self.c_onesF[:, 0:1], parts[:, 0:48],
                           start=True, stop=True,
                           wait=free[0] + free[1], mark=True)
            self.op("vector", "tensor_copy", strow[0:1, 0:48], ptot[0:1, :],
                    wait=(t_mm,))
            self.op("vector", "reduce_sum", strow[0:1, 48:49],
                    strow[0:1, 0:32], axis=AX)
            self.op("vector", "reduce_sum", strow[0:1, 49:50],
                    strow[0:1, 32:48], axis=AX)
            self.op("vector", "tensor_scalar", strow[0:1, 50:51],
                    strow[0:1, 48:49], 1.0 / (DFF * D), EPS_LN,
                    ALU.mult, ALU.add)
            self.op("vector", "tensor_scalar", strow[0:1, 51:52],
                    strow[0:1, 49:50], 1.0 / (DFF * D), EPS_LN,
                    ALU.mult, ALU.add)
            t_rc = self.op("vector", "reciprocal", strow[0:1, 52:54],
                           strow[0:1, 50:52])
            t_bc = self.op("tensor", "matmul", pbc[:, :],
                           self.c_onesFr[0:1, :],
                           strow[0:1, 52:54], start=True, stop=True,
                           wait=(t_rc,), mark=True)
            self.op("vector", "tensor_copy", self.c_rs[:, :], pbc[:, :],
                    wait=(t_bc,))

    def p_quant_apply(self):
        nc = self.nc
        W1, W2 = self.inp["L0_ffn_W1"], self.inp["L0_ffn_W2"]
        with self.phase() as ctx:
            al = ctx.enter_context
            wt = [al(nc.sbuf_tensor(f"qa_w{i}", [128, 1024], F32))
                  for i in range(2)]
            qtmp = al(nc.sbuf_tensor("qa_qtmp", [128, 1024], F32))
            qbf = [al(nc.sbuf_tensor(f"qa_qbf{i}", [128, 1024], BF16))
                   for i in range(2)]
            tbuf = [al(nc.sbuf_tensor(f"qa_tbuf{i}", [128, 1024], BF16))
                    for i in range(2)]
            ptr = al(nc.psum_tensor("qa_ptr", [128, 512], BF16))
            s_w = [self.psem(0), self.psem(1)]
            s_out = self.psem(2)
            k = 0
            free = [(), ()]
            ptr_free = None
            tq_free = {}
            for (w_ap, out_ap, ntile, nch, col) in (
                    (W1, self.w1qt, 32, 1, 0), (W2, self.w2qt, 8, 4, 1)):
                for tb in range(ntile):
                    for hw in range(nch):
                        buf = wt[k % 2]
                        ld = self.dma("scalar", buf[:, :],
                                      w_ap[tb * 128:(tb + 1) * 128,
                                           hw * 1024:(hw + 1) * 1024],
                                      s_w[k % 2], wait=free[k % 2])
                        qb = qbf[k % 2]
                        tb_ = tbuf[k % 2]
                        t_q1 = self.op("vector", "tensor_scalar",
                                       qtmp[:, :], buf[:, :],
                                       self.c_rs[:, col:col + 1],
                                       self.c_eps[:, 1:2],
                                       ALU.mult, ALU.add, wait=(ld,))
                        self.op("vector", "tensor_scalar", qtmp[:, :],
                                qtmp[:, :], self.c_eps[:, 1:2], 1.0,
                                ALU.subtract, ALU.min)
                        t_q = self.op("vector", "tensor_scalar_max",
                                      qb[:, :], qtmp[:, :], -1.0,
                                      wait=(tq_free.get(k % 2),))
                        free[k % 2] = (t_q1,)
                        ev = []
                        for grp in range(2):
                            for j in range(4):
                                c = grp * 4 + j
                                pw = ()
                                if j == 0:
                                    pw = (t_q,)
                                    if ptr_free is not None:
                                        pw = pw + (ptr_free,)
                                tt = self.op("tensor", "transpose",
                                             ptr[:, j * 128:(j + 1) * 128],
                                             qb[:, c * 128:(c + 1) * 128],
                                             self.c_identB[:, :],
                                             wait=pw, mark=(j == 3))
                            evw = [tt]
                            if grp == 0:
                                evw.append((s_out, s_out.n))
                            ptr_free = self.op(
                                "scalar", "copy",
                                tb_[:, grp * 512:(grp + 1) * 512],
                                ptr[:, :], wait=tuple(evw))
                            ev.append(ptr_free)
                        tq_free[k % 2] = tt
                        self.dma("scalar",
                                 out_ap.rearrange("(c p) f -> p c f", p=128)
                                 [:, hw * 8:(hw + 1) * 8,
                                  tb * 128:(tb + 1) * 128],
                                 tb_[:, :].rearrange("p (c f) -> p c f",
                                                     c=8),
                                 s_out, wait=(ev[-1],))
                        k += 1

    def p_ffn0(self):
        nc = self.nc
        NT = self.NT
        NCH = NT // 512
        with self.phase() as ctx:
            al = ctx.enter_context
            w1t = al(nc.sbuf_tensor("f0_w1t", [128, 8 * DFF], BF16))
            w1v = w1t[:, :].rearrange("p (c f) -> p c f", c=8)
            w2q = [al(nc.sbuf_tensor(f"f0_w2q{i}", [128, 4 * D], BF16))
                   for i in range(2)]
            h_sb = al(nc.sbuf_tensor("f0_h", [128, 32 * 512], BF16))
            hv = h_sb[:, :].rearrange("p (f t) -> p f t", f=32)
            g2t = [al(nc.sbuf_tensor(f"f0_g2_{i}", [128, 512], F32))
                   for i in range(2)]
            pp = [al(nc.psum_tensor(f"f0_p{i}", [128, 512], F32))
                  for i in range(8)]
            s_w1 = self.psem(0)
            s_w2 = [self.psem(1), self.psem(2)]
            s_g2 = [self.psem(3), self.psem(4)]
            ld1 = self.dma("sync", w1v,
                           self.w1qt.rearrange("(c p) f -> p c f", p=128),
                           s_w1)
            w2r = self.w2qt.rearrange("(c p) f -> p c f", p=128)

            def load_w2(g, wait=()):
                q = g % 8
                return self.dma("sync",
                                w2q[g % 2][:, :].rearrange(
                                    "p (c f) -> p c f", c=4),
                                w2r[:, q * 4:(q + 1) * 4, :], s_w2[g % 2],
                                wait=wait)

            w2tok = {0: load_w2(0), 1: load_w2(1)}
            g2free = {}
            kg = 0
            for tc in range(NCH):
                relu_toks = []
                for fb in range(32):
                    bank = pp[fb % 2]
                    for c in range(8):
                        pw = []
                        if c == 0:
                            if fb < 2:
                                if tc == 0 and fb == 0:
                                    pw.append(ld1)
                                gf = g2free.get(fb)
                                if gf:
                                    pw.append(gf)
                            else:
                                pw.append(relu_toks[fb - 2])
                        self.op("tensor", "matmul", bank[:, :],
                                w1v[:, c, fb * 128:(fb + 1) * 128],
                                self.xnv[:, c, tc * 512:(tc + 1) * 512],
                                start=(c == 0), stop=(c == 7),
                                wait=tuple(pw), mark=(c == 7))
                    t_mm = (self.ch["tensor"], self.ch["tensor"].n)
                    relu_toks.append(self.op(
                        "scalar", "activation", hv[:, fb, :], bank[:, :],
                        AF.Relu, bias=self.c_b1ff[:, fb:fb + 1], scale=1.0,
                        wait=(t_mm,)))
                for q in range(8):
                    g = tc * 8 + q
                    for tb in range(4):
                        for dc in range(2):
                            bank = pp[tb * 2 + dc]
                            for j in range(4):
                                fc = q * 4 + j
                                pw = []
                                if j == 0:
                                    if tb == 0 and dc == 0:
                                        pw.append(w2tok[g])
                                    if q == 0:
                                        if tb == 0 and dc == 0:
                                            pw.append(relu_toks[-1])
                                        gf = g2free.get(tb * 2 + dc)
                                        if gf:
                                            pw.append(gf)
                                pe_tok = self.op(
                                    "tensor", "matmul", bank[:, :],
                                    hv[:, fc, tb * 128:(tb + 1) * 128],
                                    w2q[g % 2][:, :].rearrange(
                                        "p (c f) -> p c f", c=4)
                                    [:, j, dc * 512:(dc + 1) * 512],
                                    start=(fc == 0), stop=False,
                                    wait=tuple(pw),
                                    mark=(tb == 3 and dc == 1 and j == 3))
                    if g + 2 < NCH * 8:
                        w2tok[g + 2] = load_w2(g + 2, wait=(pe_tok,))
                for tb in range(4):
                    for dc in range(2):
                        bank = pp[tb * 2 + dc]
                        t_b = self.op("tensor", "matmul", bank[:, :],
                                      self.c_onesB[0:1, :],
                                      self.c_b2ff[0:1, dc * 512:(dc + 1) * 512],
                                      start=False, stop=True, mark=True)
                        gb = g2t[kg % 2]
                        t_g = self.op("scalar", "activation", gb[:, :],
                                      bank[:, :], AF.Silu, bias=0.0,
                                      scale=1.0,
                                      wait=(t_b,
                                            (s_g2[kg % 2], s_g2[kg % 2].n)))
                        self.dma(
                            "sync",
                            self.g20[(tc * 4 + tb) * 128:
                                     (tc * 4 + tb + 1) * 128,
                                     dc * 512:(dc + 1) * 512],
                            gb[:, :], s_g2[kg % 2], wait=(t_g,))
                        g2free[tb * 2 + dc] = t_g
                        kg += 1
        return

    def p_attn0(self):
        nc = self.nc
        B, T, TB = self.B, self.T, self.TB
        with self.phase() as ctx:
            al = ctx.enter_context
            qk = [al(nc.sbuf_tensor(f"a0_qk{i}", [128, 2 * T], BF16))
                  for i in range(2)]
            vx = [al(nc.sbuf_tensor(f"a0_vx{i}", [128, TB * 132], BF16))
                  for i in range(2)]
            sg = [al(nc.sbuf_tensor(f"a0_sg{i}", [128, TB * 128], BF16))
                  for i in range(2)]
            og = [al(nc.sbuf_tensor(f"a0_og{i}", [128, T], BF16))
                  for i in range(2)]
            oraw = [al(nc.sbuf_tensor(f"a0_or{i}", [128, TB * 132], F32))
                    for i in range(2)]
            onb = al(nc.sbuf_tensor("a0_onb", [128, TB * 128], F32))
            sqb = al(nc.sbuf_tensor("a0_sqb", [128, TB * 128], F32))
            nrmb = al(nc.sbuf_tensor("a0_nrmb", [128, TB * 128], F32))
            ogb = [al(nc.sbuf_tensor(f"a0_ogb{i}", [128, TB * 128], BF16))
                   for i in range(2)]
            S_sb = [al(nc.sbuf_tensor(f"a0_S{i}", [128, 132], BF16))
                    for i in range(4)]
            ktm = [al(nc.sbuf_tensor(f"a0_ktm{i}", [128, TB * 64], BF16))
                   for i in range(2)]
            dTs = [al(nc.sbuf_tensor(f"a0_dT{i}", [128, 128], BF16))
                   for i in range(2)]
            dNs = [al(nc.sbuf_tensor(f"a0_dN{i}", [128, 128], BF16))
                   for i in range(2)]
            st = al(nc.sbuf_tensor("a0_st", [128, 8 * TB], F32))
            stv = st[:, :].rearrange("p (k i) -> p k i", k=8)
            psS = [al(nc.psum_tensor(f"a0_psS{i}", [64, 132], F32))
                   for i in range(2)]
            psD = [al(nc.psum_tensor(f"a0_psD{i}", [128, 256], F32))
                   for i in range(2)]
            psO = [al(nc.psum_tensor(f"a0_psO{i}", [128, 132], F32))
                   for i in range(2)]
            psT2 = al(nc.psum_tensor("a0_psT2", [128, 512], BF16))
            s_qk = [self.psem(0), self.psem(1)]
            s_kt = [self.psem(2), self.psem(3)]
            s_vx = [self.psem(4), self.psem(5)]
            s_sg = [self.psem(6), self.psem(7)]
            s_og = [self.psem(8), self.psem(9)]
            mset = []
            for i in range(2):
                mset.append(self.op(
                    "vector", "memset",
                    vx[i][:, :].rearrange("p (i c) -> p i c", c=132)
                    [:, :, 128:132], 1.0))
            vsc_r = self.vsc.rearrange("(b i p) c -> b p i c", i=TB, p=128)
            ktm_r = self.ksctm.rearrange("(b i p) c -> b p i c", i=TB, p=128)
            gsc_r = self.gsc.rearrange("(b i p) c -> b p i c", i=TB, p=128)
            tkD = {}
            tkO = {}
            last_snap = {}
            t_av = {}
            b_on_last = None
            b_sq_last = None
            b_ssq = None
            b_og = None
            b_abs = None
            b_nrm_last = None
            b_pmn = None
            b_sd = None
            pe_group = {}
            pe_head = {}
            kp = kh = 0
            prev_c = None
            pend_b = None
            cstate = {"tkT2": None, "c_tr_last": None}

            def emit_c(kh_, s_, h_, pv_, b_og_):
                ev_last = None
                for grp in range(TB // 4):
                    for j in range(4):
                        i = grp * 4 + j
                        pw = ()
                        if j == 0:
                            pw = ((b_og_, cstate["tkT2"]) if grp == 0
                                  else (cstate["tkT2"],))
                        cstate["c_tr_last"] = self.op(
                            "tensor", "transpose",
                            psT2[:, j * 128:(j + 1) * 128],
                            ogb[kh_ % 2][:, i * 128:(i + 1) * 128],
                            self.c_identB[:, :], wait=pw,
                            mark=(j == 3))
                    ev_last = self.op(
                        "scalar", "copy",
                        og[pv_][:, grp * 512:(grp + 1) * 512],
                        psT2[:, :],
                        wait=(cstate["c_tr_last"],
                              (s_og[pv_], s_og[pv_].n)
                              if grp == 0 else None))
                    cstate["tkT2"] = ev_last
                self.dma("sync",
                         self.ogf[128 * h_:128 * (h_ + 1),
                                  s_ * T:(s_ + 1) * T],
                         og[pv_][:, :], s_og[pv_], wait=(ev_last,))

            def emit_b(kh_, s_, h_, pv_, l_s_, a_or_last_, a_or_prev_):
                nonlocal b_abs, b_on_last, b_sq_last, b_ssq, b_og, \
                    b_nrm_last, b_pmn, b_sd
                orv_ = oraw[pv_][:, :].rearrange("p (i c) -> p i c", c=132)
                b_abs = self.op("scalar", "activation",
                                stv[:, 0, :], orv_[:, :, 128],
                                AF.Abs, bias=0.0, scale=1.0,
                                wait=(a_or_last_, a_or_prev_,
                                      b_nrm_last))
                self.op("vector", "tensor_scalar_max",
                        stv[:, 1, :], stv[:, 0, :], 1.0,
                        wait=(b_abs,))
                b_rr = self.op("vector", "reciprocal",
                               stv[:, 2, :], stv[:, 1, :])
                prev_sq = b_sq_last
                for i in range(TB):
                    b_on_last = self.op(
                        "vector", "tensor_scalar_mul",
                        onb[:, i * 128:(i + 1) * 128],
                        orv_[:, i, 0:128], stv[:, 2, i:i + 1],
                        wait=(prev_sq,) if i == 0 else ())
                b_mn = self.op("vector", "reduce_sum", stv[:, 3, :],
                               onb[:, :].rearrange(
                                   "p (i c) -> p i c", c=128),
                               axis=AX, wait=(b_pmn,))
                b_nmn = self.op("vector", "tensor_scalar_mul",
                                stv[:, 4, :], stv[:, 3, :],
                                -1.0 / 128, wait=(b_mn, b_sq_last))
                b_pmn = self.op("vector", "tensor_scalar_mul",
                                stv[:, 5, :], stv[:, 3, :],
                                1.0 / 128)
                prev_ssq = b_ssq
                for i in range(TB):
                    b_sq_last = self.op(
                        "scalar", "activation",
                        sqb[:, i * 128:(i + 1) * 128],
                        onb[:, i * 128:(i + 1) * 128], AF.Square,
                        bias=stv[:, 4, i:i + 1], scale=1.0,
                        wait=(b_nmn, b_on_last, prev_ssq)
                        if i == 0 else ())
                b_ssq = self.op("vector", "reduce_sum", stv[:, 6, :],
                                sqb[:, :].rearrange(
                                    "p (i c) -> p i c", c=128),
                                axis=AX, wait=(b_sq_last, b_sd))
                b_sd = self.op("scalar", "activation", stv[:, 7, :],
                               stv[:, 6, :], AF.Sqrt,
                               bias=self.c_eps[:, 0:1],
                               scale=1.0 / 128, wait=(b_ssq,))
                b_rstd = self.op("vector", "reciprocal",
                                 stv[:, 0, :], stv[:, 7, :],
                                 wait=(b_sd,))
                prev_og = b_og
                for i in range(TB):
                    b_nrm_last = self.op(
                        "vector", "tensor_scalar",
                        nrmb[:, i * 128:(i + 1) * 128],
                        onb[:, i * 128:(i + 1) * 128],
                        stv[:, 5, i:i + 1], stv[:, 0, i:i + 1],
                        ALU.subtract, ALU.mult,
                        wait=(b_rstd, prev_og) if i == 0 else ())
                b_og = self.op("gpsimd", "tensor_mul",
                               ogb[kh_ % 2][:, :],
                               nrmb[:, :], sg[pv_][:, :],
                               wait=(l_s_, cstate["c_tr_last"],
                                     b_nrm_last))
                return (kh_, s_, h_, pv_, b_og)

            for s in range(B):
                for hp in range(8):
                    pq = kp % 2
                    self.dma("sync", qk[pq][:, 0:T],
                             self.qsc[hp * 128:(hp + 1) * 128,
                                      s * T:(s + 1) * T], s_qk[pq],
                             wait=(pe_group.get(pq),))
                    l_k = self.dma("sync", qk[pq][:, T:2 * T],
                                   self.ksc[hp * 128:(hp + 1) * 128,
                                            s * T:(s + 1) * T], s_qk[pq])
                    for hh in range(2):
                        h = 2 * hp + hh
                        rb = 64 * hh
                        pv = kh % 2
                        wv = (pe_head.get(pv), mset[pv] if kh < 2 else None)
                        l_v = self.dma(
                            "sync",
                            vx[pv][:, :].rearrange("p (i c) -> p i c", c=132)
                            [:, :, 0:128],
                            vsc_r[s][:, :, 128 * h:128 * h + 128],
                            s_vx[pv], wait=wv)
                        l_s = self.dma(
                            "sync",
                            sg[pv][:, :].rearrange("p (i c) -> p i c", c=128),
                            gsc_r[s][:, :, 128 * h:128 * h + 128],
                            s_sg[pv], wait=(b_og,))
                        l_kt = self.dma(
                            "sync",
                            ktm[pv][:, :].rearrange("p (i c) -> p i c", c=64),
                            ktm_r[s][:, :, 64 * h:64 * h + 64],
                            s_kt[pv], wait=(pe_head.get(pv),))
                        qs = qk[pq][rb:rb + 64, 0:T]
                        ks = qk[pq][rb:rb + 64, T:2 * T]
                        orv = oraw[pv][:, :].rearrange("p (i c) -> p i c",
                                                       c=132)
                        ktv = ktm[pv][:, :].rearrange("p (i c) -> p i c",
                                                      c=64)
                        vxv = vx[pv][:, :].rearrange("p (i c) -> p i c",
                                                    c=132)
                        snap = {-2: last_snap.get(0),
                                -1: last_snap.get(1)}
                        buf = lambda j: 2 * (j % 2) + ((j // 2) % 2)
                        dT_tok = {}

                        def emit_scores(j, first):
                            pj = j % 2
                            jb = slice(j * 128, (j + 1) * 128)
                            jp = slice((j - 1) * 128, j * 128)
                            t3 = self.op("tensor", "matmul",
                                         psD[pj][:, 0:128],
                                         ks[:, jb], qs[:, jb],
                                         start=True, stop=True,
                                         wait=(tkD.get(pj),
                                               l_k if first else None),
                                         mark=(j == 0))
                            if j > 0:
                                t3 = self.op("tensor", "matmul",
                                             psD[pj][:, 128:256],
                                             ks[:, jp], qs[:, jb],
                                             start=True, stop=True,
                                             mark=True)
                            t4 = self.op("vector", "tensor_mul",
                                         dTs[pj][:, :], psD[pj][:, 0:128],
                                         self.c_triu[:, :],
                                         wait=(t3, t_av.get(pj)))
                            t4n = None
                            if j > 0:
                                t4n = self.op("vector", "tensor_copy",
                                              dNs[pj][:, :],
                                              psD[pj][:, 128:256],
                                              wait=(t3,))
                            tkD[pj] = (t4, t4n) if j > 0 else (t4,)
                            dT_tok[j] = (t4, t4n)

                        emit_scores(0, hh == 0)
                        for i in range(TB):
                            pi = i % 2
                            ib = slice(i * 128, (i + 1) * 128)
                            vxi = vxv[:, i, :]
                            if i + 1 < TB:
                                emit_scores(i + 1, False)
                            t6 = self.op("tensor", "matmul",
                                         psS[pi][:, :],
                                         ktv[:, i, :], vxi,
                                         start=(i < 2), stop=(i >= TB - 2),
                                         skip_group_check=True,
                                         wait=(snap.get(i - 2),
                                               l_kt if i == 0 else None,
                                               l_v if i == 0 else None),
                                         mark=True)
                            t4, t4n = dT_tok[i]
                            t5 = self.op("tensor", "matmul", psO[pi][:, :],
                                         dTs[pi][:, :], vxi,
                                         start=True, stop=(i == 0),
                                         wait=(t4, tkO.get(pi)),
                                         mark=(i == 0))
                            if i > 0:
                                t5 = self.op("tensor", "matmul",
                                             psO[pi][:, :],
                                             dNs[pi][:, :], vxv[:, i - 1, :],
                                             start=False, stop=(i == 1),
                                             wait=(t4n,), mark=(i == 1))
                            for j in (i - 2, i - 3):
                                if j >= 0:
                                    t5 = self.op(
                                        "tensor", "matmul", psO[pi][:, :],
                                        qs[:, ib],
                                        S_sb[buf(j)][rb:rb + 64, :],
                                        start=False, stop=(j == min(
                                            x for x in (i - 2, i - 3)
                                            if x >= 0)),
                                        wait=(snap[j],), mark=True)
                            snap[i] = self.op("scalar", "copy",
                                              S_sb[buf(i)][rb:rb + 64, :],
                                              psS[pi][:, :], wait=(t6, t5))
                            tkO[pi] = self.op("scalar", "copy",
                                              orv[:, i, :], psO[pi][:, :],
                                              wait=(t5, b_on_last,
                                                    (b_abs if i == 0
                                                     else None)))
                            t_av[pi] = t5
                        last_snap = {0: snap[TB - 2], 1: snap[TB - 1]}
                        a_or_last = tkO[(TB - 1) % 2]
                        a_or_prev = tkO[TB % 2]
                        pe_head[pv] = t5
                        if pend_b is not None:
                            bret = emit_b(*pend_b)
                            if prev_c is not None:
                                emit_c(*prev_c)
                            prev_c = bret
                        pend_b = (kh, s, h, pv, l_s,
                                  a_or_last, a_or_prev)
                        kh += 1
                    pe_group[pq] = t5
                    kp += 1
            if pend_b is not None:
                bret = emit_b(*pend_b)
                if prev_c is not None:
                    emit_c(*prev_c)
                emit_c(*bret)

    def p_oproj(self):
        nc = self.nc
        NTB = self.NTB
        with self.phase() as ctx:
            al = ctx.enter_context
            ow = al(nc.sbuf_tensor("op_ow", [128, 16 * D], BF16))
            owv = ow[:, :].rearrange("p (c f) -> p c f", c=16)
            ogb = [al(nc.sbuf_tensor(f"op_og{i}", [128, 16 * 128], BF16))
                   for i in range(2)]
            s0t = [al(nc.sbuf_tensor(f"op_s0{i}", [128, D], F32))
                   for i in range(2)]
            pb = [al(nc.psum_tensor(f"op_p{i}", [128, 512], F32))
                  for i in range(2)]
            s_w = self.psem(0, swdge=True)
            s_og = [self.psem(1), self.psem(2)]
            s_out = [self.psem(3), self.psem(4)]
            ldw = self.dma("gpsimd", owv,
                           self.inp["L0_attn_o_w"].rearrange(
                               "(c p) f -> p c f", p=128), s_w)
            ogr = self.ogf.rearrange("(c p) t -> p c t", p=128)

            def load_og(tb, wait=()):
                return self.dma("sync",
                                ogb[tb % 2][:, :].rearrange(
                                    "p (c t) -> p c t", c=16),
                                ogr[:, :, tb * 128:(tb + 1) * 128],
                                s_og[tb % 2], wait=wait)

            og_tok = {0: load_og(0), 1: load_og(1)}
            ev_tok = {}
            pe_done = {}
            for tb in range(NTB):
                for fc in range(2):
                    bank = pb[fc]
                    for c in range(16):
                        pw = []
                        if c == 0:
                            if fc == 0:
                                pw.append(og_tok[tb])
                                if tb == 0:
                                    pw.append(ldw)
                            prev = ev_tok.get((tb - 1, fc))
                            if prev:
                                pw.append(prev)
                        self.op("tensor", "matmul", bank[:, :],
                                ogb[tb % 2][:, :].rearrange(
                                    "p (c t) -> p c t", c=16)[:, c, :],
                                owv[:, c, fc * 512:(fc + 1) * 512],
                                start=(c == 0), stop=False, wait=tuple(pw))
                    t_b = self.op("tensor", "matmul", bank[:, :],
                                  self.c_onesB[0:1, :],
                                  self.c_ob0[0:1, fc * 512:(fc + 1) * 512],
                                  start=False, stop=True,
                                  mark=True)
                    ev_tok[(tb, fc)] = self.op(
                        "vector", "tensor_copy",
                        s0t[tb % 2][:, fc * 512:(fc + 1) * 512], bank[:, :],
                        wait=(t_b, (s_out[tb % 2], s_out[tb % 2].n))
                        if fc == 0 else (t_b,))
                    if fc == 1:
                        pe_done[tb] = t_b
                if tb + 2 < NTB:
                    og_tok[tb + 2] = load_og(tb + 2, wait=(pe_done[tb],))
                self.dma("sync", self.s0[tb * 128:(tb + 1) * 128, :],
                         s0t[tb % 2][:, :], s_out[tb % 2],
                         wait=(ev_tok[(tb, 1)],))

    def p_oproj_resid0(self):
        nc = self.nc
        NTB = self.NTB
        with self.phase() as ctx:
            al = ctx.enter_context
            ow = al(nc.sbuf_tensor("op_ow", [128, 16 * D], BF16))
            owv = ow[:, :].rearrange("p (c f) -> p c f", c=16)
            ogb = [al(nc.sbuf_tensor(f"op_og{i}", [128, 16 * 128], BF16))
                   for i in range(2)]
            s0t = [al(nc.sbuf_tensor(f"op_s0{i}", [128, D], F32))
                   for i in range(2)]
            xg = [al(nc.sbuf_tensor(f"op_xg{i}", [128, 2 * D], F32))
                  for i in range(2)]
            x1t = [al(nc.sbuf_tensor(f"op_x1{i}", [128, D], F32))
                   for i in range(2)]
            xn_tm = [al(nc.sbuf_tensor(f"op_xn{i}", [128, D], BF16))
                     for i in range(2)]
            sq = [al(nc.sbuf_tensor(f"op_sq{i}", [128, D], F32))
                  for i in range(2)]
            st = al(nc.sbuf_tensor("op_st", [128, 16], F32))
            pb = [al(nc.psum_tensor(f"op_p{i}", [128, 512], F32))
                  for i in range(4)]
            pt = al(nc.psum_tensor("op_pt", [128, 512], BF16))
            s_w = self.psem(0, swdge=True)
            s_og = [self.psem(1), self.psem(2)]
            s_xg = [self.psem(3), self.psem(4)]
            s_x1 = self.psem(5)
            ldw = self.dma("gpsimd", owv,
                           self.inp["L0_attn_o_w"].rearrange(
                               "(c p) f -> p c f", p=128), s_w)
            ogr = self.ogf.rearrange("(c p) t -> p c t", p=128)
            x_ap = self.inp["x"]

            def load_og(tb, wait=()):
                return self.dma("sync",
                                ogb[tb % 2][:, :].rearrange(
                                    "p (c t) -> p c t", c=16),
                                ogr[:, :, tb * 128:(tb + 1) * 128],
                                s_og[tb % 2], wait=wait)

            def load_xg(tb, wait=()):
                par = tb % 2
                self.dma("scalar", xg[par][:, 0:D],
                         x_ap[tb * 128:(tb + 1) * 128, :], s_xg[par],
                         wait=wait)
                return self.dma("scalar", xg[par][:, D:2 * D],
                                self.g20[tb * 128:(tb + 1) * 128, :],
                                s_xg[par])

            og_tok = {0: load_og(0), 1: load_og(1)}
            xg_tok = {0: load_xg(0), 1: load_xg(1)}
            ev_tok = {}
            pe_done = {}
            sq_w = {}
            x1_free = {}
            xn_free = {}
            ptst = {"free": None}

            def emit_ln(tb):
                par = tb % 2
                c0 = par * 8
                lxg = xg_tok[tb]
                t_mu = self.op("vector", "tensor_mul", sq[par][:, :],
                               s0t[par][:, :], xg[par][:, D:2 * D],
                               wait=(ev_tok[(tb, 1)], lxg))
                t_x1 = self.op("vector", "tensor_add", x1t[par][:, :],
                               xg[par][:, 0:D], sq[par][:, :],
                               wait=x1_free.get(par, ()))
                xdma1 = self.dma("sync",
                                 self.x1s[tb * 128:(tb + 1) * 128, :],
                                 x1t[par][:, :], s_x1, wait=(t_x1,))
                self.op("vector", "reduce_sum", st[:, c0:c0 + 1],
                        x1t[par][:, :], axis=AX)
                self.op("vector", "tensor_scalar_mul", st[:, c0 + 1:c0 + 2],
                        st[:, c0:c0 + 1], -1.0 / D,
                        wait=(sq_w.get(par),))
                t_nm = self.op("vector", "tensor_scalar_mul",
                               st[:, c0 + 2:c0 + 3], st[:, c0:c0 + 1],
                               1.0 / D, mark=True)
                t_sq = self.op("scalar", "activation", sq[par][:, :],
                               x1t[par][:, :], AF.Square,
                               bias=st[:, c0 + 1:c0 + 2], scale=1.0,
                               wait=(t_nm,))
                sq_w[par] = t_sq
                t_ss = self.op("vector", "reduce_sum", st[:, c0 + 3:c0 + 4],
                               sq[par][:, :], axis=AX, wait=(t_sq,))
                t_sd = self.op("scalar", "activation", st[:, c0 + 4:c0 + 5],
                               st[:, c0 + 3:c0 + 4], AF.Sqrt,
                               bias=self.c_eps[:, 0:1],
                               scale=1.0 / D, wait=(t_ss,))
                t_rc = self.op("vector", "reciprocal", st[:, c0 + 5:c0 + 6],
                               st[:, c0 + 4:c0 + 5], wait=(t_sd,),
                               mark=True)
                t_xn = self.op("vector", "tensor_scalar", xn_tm[par][:, :],
                               x1t[par][:, :], st[:, c0 + 2:c0 + 3],
                               st[:, c0 + 5:c0 + 6],
                               ALU.subtract, ALU.mult,
                               wait=xn_free.get(par, ()), mark=True)
                x1_free[par] = (t_sq, xdma1)
                xdma2 = self.dma("sync",
                                 self.xn1tm[tb * 128:(tb + 1) * 128, :],
                                 xn_tm[par][:, :], s_x1, wait=(t_xn,))
                for half in range(2):
                    toks = []
                    for j in range(4):
                        c = half * 4 + j
                        pw = ()
                        if j == 0:
                            pw = ((t_xn,) if half == 0 else ()) + \
                                 ((ptst["free"],) if ptst["free"] else ())
                        tt = self.op(
                            "tensor", "transpose",
                            pt[:, j * 128:(j + 1) * 128],
                            xn_tm[par][:, c * 128:(c + 1) * 128],
                            self.c_identB[:, :],
                            wait=pw, mark=(j == 3))
                        if tt:
                            toks.append(tt)
                    ptst["free"] = self.op(
                        "vector", "tensor_copy",
                        self.xnv[:, half * 4:(half + 1) * 4,
                                 tb * 128:(tb + 1) * 128],
                        pt[:, :].rearrange("p (j t) -> p j t", j=4),
                        wait=tuple(toks), mark=True)
                xn_free[par] = (tt, xdma2)
                if tb + 2 < NTB:
                    xg_tok[tb + 2] = load_xg(tb + 2, wait=(t_x1,))

            pend_ln = None
            for tb in range(NTB):
                par = tb % 2
                for fc in range(2):
                    bank = pb[par * 2 + fc]
                    for c in range(16):
                        pw = []
                        if c == 0:
                            if fc == 0:
                                pw.append(og_tok[tb])
                                if tb == 0:
                                    pw.append(ldw)
                            prev = ev_tok.get((tb - 2, fc))
                            if prev:
                                pw.append(prev)
                        self.op("tensor", "matmul", bank[:, :],
                                ogb[tb % 2][:, :].rearrange(
                                    "p (c t) -> p c t", c=16)[:, c, :],
                                owv[:, c, fc * 512:(fc + 1) * 512],
                                start=(c == 0), stop=False, wait=tuple(pw))
                    t_b = self.op("tensor", "matmul", bank[:, :],
                                  self.c_onesB[0:1, :],
                                  self.c_ob0[0:1, fc * 512:(fc + 1) * 512],
                                  start=False, stop=True,
                                  mark=True)
                    ev_tok[(tb, fc)] = self.op(
                        "vector", "tensor_copy",
                        s0t[par][:, fc * 512:(fc + 1) * 512], bank[:, :],
                        wait=(t_b,))
                    if fc == 1:
                        pe_done[tb] = t_b
                if tb + 2 < NTB:
                    og_tok[tb + 2] = load_og(tb + 2, wait=(pe_done[tb],))
                if pend_ln is not None:
                    emit_ln(pend_ln)
                pend_ln = tb
            emit_ln(NTB - 1)

    def p_qkv1(self):
        nc = self.nc
        NT, T, NTB = self.NT, self.T, self.NTB
        NC = NT // 512
        with self.phase() as ctx:
            al = ctx.enter_context
            wq = al(nc.sbuf_tensor("q1_wq", [128, 8 * 3 * D], BF16))
            wqv = wq[:, :].rearrange("p (c f) -> p c f", c=8)
            wraw = al(nc.sbuf_tensor("q1_wraw", [128, 4 * D], BF16))
            tabc = al(nc.sbuf_tensor("q1_tab", [128, 2 * T], F32))
            qtmp = al(nc.sbuf_tensor("q1_tmp", [128, NT], BF16))
            t1 = al(nc.sbuf_tensor("q1_t1", [128, 512], F32))
            t2 = al(nc.sbuf_tensor("q1_t2", [128, 512], F32))
            qout = al(nc.sbuf_tensor("q1_out", [128, NT], BF16))
            vout = [al(nc.sbuf_tensor(f"q1_v{i}", [128, D], BF16))
                    for i in range(2)]
            ktm1 = al(nc.sbuf_tensor("q1_ktm", [128, NT], BF16))
            psT = al(nc.psum_tensor("q1_psT", [128, 512], BF16))
            pq = [al(nc.psum_tensor(f"q1_pq{i}", [128, 512], F32))
                  for i in range(2)]
            pr = [al(nc.psum_tensor(f"q1_pr{i}", [128, 512], F32))
                  for i in range(2)]
            s_w = self.psem(0, swdge=True)
            s_tab = self.psem(1)
            s_out = self.psem(2)
            s_vout = [self.psem(4), self.psem(5)]
            s_tm1 = self.psem(6)
            self.dma("sync", tabc[:, 0:T], self.inp["cos1"], s_tab)
            ltab = self.dma("sync", tabc[:, T:2 * T], self.inp["sin1"], s_tab)
            wr = self.inp["L1_attn_W_qkv"].rearrange("(r p) d -> p r d", p=128)
            tp_free = None
            for grp6 in range(6):
                ldw = self.dma("gpsimd",
                               wraw[:, :].rearrange("p (r d) -> p r d", r=4),
                               wr[:, grp6 * 4:(grp6 + 1) * 4, :], s_w,
                               wait=(tp_free,) if grp6 else ())
                for c in range(8):
                    for j in range(4):
                        r = j
                        pw = ()
                        if j == 0:
                            pw = (ldw,) if (c == 0) else ()
                            if tp_free is not None:
                                pw = pw + (tp_free,)
                        tt = self.op(
                            "tensor", "transpose",
                            psT[:, j * 128:(j + 1) * 128],
                            wraw[:, r * D + c * 128:
                                 r * D + (c + 1) * 128],
                            self.c_identB[:, :], wait=pw,
                            mark=(j == 3))
                    tp_free = self.op(
                        "vector", "tensor_copy",
                        wqv[:, c, (grp6 * 4) * 128:
                            (grp6 * 4 + 4) * 128],
                        psT[:, :], wait=(tt,))
            t_wq = tp_free
            evac_toks = {}
            rot_toks = {}
            add_toks = {}
            out_tok = None
            tm1_tr = None
            tm1_ev = None
            last_add = None

            def flush1(fbgc, nc_, t_evc, tcolc):
                nonlocal last_add
                rbank = pr[nc_ % 2]
                rw = [t_evc]
                prev_t2 = add_toks.get((fbgc, nc_ - 2) if nc_ >= 2
                                       else (fbgc - 1, NC - 2 + nc_))
                if prev_t2:
                    rw.append(prev_t2)
                t_rot = self.op("tensor", "matmul", rbank[:, :],
                                self.c_R1T[:, :],
                                qtmp[:, nc_ * 512:(nc_ + 1) * 512],
                                start=True, stop=True,
                                wait=tuple(rw), mark=True)
                rot_toks[(fbgc, nc_)] = t_rot
                w1 = [t_evc, ltab if fbgc == 0 and nc_ == 0 else None]
                if out_tok is not None and nc_ == 0:
                    w1.append(out_tok)
                    if tm1_tr is not None:
                        w1.append(tm1_tr)
                self.op("vector", "tensor_mul", t1[:, :],
                        qtmp[:, nc_ * 512:(nc_ + 1) * 512],
                        tabc[:, tcolc:tcolc + 512],
                        wait=tuple(t for t in w1 if t))
                self.op("vector", "tensor_mul", t2[:, :], rbank[:, :],
                        tabc[:, T + tcolc:T + tcolc + 512],
                        wait=(t_rot,))
                last_add = self.op("vector", "tensor_add",
                                   qout[:, nc_ * 512:(nc_ + 1) * 512],
                                   t1[:, :], t2[:, :], mark=True)
                add_toks[(fbgc, nc_)] = last_add

            for part, out_ap in ((0, self.qr1), (1, self.kr1)):
                for fb in range(8):
                    fbg = part * 8 + fb
                    pend = None
                    for n in range(NC):
                        sseq = (n * 512) // T
                        tcol = n * 512 - sseq * T
                        bank = pq[n % 2]
                        for c in range(8):
                            pw = []
                            if c == 0:
                                if fbg == 0 and n < 2:
                                    pw.append(t_wq)
                                prev = evac_toks.get((fbg, n - 2) if n >= 2
                                                     else (fbg - 1, NC - 2 + n))
                                if prev:
                                    pw.append(prev)
                            self.op("tensor", "matmul", bank[:, :],
                                    wqv[:, c, fbg * 128:(fbg + 1) * 128],
                                    self.xnv[:, c, n * 512:(n + 1) * 512],
                                    start=(c == 0), stop=(c == 7),
                                    wait=tuple(pw), mark=(c == 7))
                        t_mm = (self.ch["tensor"], self.ch["tensor"].n)
                        ev_w = [t_mm]
                        prev_rot = rot_toks.get((fbg - 1, n))
                        if prev_rot:
                            ev_w.append(prev_rot)
                        prev_t1 = add_toks.get((fbg - 1, n))
                        if prev_t1:
                            ev_w.append(prev_t1)
                        t_ev = self.op("scalar", "add",
                                       qtmp[:, n * 512:(n + 1) * 512],
                                       bank[:, :],
                                       self.c_qkb1[:, fbg:fbg + 1],
                                       wait=tuple(ev_w), mark=True)
                        evac_toks[(fbg, n)] = t_ev
                        if pend is not None:
                            flush1(*pend)
                        pend = (fbg, n, t_ev, tcol)
                    flush1(*pend)
                    out_tok = self.dma("sync",
                                       out_ap[fb * 128:(fb + 1) * 128, :],
                                       qout[:, :], s_out, wait=(last_add,))
                    if part == 1:
                        for grp in range(NTB // 4):
                            for j in range(4):
                                tbk = grp * 4 + j
                                pw = ()
                                if j == 0:
                                    pw = ((last_add,) if grp == 0 else ()) + \
                                         ((tm1_ev,) if tm1_tr
                                          else (t_wq,))
                                tm1_tr = self.op(
                                    "tensor", "transpose",
                                    psT[:, j * 128:(j + 1) * 128],
                                    qout[:, tbk * 128:(tbk + 1) * 128],
                                    self.c_identB[:, :], wait=pw,
                                    mark=(j == 3))
                            tm1_ev = self.op(
                                "scalar", "copy",
                                ktm1[:, grp * 512:(grp + 1) * 512],
                                psT[:, :],
                                wait=(tm1_tr,) +
                                     (((s_tm1, s_tm1.n),) if grp == 0
                                      else ()))
                        self.dma("sync",
                                 self.kr1tm.rearrange("(i p) f -> p i f",
                                                      p=128)
                                 [:, :, fb * 128:(fb + 1) * 128],
                                 ktm1[:, :].rearrange("p (i f) -> p i f",
                                                      f=128),
                                 s_tm1, wait=(tm1_ev,))
            ev_tok = {}
            vo_dma = {}
            k = 0
            for tb in range(NTB):
                obuf = vout[tb % 2]
                for fc in range(2):
                    bank = pq[k % 2]
                    for c in range(8):
                        pw = []
                        if c == 0:
                            prev = ev_tok.get(k - 2)
                            if prev:
                                pw.append(prev)
                            elif k < 2:
                                pw.append(evac_toks[(15, NC - 2 + k)])
                        self.op("tensor", "matmul", bank[:, :],
                                self.xnv[:, c, tb * 128:(tb + 1) * 128],
                                wqv[:, c, 2 * D + fc * 512:
                                    2 * D + (fc + 1) * 512],
                                start=(c == 0), stop=False, wait=tuple(pw))
                    self.op("tensor", "matmul", bank[:, :],
                            self.c_onesB[0:1, :],
                            self.c_vb1[0:1, fc * 512:(fc + 1) * 512],
                            start=False, stop=True, mark=True)
                    t_mm = (self.ch["tensor"], self.ch["tensor"].n)
                    ew = [t_mm]
                    if fc == 0 and tb >= 2:
                        ew.append((s_vout[tb % 2], s_vout[tb % 2].n))
                    ev_tok[k] = self.op("vector", "tensor_copy",
                                        obuf[:, fc * 512:(fc + 1) * 512],
                                        bank[:, :], wait=tuple(ew))
                    k += 1
                vo_dma[tb] = self.dma("sync",
                                      self.v1s[tb * 128:(tb + 1) * 128, :],
                                      obuf[:, :], s_vout[tb % 2],
                                      wait=(ev_tok[k - 1],))

    def p_attn1(self):
        nc = self.nc
        B, T, TB = self.B, self.T, self.TB
        with self.phase() as ctx:
            al = ctx.enter_context
            qk = [al(nc.sbuf_tensor(f"a1_qk{i}", [128, 2 * T], BF16))
                  for i in range(2)]
            vx = [al(nc.sbuf_tensor(f"a1_vx{i}", [128, TB * 128], BF16))
                  for i in range(2)]
            kt1 = [al(nc.sbuf_tensor(f"a1_kt{i}", [128, TB * 128], BF16))
                   for i in range(2)]
            sh1 = [al(nc.sbuf_tensor(f"a1_sh{i}", [128, TB * 64], F32))
                   for i in range(2)]
            S_sb = [al(nc.sbuf_tensor(f"a1_S{i}", [128, 64], BF16))
                    for i in range(4)]
            dTs = [al(nc.sbuf_tensor(f"a1_dT{i}", [128, 128], BF16))
                   for i in range(2)]
            dNs = [al(nc.sbuf_tensor(f"a1_dN{i}", [128, 128], BF16))
                   for i in range(2)]
            psS = [al(nc.psum_tensor(f"a1_psS{i}", [64, 64], F32))
                   for i in range(2)]
            psD = [al(nc.psum_tensor(f"a1_psD{i}", [128, 256], F32))
                   for i in range(2)]
            psO = al(nc.psum_tensor("a1_psO", [128, 256], F32))
            s_qk = [self.psem(0), self.psem(1)]
            s_vx = [self.psem(2), self.psem(3)]
            s_sh = [self.psem(4), self.psem(5)]
            v1r = self.v1s.rearrange("(b i p) c -> b p i c", i=TB, p=128)
            kt1r = self.kr1tm.rearrange("(b i p) c -> b p i c", i=TB, p=128)
            s1r = self.s1.rearrange("(b i p) c -> b p i c", i=TB, p=128)
            tkD = {}
            tkO = {}
            last_snap = {}
            t_av = {}
            pe_group = {}
            pe_pair = {}
            kp = 0
            for s in range(B):
                for hp in range(8):
                    pq_ = kp % 2
                    self.dma("sync", qk[pq_][:, 0:T],
                             self.qr1[hp * 128:(hp + 1) * 128,
                                      s * T:(s + 1) * T], s_qk[pq_],
                             wait=(pe_group.get(pq_),))
                    l_k = self.dma("sync", qk[pq_][:, T:2 * T],
                                   self.kr1[hp * 128:(hp + 1) * 128,
                                            s * T:(s + 1) * T], s_qk[pq_])
                    l_v = self.dma("sync",
                                   vx[pq_][:, :].rearrange(
                                       "p (i c) -> p i c", c=128),
                                   v1r[s][:, :, 128 * hp:128 * hp + 128],
                                   s_vx[pq_], wait=(pe_pair.get(pq_),))
                    l_kt = self.dma("sync",
                                    kt1[pq_][:, :].rearrange(
                                        "p (i c) -> p i c", c=128),
                                    kt1r[s][:, :, 128 * hp:128 * hp + 128],
                                    s_vx[pq_])
                    for hh in range(2):
                        h = 2 * hp + hh
                        rb = 64 * hh
                        qs = qk[pq_][rb:rb + 64, 0:T]
                        ks = qk[pq_][rb:rb + 64, T:2 * T]
                        ktv = kt1[pq_][:, :].rearrange(
                            "p (i c) -> p i c", c=128)
                        ev4 = None
                        snap = {-2: last_snap.get(0),
                                -1: last_snap.get(1)}
                        buf = lambda j: 2 * (j % 2) + ((j // 2) % 2)
                        vxv1 = vx[pq_][:, :].rearrange(
                            "p (i c) -> p i c", c=128)
                        dT_tok = {}

                        def emit_scores1(j, first):
                            pj = j % 2
                            jb = slice(j * 128, (j + 1) * 128)
                            jp = slice((j - 1) * 128, j * 128)
                            t3 = self.op("tensor", "matmul",
                                         psD[pj][:, 0:128],
                                         ks[:, jb], qs[:, jb],
                                         start=True, stop=True,
                                         wait=(tkD.get(pj),
                                               l_k if first else None),
                                         mark=(j == 0))
                            if j > 0:
                                t3 = self.op("tensor", "matmul",
                                             psD[pj][:, 128:256],
                                             ks[:, jp], qs[:, jb],
                                             start=True, stop=True,
                                             mark=True)
                            t4 = self.op("vector", "tensor_mul",
                                         dTs[pj][:, :], psD[pj][:, 0:128],
                                         self.c_triu[:, :],
                                         wait=(t3, t_av.get(pj)))
                            t4n = None
                            if j > 0:
                                t4n = self.op("vector", "tensor_copy",
                                              dNs[pj][:, :],
                                              psD[pj][:, 128:256],
                                              wait=(t3,))
                            tkD[pj] = (t4, t4n) if j > 0 else (t4,)
                            dT_tok[j] = (t4, t4n)

                        emit_scores1(0, hh == 0)
                        for i in range(TB):
                            ib = slice(i * 128, (i + 1) * 128)
                            pi = i % 2
                            vxi = vxv1[:, i, rb:rb + 64]
                            oseg = psO[:, (i % 4) * 64:(i % 4) * 64 + 64]
                            if i + 1 < TB:
                                emit_scores1(i + 1, False)
                            t6 = self.op("tensor", "matmul", psS[pi][:, :],
                                         ktv[:, i, rb:rb + 64], vxi,
                                         start=(i < 2), stop=(i >= TB - 2),
                                         skip_group_check=True,
                                         wait=(snap.get(i - 2),
                                               l_kt if (i == 0 and hh == 0)
                                               else None),
                                         mark=True)
                            t4, t4n = dT_tok[i]
                            t5 = self.op("tensor", "matmul", oseg,
                                         dTs[pi][:, :], vxi,
                                         start=True, stop=(i == 0),
                                         wait=(t4,
                                               tkO.get(0)
                                               if i % 4 == 0 else None),
                                         mark=(i == 0))
                            if i > 0:
                                t5 = self.op("tensor", "matmul", oseg,
                                             dNs[pi][:, :],
                                             vxv1[:, i - 1, rb:rb + 64],
                                             start=False, stop=(i == 1),
                                             wait=(t4n,), mark=(i == 1))
                            for j in (i - 2, i - 3):
                                if j >= 0:
                                    t5 = self.op(
                                        "tensor", "matmul", oseg,
                                        qs[:, ib],
                                        S_sb[buf(j)][rb:rb + 64, :],
                                        start=False, stop=(j == min(
                                            x for x in (i - 2, i - 3)
                                            if x >= 0)),
                                        wait=(snap[j],), mark=True)
                            snap[i] = self.op("scalar", "copy",
                                              S_sb[buf(i)][rb:rb + 64, :],
                                              psS[pi][:, :], wait=(t6, t5))
                            t_av[pi] = t5
                            if i % 4 == 3:
                                ev4 = self.op(
                                    "vector", "tensor_copy",
                                    sh1[hh][:, (i - 3) * 64:(i + 1) * 64],
                                    psO[:, :],
                                    wait=(t5,) +
                                         (((s_sh[hh], s_sh[hh].n),)
                                          if i == 3 else ()))
                                tkO[0] = ev4
                        last_snap = {0: snap[TB - 2], 1: snap[TB - 1]}
                        self.dma("sync",
                                 s1r[s][:, :, 64 * h:64 * h + 64],
                                 sh1[hh][:, :].rearrange(
                                     "p (i c) -> p i c", c=64),
                                 s_sh[hh], wait=(ev4,))
                    pe_pair[pq_] = t5
                    pe_group[pq_] = t5
                    kp += 1

    def p_router(self):
        nc = self.nc
        NT, NTB = self.NT, self.NTB
        NC = NT // 512
        with self.phase() as ctx:
            al = ctx.enter_context
            wr_sb = al(nc.sbuf_tensor("rt_wr", [128, 8 * E], BF16))
            logb = al(nc.sbuf_tensor("rt_log", [8, NT], F32))
            lt = al(nc.sbuf_tensor("rt_lt", [128, E], F32))
            ct = al(nc.sbuf_tensor("rt_ct", [128, E], F32))
            ovt = al(nc.sbuf_tensor("rt_ovt", [128, E], F32))
            exb = al(nc.sbuf_tensor("rt_ex", [128, E], F32))
            stt = al(nc.sbuf_tensor("rt_st", [128, 4], F32))
            offs = al(nc.sbuf_tensor("rt_offs", [1, E], F32))
            xtile = [al(nc.sbuf_tensor(f"rt_xt{i}", [128, D], BF16))
                     for i in range(2)]
            psL = al(nc.psum_tensor("rt_psL", [8, 512], F32))
            psTr = al(nc.psum_tensor("rt_psTr", [128, E], F32))
            psC = al(nc.psum_tensor("rt_psC", [128, 2 * E], F32))
            s_w = self.psem(0, swdge=True)
            s_xt = [self.psem(1), self.psem(2)]
            s_sc = self.psem(3, swdge=True)
            ldw = self.dma("gpsimd",
                           wr_sb[:, :].rearrange("p (c e) -> p c e", c=8),
                           self.inp["L1_ffn_Wr"].rearrange(
                               "(c p) e -> p c e", p=128), s_w)
            t_off = self.op("vector", "memset", offs[:, :], 0.0)
            ev_prev = None
            for n in range(NC):
                for c in range(8):
                    self.op("tensor", "matmul", psL[:, :],
                            wr_sb[:, :].rearrange("p (c e) -> p c e",
                                                  c=8)[:, c, :],
                            self.xnv[:, c, n * 512:(n + 1) * 512],
                            start=(c == 0), stop=(c == 7),
                            wait=((ldw,) if (n == 0 and c == 0) else ()) +
                                 ((ev_prev,) if (c == 0 and ev_prev)
                                  else ()),
                            mark=(c == 7))
                t_mm = (self.ch["tensor"], self.ch["tensor"].n)
                ev_prev = self.op("vector", "tensor_scalar_add",
                                  logb[:, n * 512:(n + 1) * 512], psL[:, :],
                                  self.c_br[:, 0:1], wait=(t_mm,))
            CAP = float(self.CAP)
            tr_free = None
            tc_free = None
            sc_prev = None
            for tb in range(NTB):
                par = tb % 2
                lx = self.dma("sync", xtile[par][:, :],
                              self.xn1tm[tb * 128:(tb + 1) * 128, :],
                              s_xt[par], wait=(sc_prev,))
                t_tr = self.op("tensor", "transpose", psTr[:, :],
                               logb[:, tb * 128:(tb + 1) * 128],
                               self.c_identF[0:8, 0:8],
                               wait=(ev_prev if tb == 0 else None,
                                     tr_free), mark=True)
                tr_free = self.op("vector", "tensor_copy", lt[:, :],
                                  psTr[:, :], wait=(t_tr,))
                self.op("vector", "reduce_max", stt[:, 0:1], lt[:, :],
                        axis=AX)
                t_nm = self.op("vector", "tensor_scalar_mul", stt[:, 1:2],
                               stt[:, 0:1], -1.0)
                t_ex = self.op("scalar", "activation", exb[:, :], lt[:, :],
                               AF.Exp, bias=stt[:, 1:2], scale=1.0,
                               wait=(t_nm,))
                t_se = self.op("vector", "reduce_sum", stt[:, 2:3],
                               exb[:, :], axis=AX, wait=(t_ex,))
                self.op("vector", "reciprocal", self.c_mp[:, tb:tb + 1],
                        stt[:, 2:3])
                t_ind = self.op("vector", "tensor_single_scalar",
                                self.c_ind[:, tb * E:(tb + 1) * E], lt[:, :],
                                stt[:, 0:1], ALU.is_ge)
                indsl = self.c_ind[:, tb * E:(tb + 1) * E]
                self.op("tensor", "matmul", psC[:, 0:E],
                        self.c_triu[:, :], indsl,
                        start=True, stop=False, wait=(t_ind, tc_free))
                self.op("tensor", "matmul", psC[0:1, E:2 * E],
                        self.c_onesF[:, 0:1], indsl,
                        start=True, stop=True, wait=(t_off,))
                t_c2 = self.op("tensor", "matmul", psC[:, 0:E],
                               self.c_onesFr[0:1, :], offs[0:1, :],
                               start=False, stop=True, mark=True)
                tc_free = self.op("vector", "tensor_copy", ct[:, :],
                                  psC[:, 0:E], wait=(t_c2,))
                t_off = self.op("vector", "tensor_add", offs[0:1, :],
                                offs[0:1, :], psC[0:1, E:2 * E])
                self.op("vector", "tensor_single_scalar", ovt[:, :],
                        ct[:, :], CAP, ALU.is_gt)
                self.op("vector", "tensor_mul", ovt[:, :], ovt[:, :], indsl)
                self.op("vector", "reduce_sum", stt[:, 2:3], ovt[:, :],
                        axis=AX)
                self.op("vector", "tensor_scalar_mul", stt[:, 3:4],
                        stt[:, 2:3], 1.0e8)
                self.op("vector", "tensor_add", ct[:, :], ct[:, :],
                        self.c_eCm1[:, :])
                self.op("vector", "tensor_mul", ct[:, :], ct[:, :], indsl)
                self.op("vector", "reduce_sum", stt[:, 1:2], ct[:, :],
                        axis=AX)
                self.op("vector", "tensor_add", stt[:, 1:2], stt[:, 1:2],
                        stt[:, 3:4])
                t_sl = self.op("vector", "tensor_copy",
                               self.c_slots[:, tb:tb + 1], stt[:, 1:2])
                sc_prev = self.dma("gpsimd", self.xnsort,
                                   xtile[par][:, :], s_sc,
                                   wait=(lx, t_sl),
                                   indirect_out=self.c_slots[:, tb:tb + 1])

    def p_moe(self):
        nc = self.nc
        CAP = self.CAP
        NCH = CAP // 512
        with self.phase() as ctx:
            al = ctx.enter_context
            w1e = al(nc.sbuf_tensor("mo_w1", [128, 8 * HMOE], BF16))
            w1v = w1e[:, :].rearrange("p (c f) -> p c f", c=8)
            w2e = al(nc.sbuf_tensor("mo_w2", [128, 16 * D], BF16))
            w2v = w2e[:, :].rearrange("p (c f) -> p c f", c=16)
            xs = [al(nc.sbuf_tensor(f"mo_xs{i}", [128, 8 * 512], BF16))
                  for i in range(2)]
            h1 = al(nc.sbuf_tensor("mo_h", [128, 16 * 512], BF16))
            hv = h1[:, :].rearrange("p (f t) -> p f t", f=16)
            yev = [al(nc.sbuf_tensor(f"mo_ev{i}", [128, 512], F32))
                   for i in range(4)]
            pp = [al(nc.psum_tensor(f"mo_pp{i}", [128, 512], F32))
                  for i in range(3)]
            s_w1 = self.psem(0, swdge=True)
            s_w2 = self.psem(1, swdge=True)
            s_xs = [self.psem(2), self.psem(3)]
            s_y = [self.psem(4 + i) for i in range(4)]
            W1, W2 = self.inp["L1_ffn_W1"], self.inp["L1_ffn_W2"]

            def load_w1(e, wait=()):
                return self.dma("gpsimd", w1v,
                                W1[e].rearrange("(c p) f -> p c f", p=128),
                                s_w1, wait=wait)

            def load_w2(e, wait=()):
                return self.dma("gpsimd", w2v,
                                W2[e].rearrange("(c p) f -> p c f", p=128),
                                s_w2, wait=wait)

            def load_xs(g, wait=()):
                e, tcs = divmod(g, NCH)
                par = g % 2
                r0 = e * CAP + tcs * 512
                tok = None
                for c in range(8):
                    tok = self.dma("sync",
                                   xs[par][:, c * 512:(c + 1) * 512],
                                   self.xnsort[r0:r0 + 512,
                                               c * 128:(c + 1) * 128],
                                   s_xs[par],
                                   wait=wait if c == 0 else (),
                                   transpose=True)
                return tok

            w1tok = load_w1(0)
            w2tok = load_w2(0)
            xstok = {0: load_xs(0), 1: load_xs(1)}
            evac_tok = {}
            kq = 0
            ky = 0
            y_prev = None
            for e in range(E):
                for tcs in range(NCH):
                    g = e * NCH + tcs
                    par = g % 2
                    xsv = xs[par][:, :].rearrange("p (c t) -> p c t", c=8)
                    h_last = None
                    for fb in range(16):
                        bank = pp[kq % 3]
                        for c in range(8):
                            pw = []
                            if c == 0:
                                if fb == 0:
                                    pw.append(xstok[g])
                                    if tcs == 0:
                                        pw.append(w1tok)
                                        if e > 0:
                                            pw.append(y_prev)
                                prev = evac_tok.get(kq % 3)
                                if prev:
                                    pw.append(prev)
                            self.op("tensor", "matmul", bank[:, :],
                                    w1v[:, c, fb * 128:(fb + 1) * 128],
                                    xsv[:, c, :],
                                    start=(c == 0), stop=(c == 7),
                                    wait=tuple(pw), mark=(c == 7))
                        h_last = (self.ch["tensor"], self.ch["tensor"].n)
                        evac_tok[kq % 3] = self.op(
                            "scalar", "activation", hv[:, fb, :], bank[:, :],
                            AF.Relu,
                            bias=self.c_b1moe[:, e * 16 + fb:e * 16 + fb + 1],
                            scale=1.0, wait=(h_last,))
                        kq += 1
                    relu_last = evac_tok[(kq - 1) % 3]
                    if g + 2 < E * NCH:
                        xstok[g + 2] = load_xs(g + 2, wait=(h_last,))
                    if tcs == NCH - 1 and e + 1 < E:
                        w1tok = load_w1(e + 1, wait=(h_last,))
                    for tt in range(4):
                        for dc in range(2):
                            bank = pp[kq % 3]
                            for j in range(16):
                                pw = []
                                if j == 0:
                                    if tt == 0 and dc == 0:
                                        if tcs == 0:
                                            pw.append(w2tok)
                                        pw.append(relu_last)
                                    prev = evac_tok.get(kq % 3)
                                    if prev:
                                        pw.append(prev)
                                y_last = self.op(
                                    "tensor", "matmul", bank[:, :],
                                    hv[:, j, tt * 128:(tt + 1) * 128],
                                    w2v[:, j, dc * 512:(dc + 1) * 512],
                                    start=(j == 0), stop=(j == 15),
                                    wait=tuple(pw), mark=(j == 15))
                            bb = ky % 4
                            t_ev = self.op(
                                "vector", "tensor_copy", yev[bb][:, :],
                                bank[:, :],
                                wait=(y_last, (s_y[bb], s_y[bb].n)))
                            evac_tok[kq % 3] = t_ev
                            kq += 1
                            ky += 1
                            r0 = e * CAP + tcs * 512 + tt * 128
                            self.dma("sync",
                                     self.ysort[r0:r0 + 128,
                                                dc * 512:(dc + 1) * 512],
                                     yev[bb][:, :], s_y[bb], wait=(t_ev,))
                if e + 1 < E:
                    w2tok = load_w2(e + 1, wait=(y_last,))
                y_prev = y_last

    def p_resid1(self):
        nc = self.nc
        NTB = self.NTB
        GRP = 2
        with self.phase() as ctx:
            al = ctx.enter_context
            xb = [al(nc.sbuf_tensor(f"r1_x{i}", [128, GRP * D], F32))
                  for i in range(2)]
            sb = [al(nc.sbuf_tensor(f"r1_s{i}", [128, GRP * D], F32))
                  for i in range(2)]
            mb = [al(nc.sbuf_tensor(f"r1_m{i}", [128, D], F32))
                  for i in range(4)]
            pre = [al(nc.sbuf_tensor(f"r1_pre{i}", [128, D], F32))
                   for i in range(2)]
            indb = [al(nc.sbuf_tensor(f"r1_indb{i}", [128, E], BF16))
                    for i in range(2)]
            indTall = al(nc.sbuf_tensor("r1_indT", [E, self.NT], BF16))
            psTi = al(nc.psum_tensor("r1_psTi", [E, 128], BF16))
            gat = [al(nc.sbuf_tensor(f"r1_gat{i}", [128, D], F32))
                   for i in range(2)]
            outt = [al(nc.sbuf_tensor(f"r1_out{i}", [128, D], F32))
                    for i in range(2)]
            pb2 = [al(nc.psum_tensor(f"r1_pb{i}", [128, 512], F32))
                   for i in range(2)]
            s_x = [self.psem(0), self.psem(1)]
            s_s = [self.psem(2), self.psem(3)]
            s_m = [self.psem(4 + i, swdge=True) for i in range(4)]
            s_y = self.psem(8)
            x_r = self.x1s.rearrange("(g a p) d -> g p a d", a=GRP, p=128)
            s_r = self.s1.rearrange("(g a p) d -> g p a d", a=GRP, p=128)
            free = [(), ()]
            t_sil = {}
            t_iT = None
            t_icp = None
            t_pre = {}
            t_mb_read = {}
            t_m_prev = {}
            out_dma = {}
            ind_tok = {}
            ldm_tok = {}
            ldxs = {}
            t_out_tok = {}

            def prefetch(tb):
                q = tb % 4
                t_ms = self.op("vector", "memset", mb[q][:, :], 0.0,
                               wait=(t_mb_read.get(q),))
                ldm_tok[tb] = self.dma("gpsimd", mb[q][:, :],
                                       self.ysort, s_m[q],
                                       wait=(t_ms,),
                                       indirect_in=self.c_slots[:, tb:tb + 1])

            for tb in range(NTB):
                ipar = tb % 2
                t_ib = self.op("vector", "tensor_copy", indb[ipar][:, :],
                               self.c_ind[:, tb * E:(tb + 1) * E],
                               wait=(ind_tok.get(tb - 2),))
                t_tr = self.op("tensor", "transpose", psTi[:, :],
                               indb[ipar][:, :], self.c_identB[:, :],
                               wait=(t_ib, t_icp), mark=True)
                t_icp = self.op("vector", "tensor_copy",
                                indTall[:, tb * 128:(tb + 1) * 128],
                                psTi[:, :], wait=(t_tr,))
                ind_tok[tb] = t_icp
                if tb < 4:
                    prefetch(tb)

            def emit_out(tb):
                par = (tb // GRP) % 2
                opar = tb % 2
                a = tb % GRP
                sl = slice(a * D, (a + 1) * D)
                t_m = self.op("vector", "tensor_mul", outt[opar][:, :],
                              sb[par][:, sl], gat[opar][:, :],
                              wait=(t_sil[tb],
                                    ldxs[tb // GRP][1] if a == 0 else None,
                                    out_dma.get(opar)))
                t_m_prev[opar] = t_m
                t_out = self.op("vector", "tensor_add", outt[opar][:, :],
                                xb[par][:, sl], outt[opar][:, :],
                                wait=(ldxs[tb // GRP][0],) if a == 0 else ())
                t_out_tok[tb] = t_out
                out_dma[opar] = self.dma(
                    "sync", self.y_out[tb * 128:(tb + 1) * 128, :],
                    outt[opar][:, :], s_y, wait=(t_out,))

            for g in range(NTB // GRP):
                par = g % 2
                ldx = self.dma("sync",
                               xb[par][:, :].rearrange("p (a d) -> p a d",
                                                       a=GRP),
                               x_r[g], s_x[par], wait=free[par])
                lds = self.dma("sync",
                               sb[par][:, :].rearrange("p (a d) -> p a d",
                                                       a=GRP),
                               s_r[g], s_s[par])
                ldxs[g] = (ldx, lds)
                for a in range(GRP):
                    tb = g * GRP + a
                    q = tb % 4
                    opar = tb % 2
                    for dc in range(2):
                        t_b2 = self.op(
                            "tensor", "matmul", pb2[dc][:, :],
                            indTall[:, tb * 128:(tb + 1) * 128],
                            self.c_b2moe[:, dc * 512:(dc + 1) * 512],
                            start=True, stop=True,
                            wait=(t_pre.get(dc), ind_tok[tb]), mark=True)
                        t_pre[dc] = self.op(
                            "vector", "tensor_add",
                            pre[opar][:, dc * 512:(dc + 1) * 512],
                            mb[q][:, dc * 512:(dc + 1) * 512],
                            pb2[dc][:, :],
                            wait=(t_b2, t_sil.get(tb - 2), ldm_tok[tb]))
                    t_mb_read[q] = t_pre[1]
                    t_sil[tb] = self.op("scalar", "activation",
                                        gat[opar][:, :],
                                        pre[opar][:, :], AF.Silu, bias=0.0,
                                        scale=self.c_mp[:, tb:tb + 1],
                                        wait=(t_pre[0], t_pre[1],
                                              t_m_prev.get(opar)))
                    if tb + 4 < NTB:
                        prefetch(tb + 4)
                    if tb > 0:
                        emit_out(tb - 1)
                if g >= 2:
                    free[par] = (t_out_tok[(g - 2) * GRP + 1],)
            emit_out(NTB - 1)

    def finalize(self):
        with self.phase():
            pass



_CACHED = {}


def get_nc(B=2, T=2048, debug=False, upto=99):
    key = (B, T, debug, upto)
    if key not in _CACHED:
        _CACHED[key] = MK(B, T, debug, upto).build()
    return _CACHED[key]


def kernel(**inputs):
    nc = get_nc()
    in_maps = [host_inputs(c, inputs) for c in range(NCORES)]
    res = run_bass_kernel_spmd(nc, in_maps, core_ids=list(range(NCORES)))
    out = np.concatenate([r["y"] for r in res.results], axis=0)
    return out.reshape(16, 2048, D)


# version 25
# speedup vs baseline: 1.2545x; 1.0107x over previous
import numpy as np
import ml_dtypes
from contextlib import ExitStack

import concourse.bass as bass
import concourse.mybir as mybir
from concourse.bass_utils import run_bass_kernel_spmd

F32 = mybir.dt.float32
BF16 = mybir.dt.bfloat16
AF = mybir.ActivationFunctionType
ALU = mybir.AluOpType
AX = mybir.AxisListType.X

NCORES = 8
D = 1024
H0, KD0, HD0 = 16, 64, 128
DV0 = 2048
DFF = 4096
H1, DH1 = 16, 64
E, HMOE = 8, 2048
EPS_LN = 1e-5

ENGINES = ("sync", "tensor", "vector", "scalar", "gpsimd")

RND = 12582912.0



def host_tables(T, B=2):
    NT = B * T
    CAP = NT // 2
    f64 = np.float64
    half = 1.0 / np.power(10000.0, np.linspace(0.0, 1.0, KD0 // 2))
    angle = np.repeat(half, 2)
    idx = np.arange(T, dtype=f64)
    cos0 = np.cos(idx[:, None] * angle[None, :])
    sin0 = np.sin(idx[:, None] * angle[None, :])
    decay = np.log(1.0 - 2.0 ** (-5.0 - np.arange(H0, dtype=f64)))
    ex = np.exp(idx[None, :] * decay[:, None])
    S = np.cumsum(ex, axis=1)
    emq = ex / np.sqrt(S)
    enk = np.exp(-idx[None, :] * decay[:, None]) * (KD0 ** -0.5)
    cosq = np.empty((D, T), f64); sinq = np.empty((D, T), f64)
    cosk = np.empty((D, T), f64); sink = np.empty((D, T), f64)
    for h in range(H0):
        r = slice(KD0 * h, KD0 * h + KD0)
        cosq[r] = cos0.T * emq[h][None, :]
        sinq[r] = sin0.T * emq[h][None, :]
        cosk[r] = cos0.T * enk[h][None, :]
        sink[r] = sin0.T * enk[h][None, :]
    inv_freq = 1.0 / 10000.0 ** (np.arange(0, DH1, 2, dtype=f64) / DH1)
    ang = idx[:, None] * inv_freq[None, :]
    cos1 = np.concatenate([np.cos(ang), np.cos(ang)], axis=1).T
    sin1 = np.concatenate([np.sin(ang), np.sin(ang)], axis=1).T
    cos1e = np.concatenate([cos1, cos1], axis=0)
    sin1e = np.concatenate([sin1, sin1], axis=0)
    R0T = np.zeros((128, 128), f64)
    for i in range(64):
        R0T[2 * i + 1, 2 * i] = -1.0
        R0T[2 * i, 2 * i + 1] = 1.0
    R1T = np.zeros((128, 128), f64)
    for b in (0, 64):
        for i in range(32):
            R1T[b + i + 32, b + i] = -1.0
            R1T[b + i, b + i + 32] = 1.0
    triu = np.triu(np.ones((128, 128), f64))
    bf = ml_dtypes.bfloat16
    return {
        "cosq": cosq.astype(np.float32), "sinq": sinq.astype(np.float32),
        "cosk": cosk.astype(np.float32), "sink": sink.astype(np.float32),
        "cos1": cos1e.astype(np.float32), "sin1": sin1e.astype(np.float32),
        "R0T": R0T.astype(bf), "R1T": R1T.astype(bf),
        "triu": triu.astype(np.float32),
        "identB": np.eye(128, dtype=bf),
        "identF": np.eye(128, dtype=np.float32),
        "onesB": np.ones((1, 128), dtype=bf),
        "eCm1": np.tile((np.arange(E) * CAP - 1.0).astype(np.float32),
                        (128, 1)),
        "iota_tb": (np.arange(128)[:, None] +
                    128 * np.arange(NT // 128)[None, :]).astype(np.int32),
    }


def host_inputs(core, full):
    T = 2048
    x = np.ascontiguousarray(
        full["L0_attn_x"].reshape(16 * T, D)[core * 2 * T:(core + 1) * 2 * T])
    tabs = host_tables(T)
    c = np.ascontiguousarray
    m = {
        "x": x,
        "L0_attn_q_w": full["L0_attn_q_w"], "L0_attn_k_w": full["L0_attn_k_w"],
        "L0_attn_v_w": full["L0_attn_v_w"], "L0_attn_g_w": full["L0_attn_g_w"],
        "L0_attn_o_w": full["L0_attn_o_w"],
        "L0_ffn_W1": full["L0_ffn_W1"], "L0_ffn_W2": full["L0_ffn_W2"],
        "L1_attn_W_qkv": full["L1_attn_W_qkv"],
        "L1_ffn_Wr": full["L1_ffn_Wr"], "L1_ffn_W1": full["L1_ffn_W1"],
        "L1_ffn_W2": full["L1_ffn_W2"],
        "qb0_pp": c(full["L0_attn_q_b"].reshape(8, 128).T),
        "kb0_pp": c(full["L0_attn_k_b"].reshape(8, 128).T),
        "b1ff_pp": c(full["L0_ffn_b1"].reshape(32, 128).T),
        "qkb1_pp": c(full["L1_attn_b_qkv"][:2048].reshape(16, 128).T),
        "b1moe_pp": c(np.stack([full["L1_ffn_b1"][e].reshape(16, 128).T
                                for e in range(E)], 1).reshape(128, E * 16)),
        "br_pp": c(full["L1_ffn_br"].reshape(E, 1)),
        "vb0_row": c(full["L0_attn_v_b"].reshape(1, DV0)),
        "gb0_row": c(full["L0_attn_g_b"].reshape(1, DV0)),
        "ob0_row": c(full["L0_attn_o_b"].reshape(1, D)),
        "b2ff_row": c(full["L0_ffn_b2"].reshape(1, D)),
        "vb1_row": c(full["L1_attn_b_qkv"][2048:].reshape(1, D)),
        "b2moe": full["L1_ffn_b2"],
    }
    m.update(tabs)
    return m



class Sem:
    __slots__ = ("h", "n")

    def __init__(self, h):
        self.h = h
        self.n = 0

    def tok(self):
        return (self, self.n)


class _PhaseCtx:
    def __init__(self, mk, name):
        self.mk = mk
        self.name = name
        self.stack = ExitStack()

    def __enter__(self):
        self.mk.streams = {e: [] for e in ENGINES}
        self.mk.op_seq = 0
        g = self.mk.current_group
        if g is not None:
            idx = g.next_idx
            g.next_idx += 1
            self.saved_ch = self.mk.ch
            self.mk.ch = {e: self.mk.sem(f"gch{idx}_{e}")
                          for e in ENGINES}
            return g.stack
        return self.stack.__enter__()

    def __exit__(self, *exc):
        g = self.mk.current_group
        if g is not None:
            if exc[0] is None:
                g.members.append(self.mk.streams)
            self.mk.ch = self.saved_ch
            self.mk.streams = None
            return False
        if exc[0] is None:
            self.mk.barrier()
            with self.mk.nc.named_scope(self.name):
                self.mk.replay()
        self.stack.__exit__(*exc)
        self.mk.streams = None
        return False


class _GroupCtx:
    def __init__(self, mk, name):
        self.mk = mk
        self.name = name
        self.stack = ExitStack()
        self.members = []
        self.next_idx = 0

    def __enter__(self):
        self.stack.__enter__()
        self.mk.current_group = self
        return self

    def __exit__(self, *exc):
        self.mk.current_group = None
        if exc[0] is None:
            totals = [max(op[0] for e in ENGINES for op in m[e])
                      for m in self.members]
            merged = {e: [] for e in ENGINES}
            for e in ENGINES:
                items = []
                for j, m in enumerate(self.members):
                    for seq, fn in m[e]:
                        items.append((seq / totals[j], j, seq, fn))
                items.sort(key=lambda t: (t[0], t[1], t[2]))
                merged[e] = [(t[2], t[3]) for t in items]
            self.mk.streams = merged
            self.mk.barrier()
            with self.mk.nc.named_scope(self.name):
                self.mk.replay()
            self.mk.streams = None
        self.stack.__exit__(*exc)
        return False


class MK:
    def __init__(self, B=2, T=2048, debug=False, upto=99):
        self.B, self.T = B, T
        self.NT = B * T
        self.NTB = self.NT // 128
        self.TB = T // 128
        self.debug = debug
        self.upto = upto
        self.nc = bass.Bass("TRN2", target_bir_lowering=False)
        self.ctx = ExitStack()
        self.sems = {}
        self.ch = {e: self.sem(f"ch_{e}") for e in ENGINES}
        self.streams = None
        self.current_group = None
        self.op_seq = 0

    def sem(self, name):
        if name not in self.sems:
            h = self.ctx.enter_context(self.nc.semaphore(name))
            self.sems[name] = Sem(h)
        return self.sems[name]

    def psem(self, i, swdge=False):
        g = self.current_group
        pid = (g.next_idx - 1) if g is not None else 0
        pre = "psg" if swdge else "ps"
        return self.sem(f"{pre}{pid}_{i}")

    def _rec(self, eng, method, args, kw, wait, inc):
        flat = []
        for t in wait:
            if t is None:
                continue
            if isinstance(t, list) or (isinstance(t, tuple) and t and
                                       isinstance(t[0], (tuple, list))):
                flat.extend(x for x in t if x is not None)
            else:
                flat.append(t)
        wait = tuple((t[0].h, t[1]) for t in flat)
        inc = tuple((s.h, v) for (s, v) in inc)
        import sys as _sys
        fr = _sys._getframe(2)
        tag = f"{fr.f_code.co_name}:{fr.f_lineno}"

        def fn(e, method=method, args=args, kw=kw, waits=wait, incs=inc,
               tag=tag):
            for h, v in waits:
                e.wait_ge(h, v)
            ins = getattr(e, method)(*args, **kw)
            ins.annotate(tag)
            for h, v in incs:
                ins.then_inc(h, v)
        self.op_seq += 1
        self.streams[eng].append((self.op_seq, fn))

    def op(self, eng, method, *args, wait=(), mark=False, **kw):
        inc = ()
        tok = None
        if eng in ("vector", "scalar", "gpsimd"):
            c = self.ch[eng]
            wait = tuple(wait) + ((c, c.n),)
            c.n += 1
            tok = (c, c.n)
            inc = ((c, 1),)
        elif mark:
            c = self.ch[eng]
            c.n += 1
            tok = (c, c.n)
            inc = ((c, 1),)
        self._rec(eng, method, args, kw, wait, inc)
        return tok

    def dma(self, eng, out, in_, sem, wait=(), indirect_out=None,
            indirect_in=None, **kw):
        sem.n += 16
        if indirect_out is not None or indirect_in is not None:
            oo = (bass.IndirectOffsetOnAxis(indirect_out, 0)
                  if indirect_out is not None else None)
            io = (bass.IndirectOffsetOnAxis(indirect_in, 0)
                  if indirect_in is not None else None)
            self._rec(eng, "indirect_dma_start", (),
                      dict(out=out, out_offset=oo, in_=in_, in_offset=io,
                           **kw), wait, ((sem, 16),))
        else:
            self._rec(eng, "dma_start", (), dict(out=out, in_=in_, **kw),
                      wait, ((sem, 16),))
        return (sem, sem.n)

    def phase(self):
        import sys as _sys
        name = _sys._getframe(1).f_code.co_name
        self.phase_count = getattr(self, "phase_count", 0) + 1
        return _PhaseCtx(self, f"{self.phase_count:02d}_{name}")

    def group(self, name):
        self.phase_count = getattr(self, "phase_count", 0) + 1
        return _GroupCtx(self, f"{self.phase_count:02d}_{name}")

    def barrier(self):
        dma_toks = tuple((s, s.n) for nm, s in self.sems.items()
                         if not nm.startswith("ch_") and s.n > 0
                         and nm != "s_zfill")
        for e in ENGINES:
            self._rec(e, "nop", (), {}, dma_toks, ())
        return

    def replay(self):
        streams = self.streams
        with self.nc.Block() as block:
            @block.sync
            def _(e):
                for _, fn in streams["sync"]:
                    fn(e)

            @block.tensor
            def _(e):
                for _, fn in streams["tensor"]:
                    fn(e)

            @block.vector
            def _(e):
                for _, fn in streams["vector"]:
                    fn(e)

            @block.scalar
            def _(e):
                for _, fn in streams["scalar"]:
                    fn(e)

            @block.gpsimd
            def _(e):
                for _, fn in streams["gpsimd"]:
                    fn(e)

    def scratch(self, name, shape, dtype):
        kind = "ExternalOutput" if self.debug else "Internal"
        return self.nc.dram_tensor(name, shape, dtype, kind=kind).ap()

    def build(self):
        nc = self.nc
        NT, T = self.NT, self.T
        inp = {}
        inp["x"] = nc.dram_tensor("x", [NT, D], F32, kind="ExternalInput").ap()
        for w, shape in [
            ("L0_attn_q_w", [D, D]), ("L0_attn_k_w", [D, D]),
            ("L0_attn_v_w", [D, DV0]), ("L0_attn_g_w", [D, DV0]),
            ("L0_attn_o_w", [DV0, D]),
            ("L0_ffn_W1", [DFF, D]), ("L0_ffn_W2", [D, DFF]),
            ("L1_attn_W_qkv", [3 * D, D]),
            ("L1_ffn_Wr", [D, E]), ("L1_ffn_W1", [E, D, HMOE]),
            ("L1_ffn_W2", [E, HMOE, D]),
            ("qb0_pp", [128, 8]), ("kb0_pp", [128, 8]),
            ("b1ff_pp", [128, 32]), ("qkb1_pp", [128, 16]),
            ("b1moe_pp", [128, E * 16]), ("br_pp", [E, 1]),
            ("vb0_row", [1, DV0]), ("gb0_row", [1, DV0]),
            ("ob0_row", [1, D]), ("b2ff_row", [1, D]),
            ("vb1_row", [1, D]), ("b2moe", [E, D]),
        ]:
            inp[w] = nc.dram_tensor(w, shape, F32, kind="ExternalInput").ap()
        for w, shape, dt in [
            ("cosq", [D, T], F32), ("sinq", [D, T], F32),
            ("cosk", [D, T], F32), ("sink", [D, T], F32),
            ("cos1", [128, T], F32), ("sin1", [128, T], F32),
            ("R0T", [128, 128], BF16), ("R1T", [128, 128], BF16),
            ("triu", [128, 128], F32),
            ("identB", [128, 128], BF16), ("identF", [128, 128], F32),
            ("onesB", [1, 128], BF16), ("eCm1", [128, E], F32),
        ]:
            inp[w] = nc.dram_tensor(w, shape, dt, kind="ExternalInput").ap()
        inp["iota_tb"] = nc.dram_tensor("iota_tb", [128, NT // 128],
                                        mybir.dt.int32,
                                        kind="ExternalInput").ap()
        self.inp = inp
        self.y_out = nc.dram_tensor("y", [NT, D], F32,
                                    kind="ExternalOutput").ap()

        self.qsc = self.scratch("qsc", [D, NT], BF16)
        self.ksc = self.scratch("ksc", [D, NT], BF16)
        self.ksctm = self.scratch("ksctm", [NT, D], BF16)
        self.vsc = self.scratch("vsc", [NT, DV0], BF16)
        self.gsc = self.scratch("gsc", [NT, DV0], BF16)
        self.w1qt = self.scratch("w1qt", [D, DFF], BF16)
        self.w2qt = self.scratch("w2qt", [DFF, D], BF16)
        self.ogf = self.scratch("ogf", [DV0, NT], BF16)
        self.g20 = self.scratch("g20", [NT, D], F32)
        self.x1s = self.scratch("x1s", [NT, D], F32)
        self.qr1 = self.scratch("qr1", [D, NT], BF16)
        self.kr1tm = self.scratch("kr1tm", [NT, D], BF16)
        self.kr1 = self.scratch("kr1", [D, NT], BF16)
        self.v1s = self.scratch("v1s", [NT, D], BF16)
        self.s1 = self.scratch("s1", [NT, D], F32)
        self.CAP = NT // 2
        self.xn1tm = self.scratch("xn1tm", [NT, D], BF16)
        self.xnsort = self.scratch("xnsort", [E * self.CAP, D], BF16)
        self.ysort = self.scratch("ysort", [E * self.CAP, D], F32)

        al = self.ctx.enter_context
        nc_ = nc
        self.xn_ctx = ExitStack()
        self.xn_fm = self.xn_ctx.enter_context(
            nc_.sbuf_tensor("xn_fm", [128, 8 * NT], BF16, side="right"))
        self.xnv = self.xn_fm[:, :].rearrange("p (c t) -> p c t", c=8)
        for nm, shape, dt in [
            ("identB", [128, 128], BF16), ("identF", [128, 128], F32),
            ("onesB", [1, 128], BF16), ("R0T", [128, 128], BF16),
            ("R1T", [128, 128], BF16), ("triu", [128, 128], F32),
            ("qb0", [128, 8], F32), ("kb0", [128, 8], F32),
            ("b1ff", [128, 32], F32), ("qkb1", [128, 16], F32),
            ("b1moe", [128, E * 16], F32), ("br", [E, 1], F32),
            ("ob0", [1, D], BF16), ("b2ff", [1, D], BF16),
            ("vb1", [1, D], BF16), ("b2moe", [E, D], BF16),
            ("mp", [128, self.NTB], F32), ("ind", [128, self.NTB * E], F32),
            ("eps", [128, 4], F32), ("rs", [128, 2], F32),
            ("eCm1", [128, E], F32),
            ("slots", [128, self.NTB], mybir.dt.int32),
            ("iota", [128, self.NTB], mybir.dt.int32),
            ("onesF", [128, 1], F32), ("onesFr", [1, 128], F32),
        ]:
            setattr(self, "c_" + nm, al(nc_.sbuf_tensor("c_" + nm, shape, dt)))

        self.p_const()
        if self.upto >= 1:
            self.p_ln(self.inp["x"], src_sem="p1x", dbg="xn0")
        if self.upto >= 2:
            self.p_proj_qk("q0", self.inp["L0_attn_q_w"], self.c_qb0[:, 0:8],
                           inp["cosq"], inp["sinq"], self.c_R0T, self.qsc)
            self.p_proj_qk("k0", self.inp["L0_attn_k_w"], self.c_kb0[:, 0:8],
                           inp["cosk"], inp["sink"], self.c_R0T, self.ksc,
                           tm_out=self.ksctm)
            with self.group("projtm_qgamma"):
                self.p_proj_tm("v0", self.inp["L0_attn_v_w"], DV0,
                               self.inp["vb0_row"], self.vsc, act=None)
                self.p_proj_tm("g0", self.inp["L0_attn_g_w"], DV0,
                               self.inp["gb0_row"], self.gsc, act=AF.Silu)
                if self.upto >= 3:
                    self.p_quant_gamma()
        if self.upto >= 5:
            with self.group("attn0_qapply"):
                self.p_attn0()
                if self.upto >= 3:
                    self.p_quant_apply()
        elif self.upto >= 3:
            self.p_quant_apply()
        if self.upto >= 4:
            self.p_ffn0()
        if self.upto >= 6:
            self.p_oproj_resid0()
        if self.upto >= 8 and self.upto >= 10:
            with self.group("qkv1_router"):
                self.p_qkv1()
                self.p_router()
        else:
            if self.upto >= 8:
                self.p_qkv1()
            if self.upto >= 10:
                self.p_router()
        self.xn_ctx.close()
        if self.upto >= 9 and self.upto >= 11:
            with self.group("moe_attn1"):
                self.p_attn1()
                self.p_moe()
        elif self.upto >= 9:
            self.p_attn1()
        if self.upto >= 12:
            self.p_resid1()
        self.finalize()
        self.ctx.close()
        return self.nc

    def p_const(self):
        inp = self.inp
        s_ld = self.sem("s_const")
        s_ldg = self.sem("s_constg")
        with self.phase():
            for src, dst in [
                ("identB", self.c_identB), ("identF", self.c_identF),
                ("onesB", self.c_onesB), ("R0T", self.c_R0T),
                ("R1T", self.c_R1T), ("triu", self.c_triu),
                ("qb0_pp", self.c_qb0), ("kb0_pp", self.c_kb0),
                ("b1ff_pp", self.c_b1ff), ("qkb1_pp", self.c_qkb1),
                ("b1moe_pp", self.c_b1moe), ("br_pp", self.c_br),
                ("eCm1", self.c_eCm1), ("iota_tb", self.c_iota),
            ]:
                self.dma("sync", dst[:, :], inp[src], s_ld)
            for src, dst in [
                ("ob0_row", self.c_ob0), ("b2ff_row", self.c_b2ff),
                ("vb1_row", self.c_vb1), ("b2moe", self.c_b2moe),
            ]:
                self.dma("gpsimd", dst[:, :], inp[src], s_ldg)
            self.op("vector", "memset", self.c_eps[:, 0:1], EPS_LN)
            self.op("vector", "memset", self.c_eps[:, 1:2], RND)
            self.op("vector", "memset", self.c_eps[:, 2:3], -RND)
            self.op("vector", "memset", self.c_eps[:, 3:4], 0.0)
            self.op("vector", "memset", self.c_onesF[:, :], 1.0)
            self.op("vector", "memset", self.c_onesFr[:, :], 1.0)

    def p_ln(self, x_ap, src_sem="p1x", dbg=None, resid=None):
        nc = self.nc
        NTB = self.NTB
        GRP = 4 if resid is None else 2
        dbg_ap = (self.scratch("dbg_" + dbg, [self.NT, D], BF16)
                  if (self.debug and dbg) else None)
        with self.phase() as ctx:
            al = ctx.enter_context
            pre = src_sem
            xt = [al(nc.sbuf_tensor(f"{pre}ln_x{i}", [128, GRP * D], F32))
                  for i in range(2)]
            if resid is not None:
                st_b = [al(nc.sbuf_tensor(f"{pre}ln_s{i}", [128, GRP * D], F32))
                        for i in range(2)]
                gt_b = [al(nc.sbuf_tensor(f"{pre}ln_g{i}", [128, GRP * D], F32))
                        for i in range(2)]
                x1t = al(nc.sbuf_tensor(pre + "ln_x1t", [128, D], F32))
                s_s = [self.psem(2), self.psem(3)]
                s_g = [self.psem(4), self.psem(5)]
                s_x1 = self.psem(6)
            xn_tm = al(nc.sbuf_tensor(pre + "ln_xn", [128, D], BF16))
            st = al(nc.sbuf_tensor(pre + "ln_st", [128, 8], F32))
            sq = al(nc.sbuf_tensor(pre + "ln_sq", [128, D], F32))
            pt = al(nc.psum_tensor(pre + "ln_pt", [128, 512], BF16))
            s_x = [self.psem(0), self.psem(1)]
            s_dbg = self.psem(7) if dbg_ap is not None else None
            src_ap = x_ap if resid is None else resid[0]
            x_r = src_ap.rearrange("(g a p) d -> g p a d", a=GRP, p=128)
            if resid is not None:
                s_r = resid[1].rearrange("(g a p) d -> g p a d", a=GRP, p=128)
                g_r = resid[2].rearrange("(g a p) d -> g p a d", a=GRP, p=128)
            free_tok = [(), ()]
            free_sg = [(), ()]
            xn_free = ()
            pt_free = None
            t_sq = None
            t_rs_prev = None
            for g in range(NTB // GRP):
                buf = xt[g % 2]
                ld = self.dma("sync",
                              buf[:, :].rearrange("p (a d) -> p a d", a=GRP),
                              x_r[g], s_x[g % 2], wait=free_tok[g % 2])
                if resid is not None:
                    lds = self.dma("sync",
                                   st_b[g % 2][:, :].rearrange(
                                       "p (a d) -> p a d", a=GRP),
                                   s_r[g], s_s[g % 2], wait=free_sg[g % 2])
                    ldg = self.dma("sync",
                                   gt_b[g % 2][:, :].rearrange(
                                       "p (a d) -> p a d", a=GRP),
                                   g_r[g], s_g[g % 2])
                for a in range(GRP):
                    tb = g * GRP + a
                    xa = buf[:, a * D:(a + 1) * D]
                    if resid is not None:
                        sl = slice(a * D, (a + 1) * D)
                        tmu = self.op("vector", "tensor_mul", sq[:, :],
                                      st_b[g % 2][:, sl], gt_b[g % 2][:, sl],
                                      wait=(lds, ldg) if a == 0 else ())
                        tx1 = self.op(
                            "vector", "tensor_add", x1t[:, :], xa, sq[:, :],
                            wait=((ld,) if a == 0 else ()) +
                                 ((t_sq,) if t_sq else ()) +
                                 ((s_x1, s_x1.n),))
                        self.dma("sync", x_ap[tb * 128:(tb + 1) * 128, :],
                                 x1t[:, :], s_x1, wait=(tx1,))
                        free_sg[g % 2] = (tmu,)
                        xa = x1t[:, :]
                    self.op("vector", "reduce_sum", st[:, 0:1], xa, axis=AX,
                            wait=(ld,) if (a == 0 and resid is None) else ())
                    self.op("vector", "tensor_scalar_mul", st[:, 2:3],
                            st[:, 0:1], 1.0 / D)
                    self.op("vector", "tensor_scalar_sub", sq[:, :], xa,
                            st[:, 2:3])
                    t_sq = self.op("vector", "tensor_mul", sq[:, :],
                                   sq[:, :], sq[:, :])
                    t_ss = self.op("vector", "reduce_sum", st[:, 3:4],
                                   sq[:, :], axis=AX, wait=(t_rs_prev,))
                    t_rs = self.op("scalar", "activation", st[:, 4:5],
                                   st[:, 3:4], AF.Sqrt,
                                   bias=self.c_eps[:, 0:1],
                                   scale=1.0 / D, wait=(t_ss,))
                    t_rs_prev = t_rs
                    self.op("vector", "reciprocal", st[:, 5:6],
                            st[:, 4:5], wait=(t_rs,))
                    t_xn = self.op("vector", "tensor_scalar", xn_tm[:, :], xa,
                                   st[:, 2:3], st[:, 5:6], ALU.subtract,
                                   ALU.mult, wait=xn_free,
                                   mark=True)
                    xn_free = ()
                    if resid is not None:
                        xtk = self.dma("sync",
                                       self.xn1tm[tb * 128:(tb + 1) * 128, :],
                                       xn_tm[:, :], s_x1, wait=(t_xn,))
                        xn_free = xn_free + (xtk,)
                    if dbg_ap is not None:
                        dtk = self.dma("sync",
                                       dbg_ap[tb * 128:(tb + 1) * 128, :],
                                       xn_tm[:, :], s_dbg, wait=(t_xn,))
                        xn_free = xn_free + (dtk,)
                    for half in range(2):
                        toks = []
                        for j in range(4):
                            c = half * 4 + j
                            pw = ()
                            if j == 0:
                                pw = (t_xn,) if half == 0 else ()
                                if pt_free is not None:
                                    pw = pw + (pt_free,)
                            tt = self.op(
                                "tensor", "transpose",
                                pt[:, j * 128:(j + 1) * 128],
                                xn_tm[:, c * 128:(c + 1) * 128],
                                self.c_identB[:, :],
                                wait=pw, mark=(j == 3))
                            if tt:
                                toks.append(tt)
                        pt_free = self.op(
                            "vector", "tensor_copy",
                            self.xnv[:, half * 4:(half + 1) * 4,
                                     tb * 128:(tb + 1) * 128],
                            pt[:, :].rearrange("p (j t) -> p j t", j=4),
                            wait=tuple(toks), mark=True)
                    xn_free = xn_free + (tt,)
                free_tok[g % 2] = (t_sq, t_xn)

    def p_proj_qk(self, name, w_ap, bias_col8, cos_ap, sin_ap, rotT, out_ap,
                  tm_out=None):
        nc = self.nc
        NT, T = self.NT, self.T
        NTB = self.NTB
        NC = NT // 512
        with self.phase() as ctx:
            al = ctx.enter_context
            if tm_out is not None:
                ktm = al(nc.sbuf_tensor(name + "_ktm", [128, NT], BF16))
                psT = al(nc.psum_tensor(name + "_psT", [128, 512], BF16))
                s_tm = self.psem(4)
            w_sb = al(nc.sbuf_tensor(name + "_w", [128, 8 * D], BF16))
            wv = w_sb[:, :].rearrange("p (c f) -> p c f", c=8)
            tab = [al(nc.sbuf_tensor(f"{name}_tab{i}", [128, 2 * T], F32))
                   for i in range(2)]
            qtmp = al(nc.sbuf_tensor(name + "_tmp", [128, NT], BF16))
            t1 = al(nc.sbuf_tensor(name + "_t1", [128, 512], F32))
            t2 = al(nc.sbuf_tensor(name + "_t2", [128, 512], F32))
            qout = al(nc.sbuf_tensor(name + "_out", [128, NT], BF16))
            pq = [al(nc.psum_tensor(f"{name}_pq{i}", [128, 512], F32))
                  for i in range(2)]
            pr = [al(nc.psum_tensor(f"{name}_pr{i}", [128, 512], F32))
                  for i in range(2)]
            s_w = self.psem(0, swdge=True)
            s_tab = [self.psem(1), self.psem(2)]
            s_out = self.psem(3)
            wr_ = w_ap.rearrange("(c p) f -> p c f", p=128)
            ldw_h = self.dma("gpsimd", wv[:, :, 0:128], wr_[:, :, 0:128],
                             s_w)
            ldw = self.dma("gpsimd", wv[:, :, 128:D], wr_[:, :, 128:D],
                           s_w)

            def load_tab(fb, wait=()):
                tbuf = tab[fb % 2]
                self.dma("sync", tbuf[:, 0:T],
                         cos_ap[fb * 128:(fb + 1) * 128, :], s_tab[fb % 2],
                         wait=wait)
                lt2 = self.dma("sync", tbuf[:, T:2 * T],
                               sin_ap[fb * 128:(fb + 1) * 128, :],
                               s_tab[fb % 2])
                return (lt2, lt2)

            tab_tok = {0: load_tab(0), 1: load_tab(1)}
            out_tok = None
            tm_tr = None
            tm_ev = None
            evac_toks = {}
            rot_toks = {}
            add_toks = {}
            last_add = None

            def flush(fbc, nc_, t_evc, tcolc, ltc, lt2c, tbufc):
                nonlocal last_add
                rbank = pr[nc_ % 2]
                rw = [t_evc]
                prev_t2 = add_toks.get((fbc, nc_ - 2) if nc_ >= 2
                                       else (fbc - 1, NC - 2 + nc_))
                if prev_t2:
                    rw.append(prev_t2)
                t_rot = self.op("tensor", "matmul", rbank[:, :],
                                rotT[:, :],
                                qtmp[:, nc_ * 512:(nc_ + 1) * 512],
                                start=True, stop=True,
                                wait=tuple(rw), mark=True)
                rot_toks[(fbc, nc_)] = t_rot
                w1 = [t_evc, (ltc if nc_ == 0 else None),
                      (lt2c if nc_ == 0 else None)]
                if out_tok is not None and nc_ == 0:
                    w1.append(out_tok)
                    if tm_out is not None and tm_tr is not None:
                        w1.append(tm_tr)
                self.op("vector", "tensor_mul", t1[:, :],
                        qtmp[:, nc_ * 512:(nc_ + 1) * 512],
                        tbufc[:, tcolc:tcolc + 512],
                        wait=tuple(t for t in w1 if t))
                self.op("vector", "tensor_mul", t2[:, :], rbank[:, :],
                        tbufc[:, T + tcolc:T + tcolc + 512],
                        wait=(t_rot,))
                last_add = self.op("vector", "tensor_add",
                                   qout[:, nc_ * 512:(nc_ + 1) * 512],
                                   t1[:, :], t2[:, :], mark=True)
                add_toks[(fbc, nc_)] = last_add

            for fb in range(8):
                lt, lt2 = tab_tok[fb]
                tbuf = tab[fb % 2]
                pend = None
                for n in range(NC):
                    s = (n * 512) // T
                    tcol = n * 512 - s * T
                    bank = pq[n % 2]
                    for c in range(8):
                        pw = []
                        if c == 0:
                            if fb == 0 and n < 2:
                                pw.append(ldw_h)
                            if fb == 1 and n < 2:
                                pw.append(ldw)
                            prev = evac_toks.get((fb, n - 2) if n >= 2
                                                 else (fb - 1, NC - 2 + n))
                            if prev:
                                pw.append(prev)
                        self.op("tensor", "matmul", bank[:, :],
                                wv[:, c, fb * 128:(fb + 1) * 128],
                                self.xnv[:, c, n * 512:(n + 1) * 512],
                                start=(c == 0), stop=(c == 7),
                                wait=tuple(pw), mark=(c == 7))
                    t_mm = (self.ch["tensor"], self.ch["tensor"].n)
                    ev_w = [t_mm]
                    prev_rot = rot_toks.get((fb - 1, n))
                    if prev_rot:
                        ev_w.append(prev_rot)
                    prev_t1 = add_toks.get((fb - 1, n))
                    if prev_t1:
                        ev_w.append(prev_t1)
                    t_ev = self.op("scalar", "add",
                                   qtmp[:, n * 512:(n + 1) * 512], bank[:, :],
                                   bias_col8[:, fb:fb + 1],
                                   wait=tuple(ev_w), mark=True)
                    evac_toks[(fb, n)] = t_ev
                    if pend is not None:
                        flush(*pend)
                    pend = (fb, n, t_ev, tcol, lt, lt2, tbuf)
                flush(*pend)
                if fb + 2 < 8:
                    tab_tok[fb + 2] = load_tab(fb + 2, wait=(last_add,))
                out_tok = self.dma("sync", out_ap[fb * 128:(fb + 1) * 128, :],
                                   qout[:, :], s_out, wait=(last_add,))
                if tm_out is not None:
                    ev = None
                    for grp in range(NTB // 4):
                        for j in range(4):
                            tbk = grp * 4 + j
                            pw = ()
                            if j == 0:
                                pw = (last_add,) if grp == 0 else ()
                                if tm_tr is not None:
                                    pw = pw + (tm_ev,)
                            tm_tr = self.op(
                                "tensor", "transpose",
                                psT[:, j * 128:(j + 1) * 128],
                                qout[:, tbk * 128:(tbk + 1) * 128],
                                self.c_identB[:, :], wait=pw, mark=(j == 3))
                        tm_ev = self.op(
                            "scalar", "copy",
                            ktm[:, grp * 512:(grp + 1) * 512], psT[:, :],
                            wait=(tm_tr,) +
                                 (((s_tm, s_tm.n),) if grp == 0 else ()))
                    self.dma("sync",
                             tm_out.rearrange("(i p) f -> p i f", p=128)
                             [:, :, fb * 128:(fb + 1) * 128],
                             ktm[:, :].rearrange("p (i f) -> p i f", f=128),
                             s_tm, wait=(tm_ev,))
                    out_tok = (out_tok[0], out_tok[1])

    def p_proj_tm(self, name, w_ap, FW, bias_ap, out_ap, act=None):
        nc = self.nc
        NTB = self.NTB
        NF = FW // 512
        with self.phase() as ctx:
            al = ctx.enter_context
            w_sb = al(nc.sbuf_tensor(name + "_w", [128, 8 * FW], BF16))
            wv = w_sb[:, :].rearrange("p (c f) -> p c f", c=8)
            bias_row = al(nc.sbuf_tensor(name + "_b", [1, FW], BF16))
            vout = al(nc.sbuf_tensor(name + "_out", [128, 2 * FW], BF16))
            pb = [al(nc.psum_tensor(f"{name}_p{i}", [128, 512], F32))
                  for i in range(2)]
            s_w = self.psem(0, swdge=True)
            s_out = self.psem(3)
            wr_ = w_ap.rearrange("(c p) f -> p c f", p=128)
            ldb = self.dma("gpsimd", bias_row[:, :], bias_ap, s_w)
            ldw_h = self.dma("gpsimd", wv[:, :, 0:512], wr_[:, :, 0:512],
                             s_w)
            ldw = self.dma("gpsimd", wv[:, :, 512:FW], wr_[:, :, 512:FW],
                           s_w)
            ev_tok = {}
            out_dma = [None, None]
            k = 0
            for tb in range(NTB):
                obuf = vout[:, (tb % 2) * FW:(tb % 2 + 1) * FW]
                for fc in range(NF):
                    bank = pb[k % 2]
                    for c in range(8):
                        pw = []
                        if c == 0:
                            if k < 2:
                                pw.append(ldw_h if k == 0 else ldw)
                            prev = ev_tok.get(k - 2)
                            if prev:
                                pw.append(prev)
                        self.op("tensor", "matmul", bank[:, :],
                                self.xnv[:, c, tb * 128:(tb + 1) * 128],
                                wv[:, c, fc * 512:(fc + 1) * 512],
                                start=(c == 0), stop=False,
                                wait=tuple(pw))
                    self.op("tensor", "matmul", bank[:, :],
                            self.c_onesB[0:1, :],
                            bias_row[0:1, fc * 512:(fc + 1) * 512],
                            start=False, stop=True, mark=True)
                    t_mm = (self.ch["tensor"], self.ch["tensor"].n)
                    ew = [t_mm]
                    if fc == 0 and out_dma[tb % 2] is not None:
                        ew.append((s_out, s_out.n))
                    if act is None:
                        t_ev = self.op("vector", "tensor_copy",
                                       obuf[:, fc * 512:(fc + 1) * 512],
                                       bank[:, :], wait=tuple(ew), mark=True)
                    else:
                        t_ev = self.op("scalar", "activation",
                                       obuf[:, fc * 512:(fc + 1) * 512],
                                       bank[:, :], act, bias=0.0, scale=1.0,
                                       wait=tuple(ew), mark=True)
                    ev_tok[k] = t_ev
                    k += 1
                out_dma[tb % 2] = self.dma(
                    "sync", out_ap[tb * 128:(tb + 1) * 128, :], obuf,
                    s_out, wait=(ev_tok[k - 1], ev_tok[k - NF]))

    def p_quant_gamma(self):
        nc = self.nc
        W1, W2 = self.inp["L0_ffn_W1"], self.inp["L0_ffn_W2"]
        with self.phase() as ctx:
            al = ctx.enter_context
            wt = [al(nc.sbuf_tensor(f"qg_w{i}", [128, 2048], F32))
                  for i in range(2)]
            parts = al(nc.sbuf_tensor("qg_parts", [128, 48], F32))
            strow = al(nc.sbuf_tensor("qg_st", [1, 56], F32))
            ptot = al(nc.psum_tensor("qg_ptot", [1, 48], F32))
            pbc = al(nc.psum_tensor("qg_pbc", [128, 2], F32))
            s_w = [self.psem(0), self.psem(1)]
            k = 0
            free = [(), ()]
            col = 0
            for (w_ap, ntile, width) in ((W1, 32, D), (W2, 8, DFF)):
                for tb in range(ntile):
                    for hw in range(width // 2048 if width > 2048 else 1):
                        wslice = w_ap[tb * 128:(tb + 1) * 128,
                                      hw * 2048:hw * 2048 + min(2048, width)]
                        buf = wt[k % 2]
                        ld = self.dma("scalar", buf[:, 0:min(2048, width)],
                                      wslice, s_w[k % 2], wait=free[k % 2])
                        free[k % 2] = (self.op(
                            "vector", "reduce_sum", parts[:, col:col + 1],
                            buf[:, 0:min(2048, width)], axis=AX,
                            apply_absolute_value=True, wait=(ld,)),)
                        k += 1
                        col += 1
            assert col == 48
            t_mm = self.op("tensor", "matmul", ptot[:, 0:48],
                           self.c_onesF[:, 0:1], parts[:, 0:48],
                           start=True, stop=True,
                           wait=free[0] + free[1], mark=True)
            self.op("vector", "tensor_copy", strow[0:1, 0:48], ptot[0:1, :],
                    wait=(t_mm,))
            self.op("vector", "reduce_sum", strow[0:1, 48:49],
                    strow[0:1, 0:32], axis=AX)
            self.op("vector", "reduce_sum", strow[0:1, 49:50],
                    strow[0:1, 32:48], axis=AX)
            self.op("vector", "tensor_scalar", strow[0:1, 50:51],
                    strow[0:1, 48:49], 1.0 / (DFF * D), EPS_LN,
                    ALU.mult, ALU.add)
            self.op("vector", "tensor_scalar", strow[0:1, 51:52],
                    strow[0:1, 49:50], 1.0 / (DFF * D), EPS_LN,
                    ALU.mult, ALU.add)
            t_rc = self.op("vector", "reciprocal", strow[0:1, 52:54],
                           strow[0:1, 50:52])
            t_bc = self.op("tensor", "matmul", pbc[:, :],
                           self.c_onesFr[0:1, :],
                           strow[0:1, 52:54], start=True, stop=True,
                           wait=(t_rc,), mark=True)
            self.op("vector", "tensor_copy", self.c_rs[:, :], pbc[:, :],
                    wait=(t_bc,))

    def p_quant_apply(self):
        nc = self.nc
        W1, W2 = self.inp["L0_ffn_W1"], self.inp["L0_ffn_W2"]
        with self.phase() as ctx:
            al = ctx.enter_context
            wt = [al(nc.sbuf_tensor(f"qa_w{i}", [128, 1024], F32))
                  for i in range(2)]
            qtmp = al(nc.sbuf_tensor("qa_qtmp", [128, 1024], F32))
            qbf = [al(nc.sbuf_tensor(f"qa_qbf{i}", [128, 1024], BF16))
                   for i in range(2)]
            tbuf = [al(nc.sbuf_tensor(f"qa_tbuf{i}", [128, 1024], BF16))
                    for i in range(2)]
            ptr = al(nc.psum_tensor("qa_ptr", [128, 512], BF16))
            s_w = [self.psem(0), self.psem(1)]
            s_out = self.psem(2)
            k = 0
            free = [(), ()]
            ptr_free = None
            tq_free = {}
            for (w_ap, out_ap, ntile, nch, col) in (
                    (W1, self.w1qt, 32, 1, 0), (W2, self.w2qt, 8, 4, 1)):
                for tb in range(ntile):
                    for hw in range(nch):
                        buf = wt[k % 2]
                        ld = self.dma("scalar", buf[:, :],
                                      w_ap[tb * 128:(tb + 1) * 128,
                                           hw * 1024:(hw + 1) * 1024],
                                      s_w[k % 2], wait=free[k % 2])
                        qb = qbf[k % 2]
                        tb_ = tbuf[k % 2]
                        t_q1 = self.op("vector", "tensor_scalar",
                                       qtmp[:, :], buf[:, :],
                                       self.c_rs[:, col:col + 1],
                                       self.c_eps[:, 1:2],
                                       ALU.mult, ALU.add, wait=(ld,))
                        self.op("vector", "tensor_scalar", qtmp[:, :],
                                qtmp[:, :], self.c_eps[:, 1:2], 1.0,
                                ALU.subtract, ALU.min)
                        t_q = self.op("vector", "tensor_scalar_max",
                                      qb[:, :], qtmp[:, :], -1.0,
                                      wait=(tq_free.get(k % 2),))
                        free[k % 2] = (t_q1,)
                        ev = []
                        for grp in range(2):
                            for j in range(4):
                                c = grp * 4 + j
                                pw = ()
                                if j == 0:
                                    pw = (t_q,)
                                    if ptr_free is not None:
                                        pw = pw + (ptr_free,)
                                tt = self.op("tensor", "transpose",
                                             ptr[:, j * 128:(j + 1) * 128],
                                             qb[:, c * 128:(c + 1) * 128],
                                             self.c_identB[:, :],
                                             wait=pw, mark=(j == 3))
                            evw = [tt]
                            if grp == 0:
                                evw.append((s_out, s_out.n))
                            ptr_free = self.op(
                                "scalar", "copy",
                                tb_[:, grp * 512:(grp + 1) * 512],
                                ptr[:, :], wait=tuple(evw))
                            ev.append(ptr_free)
                        tq_free[k % 2] = tt
                        self.dma("scalar",
                                 out_ap.rearrange("(c p) f -> p c f", p=128)
                                 [:, hw * 8:(hw + 1) * 8,
                                  tb * 128:(tb + 1) * 128],
                                 tb_[:, :].rearrange("p (c f) -> p c f",
                                                     c=8),
                                 s_out, wait=(ev[-1],))
                        k += 1

    def p_ffn0(self):
        nc = self.nc
        NT = self.NT
        NCH = NT // 512
        with self.phase() as ctx:
            al = ctx.enter_context
            w1t = al(nc.sbuf_tensor("f0_w1t", [128, 8 * DFF], BF16))
            w1v = w1t[:, :].rearrange("p (c f) -> p c f", c=8)
            w2q = [al(nc.sbuf_tensor(f"f0_w2q{i}", [128, 4 * D], BF16))
                   for i in range(2)]
            h_sb = al(nc.sbuf_tensor("f0_h", [128, 32 * 512], BF16))
            hv = h_sb[:, :].rearrange("p (f t) -> p f t", f=32)
            g2t = [al(nc.sbuf_tensor(f"f0_g2_{i}", [128, 512], F32))
                   for i in range(2)]
            pp = [al(nc.psum_tensor(f"f0_p{i}", [128, 512], F32))
                  for i in range(8)]
            s_w1 = self.psem(0)
            s_w2 = [self.psem(1), self.psem(2)]
            s_g2 = [self.psem(3), self.psem(4)]
            ld1 = self.dma("sync", w1v,
                           self.w1qt.rearrange("(c p) f -> p c f", p=128),
                           s_w1)
            w2r = self.w2qt.rearrange("(c p) f -> p c f", p=128)

            def load_w2(g, wait=()):
                q = g % 8
                return self.dma("sync",
                                w2q[g % 2][:, :].rearrange(
                                    "p (c f) -> p c f", c=4),
                                w2r[:, q * 4:(q + 1) * 4, :], s_w2[g % 2],
                                wait=wait)

            w2tok = {0: load_w2(0), 1: load_w2(1)}
            g2free = {}
            kg = 0
            for tc in range(NCH):
                relu_toks = []
                for fb in range(32):
                    bank = pp[fb % 2]
                    for c in range(8):
                        pw = []
                        if c == 0:
                            if fb < 2:
                                if tc == 0 and fb == 0:
                                    pw.append(ld1)
                                gf = g2free.get(fb)
                                if gf:
                                    pw.append(gf)
                            else:
                                pw.append(relu_toks[fb - 2])
                        self.op("tensor", "matmul", bank[:, :],
                                w1v[:, c, fb * 128:(fb + 1) * 128],
                                self.xnv[:, c, tc * 512:(tc + 1) * 512],
                                start=(c == 0), stop=(c == 7),
                                wait=tuple(pw), mark=(c == 7))
                    t_mm = (self.ch["tensor"], self.ch["tensor"].n)
                    relu_toks.append(self.op(
                        "scalar", "activation", hv[:, fb, :], bank[:, :],
                        AF.Relu, bias=self.c_b1ff[:, fb:fb + 1], scale=1.0,
                        wait=(t_mm,)))
                for q in range(8):
                    g = tc * 8 + q
                    for tb in range(4):
                        for dc in range(2):
                            bank = pp[tb * 2 + dc]
                            for j in range(4):
                                fc = q * 4 + j
                                pw = []
                                if j == 0:
                                    if tb == 0 and dc == 0:
                                        pw.append(w2tok[g])
                                    if q == 0:
                                        if tb == 0 and dc == 0:
                                            pw.append(relu_toks[-1])
                                        gf = g2free.get(tb * 2 + dc)
                                        if gf:
                                            pw.append(gf)
                                pe_tok = self.op(
                                    "tensor", "matmul", bank[:, :],
                                    hv[:, fc, tb * 128:(tb + 1) * 128],
                                    w2q[g % 2][:, :].rearrange(
                                        "p (c f) -> p c f", c=4)
                                    [:, j, dc * 512:(dc + 1) * 512],
                                    start=(fc == 0), stop=False,
                                    wait=tuple(pw),
                                    mark=(tb == 3 and dc == 1 and j == 3))
                    if g + 2 < NCH * 8:
                        w2tok[g + 2] = load_w2(g + 2, wait=(pe_tok,))
                for tb in range(4):
                    for dc in range(2):
                        bank = pp[tb * 2 + dc]
                        t_b = self.op("tensor", "matmul", bank[:, :],
                                      self.c_onesB[0:1, :],
                                      self.c_b2ff[0:1, dc * 512:(dc + 1) * 512],
                                      start=False, stop=True, mark=True)
                        gb = g2t[kg % 2]
                        t_g = self.op("scalar", "activation", gb[:, :],
                                      bank[:, :], AF.Silu, bias=0.0,
                                      scale=1.0,
                                      wait=(t_b,
                                            (s_g2[kg % 2], s_g2[kg % 2].n)))
                        self.dma(
                            "sync",
                            self.g20[(tc * 4 + tb) * 128:
                                     (tc * 4 + tb + 1) * 128,
                                     dc * 512:(dc + 1) * 512],
                            gb[:, :], s_g2[kg % 2], wait=(t_g,))
                        g2free[tb * 2 + dc] = t_g
                        kg += 1
        return

    def p_attn0(self):
        nc = self.nc
        B, T, TB = self.B, self.T, self.TB
        with self.phase() as ctx:
            al = ctx.enter_context
            qk = [al(nc.sbuf_tensor(f"a0_qk{i}", [128, 2 * T], BF16))
                  for i in range(2)]
            vx = [al(nc.sbuf_tensor(f"a0_vx{i}", [128, TB * 132], BF16))
                  for i in range(2)]
            sg = [al(nc.sbuf_tensor(f"a0_sg{i}", [128, TB * 128], BF16))
                  for i in range(2)]
            og = [al(nc.sbuf_tensor(f"a0_og{i}", [128, T], BF16))
                  for i in range(2)]
            oraw = [al(nc.sbuf_tensor(f"a0_or{i}", [128, TB * 132], F32))
                    for i in range(2)]
            onb = al(nc.sbuf_tensor("a0_onb", [128, TB * 128], F32))
            sqb = al(nc.sbuf_tensor("a0_sqb", [128, TB * 128], F32))
            nrmb = al(nc.sbuf_tensor("a0_nrmb", [128, TB * 128], F32))
            ogb = [al(nc.sbuf_tensor(f"a0_ogb{i}", [128, TB * 128], BF16))
                   for i in range(2)]
            S_sb = [al(nc.sbuf_tensor(f"a0_S{i}", [128, 132], BF16))
                    for i in range(4)]
            ktm = [al(nc.sbuf_tensor(f"a0_ktm{i}", [128, TB * 64], BF16))
                   for i in range(2)]
            dTs = [al(nc.sbuf_tensor(f"a0_dT{i}", [128, 128], BF16))
                   for i in range(2)]
            dNs = [al(nc.sbuf_tensor(f"a0_dN{i}", [128, 128], BF16))
                   for i in range(2)]
            st = al(nc.sbuf_tensor("a0_st", [128, 8 * TB], F32))
            stv = st[:, :].rearrange("p (k i) -> p k i", k=8)
            psS = [al(nc.psum_tensor(f"a0_psS{i}", [64, 132], F32))
                   for i in range(2)]
            psD = [al(nc.psum_tensor(f"a0_psD{i}", [128, 256], F32))
                   for i in range(2)]
            psO = [al(nc.psum_tensor(f"a0_psO{i}", [128, 132], F32))
                   for i in range(2)]
            psT2 = al(nc.psum_tensor("a0_psT2", [128, 512], BF16))
            s_qk = [self.psem(0), self.psem(1)]
            s_kt = [self.psem(2), self.psem(3)]
            s_vx = [self.psem(4), self.psem(5)]
            s_sg = [self.psem(6), self.psem(7)]
            s_og = [self.psem(8), self.psem(9)]
            mset = []
            for i in range(2):
                mset.append(self.op(
                    "vector", "memset",
                    vx[i][:, :].rearrange("p (i c) -> p i c", c=132)
                    [:, :, 128:132], 1.0))
            vsc_r = self.vsc.rearrange("(b i p) c -> b p i c", i=TB, p=128)
            ktm_r = self.ksctm.rearrange("(b i p) c -> b p i c", i=TB, p=128)
            gsc_r = self.gsc.rearrange("(b i p) c -> b p i c", i=TB, p=128)
            tkD = {}
            tkO = {}
            last_snap = {}
            t_av = {}
            b_on_last = None
            b_sq_last = None
            b_ssq = None
            b_og = None
            b_abs = None
            b_nrm_last = None
            b_pmn = None
            b_sd = None
            pe_group = {}
            pe_head = {}
            kp = kh = 0
            prev_c = None
            pend_b = None
            cstate = {"tkT2": None, "c_tr_last": None}

            def emit_c(kh_, s_, h_, pv_, b_og_):
                ev_last = None
                for grp in range(TB // 4):
                    for j in range(4):
                        i = grp * 4 + j
                        pw = ()
                        if j == 0:
                            pw = ((b_og_, cstate["tkT2"]) if grp == 0
                                  else (cstate["tkT2"],))
                        cstate["c_tr_last"] = self.op(
                            "tensor", "transpose",
                            psT2[:, j * 128:(j + 1) * 128],
                            ogb[kh_ % 2][:, i * 128:(i + 1) * 128],
                            self.c_identB[:, :], wait=pw,
                            mark=(j == 3))
                    ev_last = self.op(
                        "scalar", "copy",
                        og[pv_][:, grp * 512:(grp + 1) * 512],
                        psT2[:, :],
                        wait=(cstate["c_tr_last"],
                              (s_og[pv_], s_og[pv_].n)
                              if grp == 0 else None))
                    cstate["tkT2"] = ev_last
                self.dma("sync",
                         self.ogf[128 * h_:128 * (h_ + 1),
                                  s_ * T:(s_ + 1) * T],
                         og[pv_][:, :], s_og[pv_], wait=(ev_last,))

            def emit_b(kh_, s_, h_, pv_, l_s_, a_or_last_, a_or_prev_):
                nonlocal b_abs, b_on_last, b_sq_last, b_ssq, b_og, \
                    b_nrm_last, b_pmn, b_sd
                orv_ = oraw[pv_][:, :].rearrange("p (i c) -> p i c", c=132)
                b_abs = self.op("scalar", "activation",
                                stv[:, 0, :], orv_[:, :, 128],
                                AF.Abs, bias=0.0, scale=1.0,
                                wait=(a_or_last_, a_or_prev_,
                                      b_nrm_last))
                self.op("vector", "tensor_scalar_max",
                        stv[:, 1, :], stv[:, 0, :], 1.0,
                        wait=(b_abs,))
                b_rr = self.op("vector", "reciprocal",
                               stv[:, 2, :], stv[:, 1, :])
                prev_sq = b_sq_last
                for i in range(TB):
                    b_on_last = self.op(
                        "vector", "tensor_scalar_mul",
                        onb[:, i * 128:(i + 1) * 128],
                        orv_[:, i, 0:128], stv[:, 2, i:i + 1],
                        wait=(prev_sq,) if i == 0 else ())
                b_mn = self.op("vector", "reduce_sum", stv[:, 3, :],
                               onb[:, :].rearrange(
                                   "p (i c) -> p i c", c=128),
                               axis=AX, wait=(b_pmn,))
                b_nmn = self.op("vector", "tensor_scalar_mul",
                                stv[:, 4, :], stv[:, 3, :],
                                -1.0 / 128, wait=(b_mn, b_sq_last))
                b_pmn = self.op("vector", "tensor_scalar_mul",
                                stv[:, 5, :], stv[:, 3, :],
                                1.0 / 128)
                prev_ssq = b_ssq
                for i in range(TB):
                    b_sq_last = self.op(
                        "scalar", "activation",
                        sqb[:, i * 128:(i + 1) * 128],
                        onb[:, i * 128:(i + 1) * 128], AF.Square,
                        bias=stv[:, 4, i:i + 1], scale=1.0,
                        wait=(b_nmn, b_on_last, prev_ssq)
                        if i == 0 else ())
                b_ssq = self.op("vector", "reduce_sum", stv[:, 6, :],
                                sqb[:, :].rearrange(
                                    "p (i c) -> p i c", c=128),
                                axis=AX, wait=(b_sq_last, b_sd))
                b_sd = self.op("scalar", "activation", stv[:, 7, :],
                               stv[:, 6, :], AF.Sqrt,
                               bias=self.c_eps[:, 0:1],
                               scale=1.0 / 128, wait=(b_ssq,))
                b_rstd = self.op("vector", "reciprocal",
                                 stv[:, 0, :], stv[:, 7, :],
                                 wait=(b_sd,))
                prev_og = b_og
                for i in range(TB):
                    b_nrm_last = self.op(
                        "vector", "tensor_scalar",
                        nrmb[:, i * 128:(i + 1) * 128],
                        onb[:, i * 128:(i + 1) * 128],
                        stv[:, 5, i:i + 1], stv[:, 0, i:i + 1],
                        ALU.subtract, ALU.mult,
                        wait=(b_rstd, prev_og) if i == 0 else ())
                b_og = self.op("gpsimd", "tensor_mul",
                               ogb[kh_ % 2][:, :],
                               nrmb[:, :], sg[pv_][:, :],
                               wait=(l_s_, cstate["c_tr_last"],
                                     b_nrm_last))
                return (kh_, s_, h_, pv_, b_og)

            for s in range(B):
                for hp in range(8):
                    pq = kp % 2
                    self.dma("sync", qk[pq][:, 0:T],
                             self.qsc[hp * 128:(hp + 1) * 128,
                                      s * T:(s + 1) * T], s_qk[pq],
                             wait=(pe_group.get(pq),))
                    l_k = self.dma("sync", qk[pq][:, T:2 * T],
                                   self.ksc[hp * 128:(hp + 1) * 128,
                                            s * T:(s + 1) * T], s_qk[pq])
                    for hh in range(2):
                        h = 2 * hp + hh
                        rb = 64 * hh
                        pv = kh % 2
                        wv = (pe_head.get(pv), mset[pv] if kh < 2 else None)
                        l_v = self.dma(
                            "sync",
                            vx[pv][:, :].rearrange("p (i c) -> p i c", c=132)
                            [:, :, 0:128],
                            vsc_r[s][:, :, 128 * h:128 * h + 128],
                            s_vx[pv], wait=wv)
                        l_s = self.dma(
                            "sync",
                            sg[pv][:, :].rearrange("p (i c) -> p i c", c=128),
                            gsc_r[s][:, :, 128 * h:128 * h + 128],
                            s_sg[pv], wait=(b_og,))
                        l_kt = self.dma(
                            "sync",
                            ktm[pv][:, :].rearrange("p (i c) -> p i c", c=64),
                            ktm_r[s][:, :, 64 * h:64 * h + 64],
                            s_kt[pv], wait=(pe_head.get(pv),))
                        qs = qk[pq][rb:rb + 64, 0:T]
                        ks = qk[pq][rb:rb + 64, T:2 * T]
                        orv = oraw[pv][:, :].rearrange("p (i c) -> p i c",
                                                       c=132)
                        ktv = ktm[pv][:, :].rearrange("p (i c) -> p i c",
                                                      c=64)
                        vxv = vx[pv][:, :].rearrange("p (i c) -> p i c",
                                                    c=132)
                        snap = {-2: last_snap.get(0),
                                -1: last_snap.get(1)}
                        buf = lambda j: 2 * (j % 2) + ((j // 2) % 2)
                        dT_tok = {}

                        def emit_scores(j, first):
                            pj = j % 2
                            jb = slice(j * 128, (j + 1) * 128)
                            jp = slice((j - 1) * 128, j * 128)
                            t3 = self.op("tensor", "matmul",
                                         psD[pj][:, 0:128],
                                         ks[:, jb], qs[:, jb],
                                         start=True, stop=True,
                                         wait=(tkD.get(pj),
                                               l_k if first else None),
                                         mark=(j == 0))
                            if j > 0:
                                t3 = self.op("tensor", "matmul",
                                             psD[pj][:, 128:256],
                                             ks[:, jp], qs[:, jb],
                                             start=True, stop=True,
                                             mark=True)
                            t4 = self.op("vector", "tensor_mul",
                                         dTs[pj][:, :], psD[pj][:, 0:128],
                                         self.c_triu[:, :],
                                         wait=(t3, t_av.get(pj)))
                            t4n = None
                            if j > 0:
                                t4n = self.op("vector", "tensor_copy",
                                              dNs[pj][:, :],
                                              psD[pj][:, 128:256],
                                              wait=(t3,))
                            tkD[pj] = (t4, t4n) if j > 0 else (t4,)
                            dT_tok[j] = (t4, t4n)

                        emit_scores(0, hh == 0)
                        for i in range(TB):
                            pi = i % 2
                            ib = slice(i * 128, (i + 1) * 128)
                            vxi = vxv[:, i, :]
                            if i + 1 < TB:
                                emit_scores(i + 1, False)
                            t6 = self.op("tensor", "matmul",
                                         psS[pi][:, :],
                                         ktv[:, i, :], vxi,
                                         start=(i < 2), stop=(i >= TB - 2),
                                         skip_group_check=True,
                                         wait=(snap.get(i - 2),
                                               l_kt if i == 0 else None,
                                               l_v if i == 0 else None),
                                         mark=True)
                            t4, t4n = dT_tok[i]
                            t5 = self.op("tensor", "matmul", psO[pi][:, :],
                                         dTs[pi][:, :], vxi,
                                         start=True, stop=(i == 0),
                                         wait=(t4, tkO.get(pi)),
                                         mark=(i == 0))
                            if i > 0:
                                t5 = self.op("tensor", "matmul",
                                             psO[pi][:, :],
                                             dNs[pi][:, :], vxv[:, i - 1, :],
                                             start=False, stop=(i == 1),
                                             wait=(t4n,), mark=(i == 1))
                            for j in (i - 2, i - 3):
                                if j >= 0:
                                    t5 = self.op(
                                        "tensor", "matmul", psO[pi][:, :],
                                        qs[:, ib],
                                        S_sb[buf(j)][rb:rb + 64, :],
                                        start=False, stop=(j == min(
                                            x for x in (i - 2, i - 3)
                                            if x >= 0)),
                                        wait=(snap[j],), mark=True)
                            snap[i] = self.op("scalar", "copy",
                                              S_sb[buf(i)][rb:rb + 64, :],
                                              psS[pi][:, :], wait=(t6, t5))
                            tkO[pi] = self.op("scalar", "copy",
                                              orv[:, i, :], psO[pi][:, :],
                                              wait=(t5, b_on_last,
                                                    (b_abs if i == 0
                                                     else None)))
                            t_av[pi] = t5
                        last_snap = {0: snap[TB - 2], 1: snap[TB - 1]}
                        a_or_last = tkO[(TB - 1) % 2]
                        a_or_prev = tkO[TB % 2]
                        pe_head[pv] = t5
                        if pend_b is not None:
                            bret = emit_b(*pend_b)
                            if prev_c is not None:
                                emit_c(*prev_c)
                            prev_c = bret
                        pend_b = (kh, s, h, pv, l_s,
                                  a_or_last, a_or_prev)
                        kh += 1
                    pe_group[pq] = t5
                    kp += 1
            if pend_b is not None:
                bret = emit_b(*pend_b)
                if prev_c is not None:
                    emit_c(*prev_c)
                emit_c(*bret)

    def p_oproj(self):
        nc = self.nc
        NTB = self.NTB
        with self.phase() as ctx:
            al = ctx.enter_context
            ow = al(nc.sbuf_tensor("op_ow", [128, 16 * D], BF16))
            owv = ow[:, :].rearrange("p (c f) -> p c f", c=16)
            ogb = [al(nc.sbuf_tensor(f"op_og{i}", [128, 16 * 128], BF16))
                   for i in range(2)]
            s0t = [al(nc.sbuf_tensor(f"op_s0{i}", [128, D], F32))
                   for i in range(2)]
            pb = [al(nc.psum_tensor(f"op_p{i}", [128, 512], F32))
                  for i in range(2)]
            s_w = self.psem(0, swdge=True)
            s_og = [self.psem(1), self.psem(2)]
            s_out = [self.psem(3), self.psem(4)]
            ldw = self.dma("gpsimd", owv,
                           self.inp["L0_attn_o_w"].rearrange(
                               "(c p) f -> p c f", p=128), s_w)
            ogr = self.ogf.rearrange("(c p) t -> p c t", p=128)

            def load_og(tb, wait=()):
                return self.dma("sync",
                                ogb[tb % 2][:, :].rearrange(
                                    "p (c t) -> p c t", c=16),
                                ogr[:, :, tb * 128:(tb + 1) * 128],
                                s_og[tb % 2], wait=wait)

            og_tok = {0: load_og(0), 1: load_og(1)}
            ev_tok = {}
            pe_done = {}
            for tb in range(NTB):
                for fc in range(2):
                    bank = pb[fc]
                    for c in range(16):
                        pw = []
                        if c == 0:
                            if fc == 0:
                                pw.append(og_tok[tb])
                                if tb == 0:
                                    pw.append(ldw)
                            prev = ev_tok.get((tb - 1, fc))
                            if prev:
                                pw.append(prev)
                        self.op("tensor", "matmul", bank[:, :],
                                ogb[tb % 2][:, :].rearrange(
                                    "p (c t) -> p c t", c=16)[:, c, :],
                                owv[:, c, fc * 512:(fc + 1) * 512],
                                start=(c == 0), stop=False, wait=tuple(pw))
                    t_b = self.op("tensor", "matmul", bank[:, :],
                                  self.c_onesB[0:1, :],
                                  self.c_ob0[0:1, fc * 512:(fc + 1) * 512],
                                  start=False, stop=True,
                                  mark=True)
                    ev_tok[(tb, fc)] = self.op(
                        "vector", "tensor_copy",
                        s0t[tb % 2][:, fc * 512:(fc + 1) * 512], bank[:, :],
                        wait=(t_b, (s_out[tb % 2], s_out[tb % 2].n))
                        if fc == 0 else (t_b,))
                    if fc == 1:
                        pe_done[tb] = t_b
                if tb + 2 < NTB:
                    og_tok[tb + 2] = load_og(tb + 2, wait=(pe_done[tb],))
                self.dma("sync", self.s0[tb * 128:(tb + 1) * 128, :],
                         s0t[tb % 2][:, :], s_out[tb % 2],
                         wait=(ev_tok[(tb, 1)],))

    def p_oproj_resid0(self):
        nc = self.nc
        NTB = self.NTB
        with self.phase() as ctx:
            al = ctx.enter_context
            ow = al(nc.sbuf_tensor("op_ow", [128, 16 * D], BF16))
            owv = ow[:, :].rearrange("p (c f) -> p c f", c=16)
            ogb = [al(nc.sbuf_tensor(f"op_og{i}", [128, 16 * 128], BF16))
                   for i in range(2)]
            s0t = [al(nc.sbuf_tensor(f"op_s0{i}", [128, D], F32))
                   for i in range(2)]
            xg = [al(nc.sbuf_tensor(f"op_xg{i}", [128, 2 * D], F32))
                  for i in range(2)]
            x1t = [al(nc.sbuf_tensor(f"op_x1{i}", [128, D], F32))
                   for i in range(2)]
            xn_tm = [al(nc.sbuf_tensor(f"op_xn{i}", [128, D], BF16))
                     for i in range(2)]
            sq = [al(nc.sbuf_tensor(f"op_sq{i}", [128, D], F32))
                  for i in range(2)]
            st = al(nc.sbuf_tensor("op_st", [128, 16], F32))
            pb = [al(nc.psum_tensor(f"op_p{i}", [128, 512], F32))
                  for i in range(4)]
            pt = al(nc.psum_tensor("op_pt", [128, 512], BF16))
            s_w = self.psem(0, swdge=True)
            s_og = [self.psem(1), self.psem(2)]
            s_xg = [self.psem(3), self.psem(4)]
            s_x1 = self.psem(5)
            ldw = self.dma("gpsimd", owv,
                           self.inp["L0_attn_o_w"].rearrange(
                               "(c p) f -> p c f", p=128), s_w)
            ogr = self.ogf.rearrange("(c p) t -> p c t", p=128)
            x_ap = self.inp["x"]

            def load_og(tb, wait=()):
                return self.dma("sync",
                                ogb[tb % 2][:, :].rearrange(
                                    "p (c t) -> p c t", c=16),
                                ogr[:, :, tb * 128:(tb + 1) * 128],
                                s_og[tb % 2], wait=wait)

            def load_xg(tb, wait=()):
                par = tb % 2
                self.dma("scalar", xg[par][:, 0:D],
                         x_ap[tb * 128:(tb + 1) * 128, :], s_xg[par],
                         wait=wait)
                return self.dma("scalar", xg[par][:, D:2 * D],
                                self.g20[tb * 128:(tb + 1) * 128, :],
                                s_xg[par])

            og_tok = {0: load_og(0), 1: load_og(1)}
            xg_tok = {0: load_xg(0), 1: load_xg(1)}
            ev_tok = {}
            pe_done = {}
            sq_w = {}
            x1_free = {}
            xn_free = {}
            ptst = {"free": None}

            def emit_ln(tb):
                par = tb % 2
                c0 = par * 8
                lxg = xg_tok[tb]
                t_mu = self.op("vector", "tensor_mul", sq[par][:, :],
                               s0t[par][:, :], xg[par][:, D:2 * D],
                               wait=(ev_tok[(tb, 1)], lxg))
                t_x1 = self.op("vector", "tensor_add", x1t[par][:, :],
                               xg[par][:, 0:D], sq[par][:, :],
                               wait=x1_free.get(par, ()))
                xdma1 = self.dma("sync",
                                 self.x1s[tb * 128:(tb + 1) * 128, :],
                                 x1t[par][:, :], s_x1, wait=(t_x1,))
                self.op("vector", "reduce_sum", st[:, c0:c0 + 1],
                        x1t[par][:, :], axis=AX)
                self.op("vector", "tensor_scalar_mul",
                        st[:, c0 + 2:c0 + 3], st[:, c0:c0 + 1],
                        1.0 / D)
                self.op("vector", "tensor_scalar_sub", sq[par][:, :],
                        x1t[par][:, :], st[:, c0 + 2:c0 + 3])
                self.op("vector", "tensor_mul", sq[par][:, :],
                        sq[par][:, :], sq[par][:, :])
                t_ss = self.op("vector", "reduce_sum", st[:, c0 + 3:c0 + 4],
                               sq[par][:, :], axis=AX,
                               wait=(sq_w.get(par),))
                t_rs = self.op("scalar", "activation", st[:, c0 + 4:c0 + 5],
                               st[:, c0 + 3:c0 + 4], AF.Sqrt,
                               bias=self.c_eps[:, 0:1],
                               scale=1.0 / D, wait=(t_ss,))
                sq_w[par] = t_rs
                self.op("vector", "reciprocal", st[:, c0 + 5:c0 + 6],
                        st[:, c0 + 4:c0 + 5], wait=(t_rs,))
                t_xn = self.op("vector", "tensor_scalar", xn_tm[par][:, :],
                               x1t[par][:, :], st[:, c0 + 2:c0 + 3],
                               st[:, c0 + 5:c0 + 6],
                               ALU.subtract, ALU.mult,
                               wait=xn_free.get(par, ()),
                               mark=True)
                x1_free[par] = (xdma1,)
                xdma2 = self.dma("sync",
                                 self.xn1tm[tb * 128:(tb + 1) * 128, :],
                                 xn_tm[par][:, :], s_x1, wait=(t_xn,))
                for half in range(2):
                    toks = []
                    for j in range(4):
                        c = half * 4 + j
                        pw = ()
                        if j == 0:
                            pw = ((t_xn,) if half == 0 else ()) + \
                                 ((ptst["free"],) if ptst["free"] else ())
                        tt = self.op(
                            "tensor", "transpose",
                            pt[:, j * 128:(j + 1) * 128],
                            xn_tm[par][:, c * 128:(c + 1) * 128],
                            self.c_identB[:, :],
                            wait=pw, mark=(j == 3))
                        if tt:
                            toks.append(tt)
                    ptst["free"] = self.op(
                        "vector", "tensor_copy",
                        self.xnv[:, half * 4:(half + 1) * 4,
                                 tb * 128:(tb + 1) * 128],
                        pt[:, :].rearrange("p (j t) -> p j t", j=4),
                        wait=tuple(toks), mark=True)
                xn_free[par] = (tt, xdma2)
                if tb + 2 < NTB:
                    xg_tok[tb + 2] = load_xg(tb + 2, wait=(t_x1,))

            pend_ln = None
            for tb in range(NTB):
                par = tb % 2
                for fc in range(2):
                    bank = pb[par * 2 + fc]
                    for c in range(16):
                        pw = []
                        if c == 0:
                            if fc == 0:
                                pw.append(og_tok[tb])
                                if tb == 0:
                                    pw.append(ldw)
                            prev = ev_tok.get((tb - 2, fc))
                            if prev:
                                pw.append(prev)
                        self.op("tensor", "matmul", bank[:, :],
                                ogb[tb % 2][:, :].rearrange(
                                    "p (c t) -> p c t", c=16)[:, c, :],
                                owv[:, c, fc * 512:(fc + 1) * 512],
                                start=(c == 0), stop=False, wait=tuple(pw))
                    t_b = self.op("tensor", "matmul", bank[:, :],
                                  self.c_onesB[0:1, :],
                                  self.c_ob0[0:1, fc * 512:(fc + 1) * 512],
                                  start=False, stop=True,
                                  mark=True)
                    ev_tok[(tb, fc)] = self.op(
                        "vector", "tensor_copy",
                        s0t[par][:, fc * 512:(fc + 1) * 512], bank[:, :],
                        wait=(t_b,))
                    if fc == 1:
                        pe_done[tb] = t_b
                if tb + 2 < NTB:
                    og_tok[tb + 2] = load_og(tb + 2, wait=(pe_done[tb],))
                if pend_ln is not None:
                    emit_ln(pend_ln)
                pend_ln = tb
            emit_ln(NTB - 1)

    def p_qkv1(self):
        nc = self.nc
        NT, T, NTB = self.NT, self.T, self.NTB
        NC = NT // 512
        with self.phase() as ctx:
            al = ctx.enter_context
            wq = al(nc.sbuf_tensor("q1_wq", [128, 8 * 3 * D], BF16))
            wqv = wq[:, :].rearrange("p (c f) -> p c f", c=8)
            wraw = al(nc.sbuf_tensor("q1_wraw", [128, 4 * D], BF16))
            tabc = al(nc.sbuf_tensor("q1_tab", [128, 2 * T], F32))
            qtmp = al(nc.sbuf_tensor("q1_tmp", [128, NT], BF16))
            t1 = al(nc.sbuf_tensor("q1_t1", [128, 512], F32))
            t2 = al(nc.sbuf_tensor("q1_t2", [128, 512], F32))
            qout = al(nc.sbuf_tensor("q1_out", [128, NT], BF16))
            vout = [al(nc.sbuf_tensor(f"q1_v{i}", [128, D], BF16))
                    for i in range(2)]
            ktm1 = al(nc.sbuf_tensor("q1_ktm", [128, NT], BF16))
            psT = al(nc.psum_tensor("q1_psT", [128, 512], BF16))
            pq = [al(nc.psum_tensor(f"q1_pq{i}", [128, 512], F32))
                  for i in range(2)]
            pr = [al(nc.psum_tensor(f"q1_pr{i}", [128, 512], F32))
                  for i in range(2)]
            s_w = self.psem(0, swdge=True)
            s_tab = self.psem(1)
            s_out = self.psem(2)
            s_vout = [self.psem(4), self.psem(5)]
            s_tm1 = self.psem(6)
            self.dma("sync", tabc[:, 0:T], self.inp["cos1"], s_tab)
            ltab = self.dma("sync", tabc[:, T:2 * T], self.inp["sin1"], s_tab)
            wr = self.inp["L1_attn_W_qkv"].rearrange("(r p) d -> p r d", p=128)
            tp_free = None
            for grp6 in range(6):
                ldw = self.dma("gpsimd",
                               wraw[:, :].rearrange("p (r d) -> p r d", r=4),
                               wr[:, grp6 * 4:(grp6 + 1) * 4, :], s_w,
                               wait=(tp_free,) if grp6 else ())
                for c in range(8):
                    for j in range(4):
                        r = j
                        pw = ()
                        if j == 0:
                            pw = (ldw,) if (c == 0) else ()
                            if tp_free is not None:
                                pw = pw + (tp_free,)
                        tt = self.op(
                            "tensor", "transpose",
                            psT[:, j * 128:(j + 1) * 128],
                            wraw[:, r * D + c * 128:
                                 r * D + (c + 1) * 128],
                            self.c_identB[:, :], wait=pw,
                            mark=(j == 3))
                    tp_free = self.op(
                        "vector", "tensor_copy",
                        wqv[:, c, (grp6 * 4) * 128:
                            (grp6 * 4 + 4) * 128],
                        psT[:, :], wait=(tt,))
            t_wq = tp_free
            evac_toks = {}
            rot_toks = {}
            add_toks = {}
            out_tok = None
            tm1_tr = None
            tm1_ev = None
            last_add = None

            def flush1(fbgc, nc_, t_evc, tcolc):
                nonlocal last_add
                rbank = pr[nc_ % 2]
                rw = [t_evc]
                prev_t2 = add_toks.get((fbgc, nc_ - 2) if nc_ >= 2
                                       else (fbgc - 1, NC - 2 + nc_))
                if prev_t2:
                    rw.append(prev_t2)
                t_rot = self.op("tensor", "matmul", rbank[:, :],
                                self.c_R1T[:, :],
                                qtmp[:, nc_ * 512:(nc_ + 1) * 512],
                                start=True, stop=True,
                                wait=tuple(rw), mark=True)
                rot_toks[(fbgc, nc_)] = t_rot
                w1 = [t_evc, ltab if fbgc == 0 and nc_ == 0 else None]
                if out_tok is not None and nc_ == 0:
                    w1.append(out_tok)
                    if tm1_tr is not None:
                        w1.append(tm1_tr)
                self.op("vector", "tensor_mul", t1[:, :],
                        qtmp[:, nc_ * 512:(nc_ + 1) * 512],
                        tabc[:, tcolc:tcolc + 512],
                        wait=tuple(t for t in w1 if t))
                self.op("vector", "tensor_mul", t2[:, :], rbank[:, :],
                        tabc[:, T + tcolc:T + tcolc + 512],
                        wait=(t_rot,))
                last_add = self.op("vector", "tensor_add",
                                   qout[:, nc_ * 512:(nc_ + 1) * 512],
                                   t1[:, :], t2[:, :], mark=True)
                add_toks[(fbgc, nc_)] = last_add

            for part, out_ap in ((0, self.qr1), (1, self.kr1)):
                for fb in range(8):
                    fbg = part * 8 + fb
                    pend = None
                    for n in range(NC):
                        sseq = (n * 512) // T
                        tcol = n * 512 - sseq * T
                        bank = pq[n % 2]
                        for c in range(8):
                            pw = []
                            if c == 0:
                                if fbg == 0 and n < 2:
                                    pw.append(t_wq)
                                prev = evac_toks.get((fbg, n - 2) if n >= 2
                                                     else (fbg - 1, NC - 2 + n))
                                if prev:
                                    pw.append(prev)
                            self.op("tensor", "matmul", bank[:, :],
                                    wqv[:, c, fbg * 128:(fbg + 1) * 128],
                                    self.xnv[:, c, n * 512:(n + 1) * 512],
                                    start=(c == 0), stop=(c == 7),
                                    wait=tuple(pw), mark=(c == 7))
                        t_mm = (self.ch["tensor"], self.ch["tensor"].n)
                        ev_w = [t_mm]
                        prev_rot = rot_toks.get((fbg - 1, n))
                        if prev_rot:
                            ev_w.append(prev_rot)
                        prev_t1 = add_toks.get((fbg - 1, n))
                        if prev_t1:
                            ev_w.append(prev_t1)
                        t_ev = self.op("scalar", "add",
                                       qtmp[:, n * 512:(n + 1) * 512],
                                       bank[:, :],
                                       self.c_qkb1[:, fbg:fbg + 1],
                                       wait=tuple(ev_w), mark=True)
                        evac_toks[(fbg, n)] = t_ev
                        if pend is not None:
                            flush1(*pend)
                        pend = (fbg, n, t_ev, tcol)
                    flush1(*pend)
                    out_tok = self.dma("sync",
                                       out_ap[fb * 128:(fb + 1) * 128, :],
                                       qout[:, :], s_out, wait=(last_add,))
                    if part == 1:
                        for grp in range(NTB // 4):
                            for j in range(4):
                                tbk = grp * 4 + j
                                pw = ()
                                if j == 0:
                                    pw = ((last_add,) if grp == 0 else ()) + \
                                         ((tm1_ev,) if tm1_tr
                                          else (t_wq,))
                                tm1_tr = self.op(
                                    "tensor", "transpose",
                                    psT[:, j * 128:(j + 1) * 128],
                                    qout[:, tbk * 128:(tbk + 1) * 128],
                                    self.c_identB[:, :], wait=pw,
                                    mark=(j == 3))
                            tm1_ev = self.op(
                                "scalar", "copy",
                                ktm1[:, grp * 512:(grp + 1) * 512],
                                psT[:, :],
                                wait=(tm1_tr,) +
                                     (((s_tm1, s_tm1.n),) if grp == 0
                                      else ()))
                        self.dma("sync",
                                 self.kr1tm.rearrange("(i p) f -> p i f",
                                                      p=128)
                                 [:, :, fb * 128:(fb + 1) * 128],
                                 ktm1[:, :].rearrange("p (i f) -> p i f",
                                                      f=128),
                                 s_tm1, wait=(tm1_ev,))
            ev_tok = {}
            vo_dma = {}
            k = 0
            for tb in range(NTB):
                obuf = vout[tb % 2]
                for fc in range(2):
                    bank = pq[k % 2]
                    for c in range(8):
                        pw = []
                        if c == 0:
                            prev = ev_tok.get(k - 2)
                            if prev:
                                pw.append(prev)
                            elif k < 2:
                                pw.append(evac_toks[(15, NC - 2 + k)])
                        self.op("tensor", "matmul", bank[:, :],
                                self.xnv[:, c, tb * 128:(tb + 1) * 128],
                                wqv[:, c, 2 * D + fc * 512:
                                    2 * D + (fc + 1) * 512],
                                start=(c == 0), stop=False, wait=tuple(pw))
                    self.op("tensor", "matmul", bank[:, :],
                            self.c_onesB[0:1, :],
                            self.c_vb1[0:1, fc * 512:(fc + 1) * 512],
                            start=False, stop=True, mark=True)
                    t_mm = (self.ch["tensor"], self.ch["tensor"].n)
                    ew = [t_mm]
                    if fc == 0 and tb >= 2:
                        ew.append((s_vout[tb % 2], s_vout[tb % 2].n))
                    ev_tok[k] = self.op("vector", "tensor_copy",
                                        obuf[:, fc * 512:(fc + 1) * 512],
                                        bank[:, :], wait=tuple(ew))
                    k += 1
                vo_dma[tb] = self.dma("sync",
                                      self.v1s[tb * 128:(tb + 1) * 128, :],
                                      obuf[:, :], s_vout[tb % 2],
                                      wait=(ev_tok[k - 1],))

    def p_attn1(self):
        nc = self.nc
        B, T, TB = self.B, self.T, self.TB
        with self.phase() as ctx:
            al = ctx.enter_context
            qk = [al(nc.sbuf_tensor(f"a1_qk{i}", [128, 2 * T], BF16))
                  for i in range(2)]
            vx = [al(nc.sbuf_tensor(f"a1_vx{i}", [128, TB * 128], BF16))
                  for i in range(2)]
            kt1 = [al(nc.sbuf_tensor(f"a1_kt{i}", [128, TB * 128], BF16))
                   for i in range(2)]
            sh1 = [al(nc.sbuf_tensor(f"a1_sh{i}", [128, TB * 64], F32))
                   for i in range(2)]
            S_sb = [al(nc.sbuf_tensor(f"a1_S{i}", [128, 64], BF16))
                    for i in range(4)]
            dTs = [al(nc.sbuf_tensor(f"a1_dT{i}", [128, 128], BF16))
                   for i in range(2)]
            dNs = [al(nc.sbuf_tensor(f"a1_dN{i}", [128, 128], BF16))
                   for i in range(2)]
            psS = [al(nc.psum_tensor(f"a1_psS{i}", [64, 64], F32))
                   for i in range(2)]
            psD = [al(nc.psum_tensor(f"a1_psD{i}", [128, 256], F32))
                   for i in range(2)]
            psO = al(nc.psum_tensor("a1_psO", [128, 256], F32))
            s_qk = [self.psem(0), self.psem(1)]
            s_vx = [self.psem(2), self.psem(3)]
            s_sh = [self.psem(4), self.psem(5)]
            v1r = self.v1s.rearrange("(b i p) c -> b p i c", i=TB, p=128)
            kt1r = self.kr1tm.rearrange("(b i p) c -> b p i c", i=TB, p=128)
            s1r = self.s1.rearrange("(b i p) c -> b p i c", i=TB, p=128)
            tkD = {}
            tkO = {}
            last_snap = {}
            t_av = {}
            pe_group = {}
            pe_pair = {}
            kp = 0
            for s in range(B):
                for hp in range(8):
                    pq_ = kp % 2
                    self.dma("sync", qk[pq_][:, 0:T],
                             self.qr1[hp * 128:(hp + 1) * 128,
                                      s * T:(s + 1) * T], s_qk[pq_],
                             wait=(pe_group.get(pq_),))
                    l_k = self.dma("sync", qk[pq_][:, T:2 * T],
                                   self.kr1[hp * 128:(hp + 1) * 128,
                                            s * T:(s + 1) * T], s_qk[pq_])
                    l_v = self.dma("sync",
                                   vx[pq_][:, :].rearrange(
                                       "p (i c) -> p i c", c=128),
                                   v1r[s][:, :, 128 * hp:128 * hp + 128],
                                   s_vx[pq_], wait=(pe_pair.get(pq_),))
                    l_kt = self.dma("sync",
                                    kt1[pq_][:, :].rearrange(
                                        "p (i c) -> p i c", c=128),
                                    kt1r[s][:, :, 128 * hp:128 * hp + 128],
                                    s_vx[pq_])
                    for hh in range(2):
                        h = 2 * hp + hh
                        rb = 64 * hh
                        qs = qk[pq_][rb:rb + 64, 0:T]
                        ks = qk[pq_][rb:rb + 64, T:2 * T]
                        ktv = kt1[pq_][:, :].rearrange(
                            "p (i c) -> p i c", c=128)
                        ev4 = None
                        snap = {-2: last_snap.get(0),
                                -1: last_snap.get(1)}
                        buf = lambda j: 2 * (j % 2) + ((j // 2) % 2)
                        vxv1 = vx[pq_][:, :].rearrange(
                            "p (i c) -> p i c", c=128)
                        dT_tok = {}

                        def emit_scores1(j, first):
                            pj = j % 2
                            jb = slice(j * 128, (j + 1) * 128)
                            jp = slice((j - 1) * 128, j * 128)
                            t3 = self.op("tensor", "matmul",
                                         psD[pj][:, 0:128],
                                         ks[:, jb], qs[:, jb],
                                         start=True, stop=True,
                                         wait=(tkD.get(pj),
                                               l_k if first else None),
                                         mark=(j == 0))
                            if j > 0:
                                t3 = self.op("tensor", "matmul",
                                             psD[pj][:, 128:256],
                                             ks[:, jp], qs[:, jb],
                                             start=True, stop=True,
                                             mark=True)
                            t4 = self.op("vector", "tensor_mul",
                                         dTs[pj][:, :], psD[pj][:, 0:128],
                                         self.c_triu[:, :],
                                         wait=(t3, t_av.get(pj)))
                            t4n = None
                            if j > 0:
                                t4n = self.op("vector", "tensor_copy",
                                              dNs[pj][:, :],
                                              psD[pj][:, 128:256],
                                              wait=(t3,))
                            tkD[pj] = (t4, t4n) if j > 0 else (t4,)
                            dT_tok[j] = (t4, t4n)

                        emit_scores1(0, hh == 0)
                        for i in range(TB):
                            ib = slice(i * 128, (i + 1) * 128)
                            pi = i % 2
                            vxi = vxv1[:, i, rb:rb + 64]
                            oseg = psO[:, (i % 4) * 64:(i % 4) * 64 + 64]
                            if i + 1 < TB:
                                emit_scores1(i + 1, False)
                            t6 = self.op("tensor", "matmul", psS[pi][:, :],
                                         ktv[:, i, rb:rb + 64], vxi,
                                         start=(i < 2), stop=(i >= TB - 2),
                                         skip_group_check=True,
                                         wait=(snap.get(i - 2),
                                               l_kt if (i == 0 and hh == 0)
                                               else None),
                                         mark=True)
                            t4, t4n = dT_tok[i]
                            t5 = self.op("tensor", "matmul", oseg,
                                         dTs[pi][:, :], vxi,
                                         start=True, stop=(i == 0),
                                         wait=(t4,
                                               tkO.get(0)
                                               if i % 4 == 0 else None),
                                         mark=(i == 0))
                            if i > 0:
                                t5 = self.op("tensor", "matmul", oseg,
                                             dNs[pi][:, :],
                                             vxv1[:, i - 1, rb:rb + 64],
                                             start=False, stop=(i == 1),
                                             wait=(t4n,), mark=(i == 1))
                            for j in (i - 2, i - 3):
                                if j >= 0:
                                    t5 = self.op(
                                        "tensor", "matmul", oseg,
                                        qs[:, ib],
                                        S_sb[buf(j)][rb:rb + 64, :],
                                        start=False, stop=(j == min(
                                            x for x in (i - 2, i - 3)
                                            if x >= 0)),
                                        wait=(snap[j],), mark=True)
                            snap[i] = self.op("scalar", "copy",
                                              S_sb[buf(i)][rb:rb + 64, :],
                                              psS[pi][:, :], wait=(t6, t5))
                            t_av[pi] = t5
                            if i % 4 == 3:
                                ev4 = self.op(
                                    "vector", "tensor_copy",
                                    sh1[hh][:, (i - 3) * 64:(i + 1) * 64],
                                    psO[:, :],
                                    wait=(t5,) +
                                         (((s_sh[hh], s_sh[hh].n),)
                                          if i == 3 else ()))
                                tkO[0] = ev4
                        last_snap = {0: snap[TB - 2], 1: snap[TB - 1]}
                        self.dma("sync",
                                 s1r[s][:, :, 64 * h:64 * h + 64],
                                 sh1[hh][:, :].rearrange(
                                     "p (i c) -> p i c", c=64),
                                 s_sh[hh], wait=(ev4,))
                    pe_pair[pq_] = t5
                    pe_group[pq_] = t5
                    kp += 1

    def p_router(self):
        nc = self.nc
        NT, NTB = self.NT, self.NTB
        NC = NT // 512
        with self.phase() as ctx:
            al = ctx.enter_context
            wr_sb = al(nc.sbuf_tensor("rt_wr", [128, 8 * E], BF16))
            logb = al(nc.sbuf_tensor("rt_log", [8, NT], F32))
            lt = al(nc.sbuf_tensor("rt_lt", [128, E], F32))
            ct = al(nc.sbuf_tensor("rt_ct", [128, E], F32))
            ovt = al(nc.sbuf_tensor("rt_ovt", [128, E], F32))
            exb = al(nc.sbuf_tensor("rt_ex", [128, E], F32))
            stt = al(nc.sbuf_tensor("rt_st", [128, 4], F32))
            offs = al(nc.sbuf_tensor("rt_offs", [1, E], F32))
            xtile = [al(nc.sbuf_tensor(f"rt_xt{i}", [128, D], BF16))
                     for i in range(2)]
            psL = al(nc.psum_tensor("rt_psL", [8, 512], F32))
            psTr = al(nc.psum_tensor("rt_psTr", [128, E], F32))
            psC = al(nc.psum_tensor("rt_psC", [128, 2 * E], F32))
            s_w = self.psem(0, swdge=True)
            s_xt = [self.psem(1), self.psem(2)]
            s_sc = self.psem(3, swdge=True)
            ldw = self.dma("gpsimd",
                           wr_sb[:, :].rearrange("p (c e) -> p c e", c=8),
                           self.inp["L1_ffn_Wr"].rearrange(
                               "(c p) e -> p c e", p=128), s_w)
            t_off = self.op("vector", "memset", offs[:, :], 0.0)
            ev_prev = None
            for n in range(NC):
                for c in range(8):
                    self.op("tensor", "matmul", psL[:, :],
                            wr_sb[:, :].rearrange("p (c e) -> p c e",
                                                  c=8)[:, c, :],
                            self.xnv[:, c, n * 512:(n + 1) * 512],
                            start=(c == 0), stop=(c == 7),
                            wait=((ldw,) if (n == 0 and c == 0) else ()) +
                                 ((ev_prev,) if (c == 0 and ev_prev)
                                  else ()),
                            mark=(c == 7))
                t_mm = (self.ch["tensor"], self.ch["tensor"].n)
                ev_prev = self.op("vector", "tensor_scalar_add",
                                  logb[:, n * 512:(n + 1) * 512], psL[:, :],
                                  self.c_br[:, 0:1], wait=(t_mm,))
            CAP = float(self.CAP)
            tr_free = None
            tc_free = None
            sc_prev = None
            for tb in range(NTB):
                par = tb % 2
                lx = self.dma("sync", xtile[par][:, :],
                              self.xn1tm[tb * 128:(tb + 1) * 128, :],
                              s_xt[par], wait=(sc_prev,))
                t_tr = self.op("tensor", "transpose", psTr[:, :],
                               logb[:, tb * 128:(tb + 1) * 128],
                               self.c_identF[0:8, 0:8],
                               wait=(ev_prev if tb == 0 else None,
                                     tr_free), mark=True)
                tr_free = self.op("vector", "tensor_copy", lt[:, :],
                                  psTr[:, :], wait=(t_tr,))
                self.op("vector", "reduce_max", stt[:, 0:1], lt[:, :],
                        axis=AX)
                t_nm = self.op("vector", "tensor_scalar_mul", stt[:, 1:2],
                               stt[:, 0:1], -1.0)
                t_ex = self.op("scalar", "activation", exb[:, :], lt[:, :],
                               AF.Exp, bias=stt[:, 1:2], scale=1.0,
                               wait=(t_nm,))
                t_se = self.op("vector", "reduce_sum", stt[:, 2:3],
                               exb[:, :], axis=AX, wait=(t_ex,))
                self.op("vector", "reciprocal", self.c_mp[:, tb:tb + 1],
                        stt[:, 2:3])
                t_ind = self.op("vector", "tensor_single_scalar",
                                self.c_ind[:, tb * E:(tb + 1) * E], lt[:, :],
                                stt[:, 0:1], ALU.is_ge)
                indsl = self.c_ind[:, tb * E:(tb + 1) * E]
                self.op("tensor", "matmul", psC[:, 0:E],
                        self.c_triu[:, :], indsl,
                        start=True, stop=False, wait=(t_ind, tc_free))
                self.op("tensor", "matmul", psC[0:1, E:2 * E],
                        self.c_onesF[:, 0:1], indsl,
                        start=True, stop=True, wait=(t_off,))
                t_c2 = self.op("tensor", "matmul", psC[:, 0:E],
                               self.c_onesFr[0:1, :], offs[0:1, :],
                               start=False, stop=True, mark=True)
                tc_free = self.op("vector", "tensor_copy", ct[:, :],
                                  psC[:, 0:E], wait=(t_c2,))
                t_off = self.op("vector", "tensor_add", offs[0:1, :],
                                offs[0:1, :], psC[0:1, E:2 * E])
                self.op("vector", "tensor_single_scalar", ovt[:, :],
                        ct[:, :], CAP, ALU.is_gt)
                self.op("vector", "tensor_mul", ovt[:, :], ovt[:, :], indsl)
                self.op("vector", "reduce_sum", stt[:, 2:3], ovt[:, :],
                        axis=AX)
                self.op("vector", "tensor_scalar_mul", stt[:, 3:4],
                        stt[:, 2:3], 1.0e8)
                self.op("vector", "tensor_add", ct[:, :], ct[:, :],
                        self.c_eCm1[:, :])
                self.op("vector", "tensor_mul", ct[:, :], ct[:, :], indsl)
                self.op("vector", "reduce_sum", stt[:, 1:2], ct[:, :],
                        axis=AX)
                self.op("vector", "tensor_add", stt[:, 1:2], stt[:, 1:2],
                        stt[:, 3:4])
                t_sl = self.op("vector", "tensor_copy",
                               self.c_slots[:, tb:tb + 1], stt[:, 1:2])
                sc_prev = self.dma("gpsimd", self.xnsort,
                                   xtile[par][:, :], s_sc,
                                   wait=(lx, t_sl),
                                   indirect_out=self.c_slots[:, tb:tb + 1])

    def p_moe(self):
        nc = self.nc
        CAP = self.CAP
        NCH = CAP // 512
        with self.phase() as ctx:
            al = ctx.enter_context
            w1e = al(nc.sbuf_tensor("mo_w1", [128, 8 * HMOE], BF16))
            w1v = w1e[:, :].rearrange("p (c f) -> p c f", c=8)
            w2e = al(nc.sbuf_tensor("mo_w2", [128, 16 * D], BF16))
            w2v = w2e[:, :].rearrange("p (c f) -> p c f", c=16)
            xs = [al(nc.sbuf_tensor(f"mo_xs{i}", [128, 8 * 512], BF16))
                  for i in range(2)]
            h1 = al(nc.sbuf_tensor("mo_h", [128, 16 * 512], BF16))
            hv = h1[:, :].rearrange("p (f t) -> p f t", f=16)
            yev = [al(nc.sbuf_tensor(f"mo_ev{i}", [128, 512], F32))
                   for i in range(4)]
            pp = [al(nc.psum_tensor(f"mo_pp{i}", [128, 512], F32))
                  for i in range(3)]
            s_w1 = self.psem(0, swdge=True)
            s_w2 = self.psem(1, swdge=True)
            s_xs = [self.psem(2), self.psem(3)]
            s_y = [self.psem(4 + i) for i in range(4)]
            W1, W2 = self.inp["L1_ffn_W1"], self.inp["L1_ffn_W2"]

            def load_w1(e, wait=()):
                return self.dma("gpsimd", w1v,
                                W1[e].rearrange("(c p) f -> p c f", p=128),
                                s_w1, wait=wait)

            def load_w2(e, wait=()):
                return self.dma("gpsimd", w2v,
                                W2[e].rearrange("(c p) f -> p c f", p=128),
                                s_w2, wait=wait)

            def load_xs(g, wait=()):
                e, tcs = divmod(g, NCH)
                par = g % 2
                r0 = e * CAP + tcs * 512
                tok = None
                for c in range(8):
                    tok = self.dma("sync",
                                   xs[par][:, c * 512:(c + 1) * 512],
                                   self.xnsort[r0:r0 + 512,
                                               c * 128:(c + 1) * 128],
                                   s_xs[par],
                                   wait=wait if c == 0 else (),
                                   transpose=True)
                return tok

            w1tok = load_w1(0)
            w2tok = load_w2(0)
            xstok = {0: load_xs(0), 1: load_xs(1)}
            evac_tok = {}
            kq = 0
            ky = 0
            y_prev = None
            for e in range(E):
                for tcs in range(NCH):
                    g = e * NCH + tcs
                    par = g % 2
                    xsv = xs[par][:, :].rearrange("p (c t) -> p c t", c=8)
                    h_last = None
                    for fb in range(16):
                        bank = pp[kq % 3]
                        for c in range(8):
                            pw = []
                            if c == 0:
                                if fb == 0:
                                    pw.append(xstok[g])
                                    if tcs == 0:
                                        pw.append(w1tok)
                                        if e > 0:
                                            pw.append(y_prev)
                                prev = evac_tok.get(kq % 3)
                                if prev:
                                    pw.append(prev)
                            self.op("tensor", "matmul", bank[:, :],
                                    w1v[:, c, fb * 128:(fb + 1) * 128],
                                    xsv[:, c, :],
                                    start=(c == 0), stop=(c == 7),
                                    wait=tuple(pw), mark=(c == 7))
                        h_last = (self.ch["tensor"], self.ch["tensor"].n)
                        evac_tok[kq % 3] = self.op(
                            "scalar", "activation", hv[:, fb, :], bank[:, :],
                            AF.Relu,
                            bias=self.c_b1moe[:, e * 16 + fb:e * 16 + fb + 1],
                            scale=1.0, wait=(h_last,))
                        kq += 1
                    relu_last = evac_tok[(kq - 1) % 3]
                    if g + 2 < E * NCH:
                        xstok[g + 2] = load_xs(g + 2, wait=(h_last,))
                    if tcs == NCH - 1 and e + 1 < E:
                        w1tok = load_w1(e + 1, wait=(h_last,))
                    for tt in range(4):
                        for dc in range(2):
                            bank = pp[kq % 3]
                            for j in range(16):
                                pw = []
                                if j == 0:
                                    if tt == 0 and dc == 0:
                                        if tcs == 0:
                                            pw.append(w2tok)
                                        pw.append(relu_last)
                                    prev = evac_tok.get(kq % 3)
                                    if prev:
                                        pw.append(prev)
                                y_last = self.op(
                                    "tensor", "matmul", bank[:, :],
                                    hv[:, j, tt * 128:(tt + 1) * 128],
                                    w2v[:, j, dc * 512:(dc + 1) * 512],
                                    start=(j == 0), stop=(j == 15),
                                    wait=tuple(pw), mark=(j == 15))
                            bb = ky % 4
                            t_ev = self.op(
                                "vector", "tensor_copy", yev[bb][:, :],
                                bank[:, :],
                                wait=(y_last, (s_y[bb], s_y[bb].n)))
                            evac_tok[kq % 3] = t_ev
                            kq += 1
                            ky += 1
                            r0 = e * CAP + tcs * 512 + tt * 128
                            self.dma("sync",
                                     self.ysort[r0:r0 + 128,
                                                dc * 512:(dc + 1) * 512],
                                     yev[bb][:, :], s_y[bb], wait=(t_ev,))
                if e + 1 < E:
                    w2tok = load_w2(e + 1, wait=(y_last,))
                y_prev = y_last

    def p_resid1(self):
        nc = self.nc
        NTB = self.NTB
        GRP = 2
        with self.phase() as ctx:
            al = ctx.enter_context
            xb = [al(nc.sbuf_tensor(f"r1_x{i}", [128, GRP * D], F32))
                  for i in range(2)]
            sb = [al(nc.sbuf_tensor(f"r1_s{i}", [128, GRP * D], F32))
                  for i in range(2)]
            mb = [al(nc.sbuf_tensor(f"r1_m{i}", [128, D], F32))
                  for i in range(4)]
            pre = [al(nc.sbuf_tensor(f"r1_pre{i}", [128, D], F32))
                   for i in range(2)]
            indb = [al(nc.sbuf_tensor(f"r1_indb{i}", [128, E], BF16))
                    for i in range(2)]
            indTall = al(nc.sbuf_tensor("r1_indT", [E, self.NT], BF16))
            psTi = al(nc.psum_tensor("r1_psTi", [E, 128], BF16))
            gat = [al(nc.sbuf_tensor(f"r1_gat{i}", [128, D], F32))
                   for i in range(2)]
            outt = [al(nc.sbuf_tensor(f"r1_out{i}", [128, D], F32))
                    for i in range(2)]
            pb2 = [al(nc.psum_tensor(f"r1_pb{i}", [128, 512], F32))
                   for i in range(2)]
            s_x = [self.psem(0), self.psem(1)]
            s_s = [self.psem(2), self.psem(3)]
            s_m = [self.psem(4 + i, swdge=True) for i in range(4)]
            s_y = self.psem(8)
            x_r = self.x1s.rearrange("(g a p) d -> g p a d", a=GRP, p=128)
            s_r = self.s1.rearrange("(g a p) d -> g p a d", a=GRP, p=128)
            free = [(), ()]
            t_sil = {}
            t_iT = None
            t_icp = None
            t_pre = {}
            t_mb_read = {}
            t_m_prev = {}
            out_dma = {}
            ind_tok = {}
            ldm_tok = {}
            ldxs = {}
            t_out_tok = {}

            def prefetch(tb):
                q = tb % 4
                t_ms = self.op("vector", "memset", mb[q][:, :], 0.0,
                               wait=(t_mb_read.get(q),))
                ldm_tok[tb] = self.dma("gpsimd", mb[q][:, :],
                                       self.ysort, s_m[q],
                                       wait=(t_ms,),
                                       indirect_in=self.c_slots[:, tb:tb + 1])

            for tb in range(NTB):
                ipar = tb % 2
                t_ib = self.op("vector", "tensor_copy", indb[ipar][:, :],
                               self.c_ind[:, tb * E:(tb + 1) * E],
                               wait=(ind_tok.get(tb - 2),))
                t_tr = self.op("tensor", "transpose", psTi[:, :],
                               indb[ipar][:, :], self.c_identB[:, :],
                               wait=(t_ib, t_icp), mark=True)
                t_icp = self.op("vector", "tensor_copy",
                                indTall[:, tb * 128:(tb + 1) * 128],
                                psTi[:, :], wait=(t_tr,))
                ind_tok[tb] = t_icp
                if tb < 4:
                    prefetch(tb)

            def emit_out(tb):
                par = (tb // GRP) % 2
                opar = tb % 2
                a = tb % GRP
                sl = slice(a * D, (a + 1) * D)
                t_m = self.op("vector", "tensor_mul", outt[opar][:, :],
                              sb[par][:, sl], gat[opar][:, :],
                              wait=(t_sil[tb],
                                    ldxs[tb // GRP][1] if a == 0 else None,
                                    out_dma.get(opar)))
                t_m_prev[opar] = t_m
                t_out = self.op("vector", "tensor_add", outt[opar][:, :],
                                xb[par][:, sl], outt[opar][:, :],
                                wait=(ldxs[tb // GRP][0],) if a == 0 else ())
                t_out_tok[tb] = t_out
                out_dma[opar] = self.dma(
                    "sync", self.y_out[tb * 128:(tb + 1) * 128, :],
                    outt[opar][:, :], s_y, wait=(t_out,))

            for g in range(NTB // GRP):
                par = g % 2
                ldx = self.dma("sync",
                               xb[par][:, :].rearrange("p (a d) -> p a d",
                                                       a=GRP),
                               x_r[g], s_x[par], wait=free[par])
                lds = self.dma("sync",
                               sb[par][:, :].rearrange("p (a d) -> p a d",
                                                       a=GRP),
                               s_r[g], s_s[par])
                ldxs[g] = (ldx, lds)
                for a in range(GRP):
                    tb = g * GRP + a
                    q = tb % 4
                    opar = tb % 2
                    for dc in range(2):
                        t_b2 = self.op(
                            "tensor", "matmul", pb2[dc][:, :],
                            indTall[:, tb * 128:(tb + 1) * 128],
                            self.c_b2moe[:, dc * 512:(dc + 1) * 512],
                            start=True, stop=True,
                            wait=(t_pre.get(dc), ind_tok[tb]), mark=True)
                        t_pre[dc] = self.op(
                            "vector", "tensor_add",
                            pre[opar][:, dc * 512:(dc + 1) * 512],
                            mb[q][:, dc * 512:(dc + 1) * 512],
                            pb2[dc][:, :],
                            wait=(t_b2, t_sil.get(tb - 2), ldm_tok[tb]))
                    t_mb_read[q] = t_pre[1]
                    t_sil[tb] = self.op("scalar", "activation",
                                        gat[opar][:, :],
                                        pre[opar][:, :], AF.Silu, bias=0.0,
                                        scale=self.c_mp[:, tb:tb + 1],
                                        wait=(t_pre[0], t_pre[1],
                                              t_m_prev.get(opar)))
                    if tb + 4 < NTB:
                        prefetch(tb + 4)
                    if tb > 0:
                        emit_out(tb - 1)
                if g >= 2:
                    free[par] = (t_out_tok[(g - 2) * GRP + 1],)
            emit_out(NTB - 1)

    def finalize(self):
        with self.phase():
            pass



_CACHED = {}


def get_nc(B=2, T=2048, debug=False, upto=99):
    key = (B, T, debug, upto)
    if key not in _CACHED:
        _CACHED[key] = MK(B, T, debug, upto).build()
    return _CACHED[key]


def kernel(**inputs):
    nc = get_nc()
    in_maps = [host_inputs(c, inputs) for c in range(NCORES)]
    res = run_bass_kernel_spmd(nc, in_maps, core_ids=list(range(NCORES)))
    out = np.concatenate([r["y"] for r in res.results], axis=0)
    return out.reshape(16, 2048, D)


# version 27
# speedup vs baseline: 1.2545x; 1.0000x over previous
import numpy as np
import ml_dtypes
from contextlib import ExitStack

import concourse.bass as bass
import concourse.mybir as mybir
from concourse.bass_utils import run_bass_kernel_spmd

F32 = mybir.dt.float32
BF16 = mybir.dt.bfloat16
AF = mybir.ActivationFunctionType
ALU = mybir.AluOpType
AX = mybir.AxisListType.X

NCORES = 8
D = 1024
H0, KD0, HD0 = 16, 64, 128
DV0 = 2048
DFF = 4096
H1, DH1 = 16, 64
E, HMOE = 8, 2048
EPS_LN = 1e-5

ENGINES = ("sync", "tensor", "vector", "scalar", "gpsimd")

RND = 12582912.0



def host_tables(T, B=2):
    NT = B * T
    CAP = NT // 2
    f64 = np.float64
    half = 1.0 / np.power(10000.0, np.linspace(0.0, 1.0, KD0 // 2))
    angle = np.repeat(half, 2)
    idx = np.arange(T, dtype=f64)
    cos0 = np.cos(idx[:, None] * angle[None, :])
    sin0 = np.sin(idx[:, None] * angle[None, :])
    decay = np.log(1.0 - 2.0 ** (-5.0 - np.arange(H0, dtype=f64)))
    ex = np.exp(idx[None, :] * decay[:, None])
    S = np.cumsum(ex, axis=1)
    emq = ex / np.sqrt(S)
    enk = np.exp(-idx[None, :] * decay[:, None]) * (KD0 ** -0.5)
    cosq = np.empty((D, T), f64); sinq = np.empty((D, T), f64)
    cosk = np.empty((D, T), f64); sink = np.empty((D, T), f64)
    for h in range(H0):
        r = slice(KD0 * h, KD0 * h + KD0)
        cosq[r] = cos0.T * emq[h][None, :]
        sinq[r] = sin0.T * emq[h][None, :]
        cosk[r] = cos0.T * enk[h][None, :]
        sink[r] = sin0.T * enk[h][None, :]
    inv_freq = 1.0 / 10000.0 ** (np.arange(0, DH1, 2, dtype=f64) / DH1)
    ang = idx[:, None] * inv_freq[None, :]
    cos1 = np.concatenate([np.cos(ang), np.cos(ang)], axis=1).T
    sin1 = np.concatenate([np.sin(ang), np.sin(ang)], axis=1).T
    cos1e = np.concatenate([cos1, cos1], axis=0)
    sin1e = np.concatenate([sin1, sin1], axis=0)
    R0T = np.zeros((128, 128), f64)
    for i in range(64):
        R0T[2 * i + 1, 2 * i] = -1.0
        R0T[2 * i, 2 * i + 1] = 1.0
    R1T = np.zeros((128, 128), f64)
    for b in (0, 64):
        for i in range(32):
            R1T[b + i + 32, b + i] = -1.0
            R1T[b + i, b + i + 32] = 1.0
    triu = np.triu(np.ones((128, 128), f64))
    bf = ml_dtypes.bfloat16
    return {
        "cosq": cosq.astype(np.float32), "sinq": sinq.astype(np.float32),
        "cosk": cosk.astype(np.float32), "sink": sink.astype(np.float32),
        "cos1": cos1e.astype(np.float32), "sin1": sin1e.astype(np.float32),
        "R0T": R0T.astype(bf), "R1T": R1T.astype(bf),
        "triu": triu.astype(np.float32),
        "identB": np.eye(128, dtype=bf),
        "identF": np.eye(128, dtype=np.float32),
        "onesB": np.ones((1, 128), dtype=bf),
        "eCm1": np.tile((np.arange(E) * CAP - 1.0).astype(np.float32),
                        (128, 1)),
        "iota_tb": (np.arange(128)[:, None] +
                    128 * np.arange(NT // 128)[None, :]).astype(np.int32),
    }


def host_inputs(core, full):
    T = 2048
    x = np.ascontiguousarray(
        full["L0_attn_x"].reshape(16 * T, D)[core * 2 * T:(core + 1) * 2 * T])
    tabs = host_tables(T)
    c = np.ascontiguousarray
    m = {
        "x": x,
        "L0_attn_q_w": full["L0_attn_q_w"], "L0_attn_k_w": full["L0_attn_k_w"],
        "L0_attn_v_w": full["L0_attn_v_w"], "L0_attn_g_w": full["L0_attn_g_w"],
        "L0_attn_o_w": full["L0_attn_o_w"],
        "L0_ffn_W1": full["L0_ffn_W1"], "L0_ffn_W2": full["L0_ffn_W2"],
        "L1_attn_W_qkv": full["L1_attn_W_qkv"],
        "L1_ffn_Wr": full["L1_ffn_Wr"], "L1_ffn_W1": full["L1_ffn_W1"],
        "L1_ffn_W2": full["L1_ffn_W2"],
        "qb0_pp": c(full["L0_attn_q_b"].reshape(8, 128).T),
        "kb0_pp": c(full["L0_attn_k_b"].reshape(8, 128).T),
        "b1ff_pp": c(full["L0_ffn_b1"].reshape(32, 128).T),
        "qkb1_pp": c(full["L1_attn_b_qkv"][:2048].reshape(16, 128).T),
        "b1moe_pp": c(np.stack([full["L1_ffn_b1"][e].reshape(16, 128).T
                                for e in range(E)], 1).reshape(128, E * 16)),
        "br_pp": c(full["L1_ffn_br"].reshape(E, 1)),
        "vb0_row": c(full["L0_attn_v_b"].reshape(1, DV0)),
        "gb0_row": c(full["L0_attn_g_b"].reshape(1, DV0)),
        "ob0_row": c(full["L0_attn_o_b"].reshape(1, D)),
        "b2ff_row": c(full["L0_ffn_b2"].reshape(1, D)),
        "vb1_row": c(full["L1_attn_b_qkv"][2048:].reshape(1, D)),
        "b2moe": full["L1_ffn_b2"],
    }
    m.update(tabs)
    return m



class Sem:
    __slots__ = ("h", "n")

    def __init__(self, h):
        self.h = h
        self.n = 0

    def tok(self):
        return (self, self.n)


class _PhaseCtx:
    def __init__(self, mk, name):
        self.mk = mk
        self.name = name
        self.stack = ExitStack()

    def __enter__(self):
        self.mk.streams = {e: [] for e in ENGINES}
        self.mk.op_seq = 0
        g = self.mk.current_group
        if g is not None:
            idx = g.next_idx
            g.next_idx += 1
            self.saved_ch = self.mk.ch
            self.mk.ch = {e: self.mk.sem(f"gch{idx}_{e}")
                          for e in ENGINES}
            return g.stack
        return self.stack.__enter__()

    def __exit__(self, *exc):
        g = self.mk.current_group
        if g is not None:
            if exc[0] is None:
                g.members.append(self.mk.streams)
            self.mk.ch = self.saved_ch
            self.mk.streams = None
            return False
        if exc[0] is None:
            self.mk.barrier()
            with self.mk.nc.named_scope(self.name):
                self.mk.replay()
        self.stack.__exit__(*exc)
        self.mk.streams = None
        return False


class _GroupCtx:
    def __init__(self, mk, name):
        self.mk = mk
        self.name = name
        self.stack = ExitStack()
        self.members = []
        self.next_idx = 0

    def __enter__(self):
        self.stack.__enter__()
        self.mk.current_group = self
        return self

    def __exit__(self, *exc):
        self.mk.current_group = None
        if exc[0] is None:
            totals = [max(op[0] for e in ENGINES for op in m[e])
                      for m in self.members]
            merged = {e: [] for e in ENGINES}
            for e in ENGINES:
                items = []
                for j, m in enumerate(self.members):
                    for seq, fn in m[e]:
                        items.append((seq / totals[j], j, seq, fn))
                items.sort(key=lambda t: (t[0], t[1], t[2]))
                merged[e] = [(t[2], t[3]) for t in items]
            self.mk.streams = merged
            self.mk.barrier()
            with self.mk.nc.named_scope(self.name):
                self.mk.replay()
            self.mk.streams = None
        self.stack.__exit__(*exc)
        return False


class MK:
    def __init__(self, B=2, T=2048, debug=False, upto=99):
        self.B, self.T = B, T
        self.NT = B * T
        self.NTB = self.NT // 128
        self.TB = T // 128
        self.debug = debug
        self.upto = upto
        self.nc = bass.Bass("TRN2", target_bir_lowering=False)
        self.ctx = ExitStack()
        self.sems = {}
        self.ch = {e: self.sem(f"ch_{e}") for e in ENGINES}
        self.streams = None
        self.current_group = None
        self.op_seq = 0

    def sem(self, name):
        if name not in self.sems:
            h = self.ctx.enter_context(self.nc.semaphore(name))
            self.sems[name] = Sem(h)
        return self.sems[name]

    def psem(self, i, swdge=False):
        g = self.current_group
        pid = (g.next_idx - 1) if g is not None else 0
        pre = "psg" if swdge else "ps"
        return self.sem(f"{pre}{pid}_{i}")

    def _rec(self, eng, method, args, kw, wait, inc):
        flat = []
        for t in wait:
            if t is None:
                continue
            if isinstance(t, list) or (isinstance(t, tuple) and t and
                                       isinstance(t[0], (tuple, list))):
                flat.extend(x for x in t if x is not None)
            else:
                flat.append(t)
        wait = tuple((t[0].h, t[1]) for t in flat)
        inc = tuple((s.h, v) for (s, v) in inc)
        import sys as _sys
        fr = _sys._getframe(2)
        tag = f"{fr.f_code.co_name}:{fr.f_lineno}"

        def fn(e, method=method, args=args, kw=kw, waits=wait, incs=inc,
               tag=tag):
            for h, v in waits:
                e.wait_ge(h, v)
            ins = getattr(e, method)(*args, **kw)
            ins.annotate(tag)
            for h, v in incs:
                ins.then_inc(h, v)
        self.op_seq += 1
        self.streams[eng].append((self.op_seq, fn))

    def op(self, eng, method, *args, wait=(), mark=False, **kw):
        inc = ()
        tok = None
        if eng in ("vector", "scalar", "gpsimd"):
            c = self.ch[eng]
            wait = tuple(wait) + ((c, c.n),)
            c.n += 1
            tok = (c, c.n)
            inc = ((c, 1),)
        elif mark:
            c = self.ch[eng]
            c.n += 1
            tok = (c, c.n)
            inc = ((c, 1),)
        self._rec(eng, method, args, kw, wait, inc)
        return tok

    def dma(self, eng, out, in_, sem, wait=(), indirect_out=None,
            indirect_in=None, **kw):
        sem.n += 16
        if indirect_out is not None or indirect_in is not None:
            oo = (bass.IndirectOffsetOnAxis(indirect_out, 0)
                  if indirect_out is not None else None)
            io = (bass.IndirectOffsetOnAxis(indirect_in, 0)
                  if indirect_in is not None else None)
            self._rec(eng, "indirect_dma_start", (),
                      dict(out=out, out_offset=oo, in_=in_, in_offset=io,
                           **kw), wait, ((sem, 16),))
        else:
            self._rec(eng, "dma_start", (), dict(out=out, in_=in_, **kw),
                      wait, ((sem, 16),))
        return (sem, sem.n)

    def phase(self):
        import sys as _sys
        name = _sys._getframe(1).f_code.co_name
        self.phase_count = getattr(self, "phase_count", 0) + 1
        return _PhaseCtx(self, f"{self.phase_count:02d}_{name}")

    def group(self, name):
        self.phase_count = getattr(self, "phase_count", 0) + 1
        return _GroupCtx(self, f"{self.phase_count:02d}_{name}")

    def barrier(self):
        dma_toks = tuple((s, s.n) for nm, s in self.sems.items()
                         if not nm.startswith("ch_") and s.n > 0
                         and nm != "s_zfill")
        for e in ENGINES:
            self._rec(e, "nop", (), {}, dma_toks, ())
        return

    def replay(self):
        streams = self.streams
        with self.nc.Block() as block:
            @block.sync
            def _(e):
                for _, fn in streams["sync"]:
                    fn(e)

            @block.tensor
            def _(e):
                for _, fn in streams["tensor"]:
                    fn(e)

            @block.vector
            def _(e):
                for _, fn in streams["vector"]:
                    fn(e)

            @block.scalar
            def _(e):
                for _, fn in streams["scalar"]:
                    fn(e)

            @block.gpsimd
            def _(e):
                for _, fn in streams["gpsimd"]:
                    fn(e)

    def scratch(self, name, shape, dtype):
        kind = "ExternalOutput" if self.debug else "Internal"
        return self.nc.dram_tensor(name, shape, dtype, kind=kind).ap()

    def build(self):
        nc = self.nc
        NT, T = self.NT, self.T
        inp = {}
        inp["x"] = nc.dram_tensor("x", [NT, D], F32, kind="ExternalInput").ap()
        for w, shape in [
            ("L0_attn_q_w", [D, D]), ("L0_attn_k_w", [D, D]),
            ("L0_attn_v_w", [D, DV0]), ("L0_attn_g_w", [D, DV0]),
            ("L0_attn_o_w", [DV0, D]),
            ("L0_ffn_W1", [DFF, D]), ("L0_ffn_W2", [D, DFF]),
            ("L1_attn_W_qkv", [3 * D, D]),
            ("L1_ffn_Wr", [D, E]), ("L1_ffn_W1", [E, D, HMOE]),
            ("L1_ffn_W2", [E, HMOE, D]),
            ("qb0_pp", [128, 8]), ("kb0_pp", [128, 8]),
            ("b1ff_pp", [128, 32]), ("qkb1_pp", [128, 16]),
            ("b1moe_pp", [128, E * 16]), ("br_pp", [E, 1]),
            ("vb0_row", [1, DV0]), ("gb0_row", [1, DV0]),
            ("ob0_row", [1, D]), ("b2ff_row", [1, D]),
            ("vb1_row", [1, D]), ("b2moe", [E, D]),
        ]:
            inp[w] = nc.dram_tensor(w, shape, F32, kind="ExternalInput").ap()
        for w, shape, dt in [
            ("cosq", [D, T], F32), ("sinq", [D, T], F32),
            ("cosk", [D, T], F32), ("sink", [D, T], F32),
            ("cos1", [128, T], F32), ("sin1", [128, T], F32),
            ("R0T", [128, 128], BF16), ("R1T", [128, 128], BF16),
            ("triu", [128, 128], F32),
            ("identB", [128, 128], BF16), ("identF", [128, 128], F32),
            ("onesB", [1, 128], BF16), ("eCm1", [128, E], F32),
        ]:
            inp[w] = nc.dram_tensor(w, shape, dt, kind="ExternalInput").ap()
        inp["iota_tb"] = nc.dram_tensor("iota_tb", [128, NT // 128],
                                        mybir.dt.int32,
                                        kind="ExternalInput").ap()
        self.inp = inp
        self.y_out = nc.dram_tensor("y", [NT, D], F32,
                                    kind="ExternalOutput").ap()

        self.qsc = self.scratch("qsc", [D, NT], BF16)
        self.ksc = self.scratch("ksc", [D, NT], BF16)
        self.ksctm = self.scratch("ksctm", [NT, D], BF16)
        self.vsc = self.scratch("vsc", [NT, DV0], BF16)
        self.gsc = self.scratch("gsc", [NT, DV0], BF16)
        self.w1qt = self.scratch("w1qt", [D, DFF], BF16)
        self.w2qt = self.scratch("w2qt", [DFF, D], BF16)
        self.ogf = self.scratch("ogf", [DV0, NT], BF16)
        self.g20 = self.scratch("g20", [NT, D], F32)
        self.x1s = self.scratch("x1s", [NT, D], F32)
        self.qr1 = self.scratch("qr1", [D, NT], BF16)
        self.kr1tm = self.scratch("kr1tm", [NT, D], BF16)
        self.kr1 = self.scratch("kr1", [D, NT], BF16)
        self.v1s = self.scratch("v1s", [NT, D], BF16)
        self.s1 = self.scratch("s1", [NT, D], F32)
        self.CAP = NT // 2
        self.xn1tm = self.scratch("xn1tm", [NT, D], BF16)
        self.xnsort = self.scratch("xnsort", [E * self.CAP, D], BF16)
        self.ysort = self.scratch("ysort", [E * self.CAP, D], F32)

        al = self.ctx.enter_context
        nc_ = nc
        self.xn_ctx = ExitStack()
        self.xn_fm = self.xn_ctx.enter_context(
            nc_.sbuf_tensor("xn_fm", [128, 8 * NT], BF16, side="right"))
        self.xnv = self.xn_fm[:, :].rearrange("p (c t) -> p c t", c=8)
        for nm, shape, dt in [
            ("identB", [128, 128], BF16), ("identF", [128, 128], F32),
            ("onesB", [1, 128], BF16), ("R0T", [128, 128], BF16),
            ("R1T", [128, 128], BF16), ("triu", [128, 128], F32),
            ("qb0", [128, 8], F32), ("kb0", [128, 8], F32),
            ("b1ff", [128, 32], F32), ("qkb1", [128, 16], F32),
            ("b1moe", [128, E * 16], F32), ("br", [E, 1], F32),
            ("ob0", [1, D], BF16), ("b2ff", [1, D], BF16),
            ("vb1", [1, D], BF16), ("b2moe", [E, D], BF16),
            ("mp", [128, self.NTB], F32), ("ind", [128, self.NTB * E], F32),
            ("eps", [128, 4], F32), ("rs", [128, 2], F32),
            ("eCm1", [128, E], F32),
            ("slots", [128, self.NTB], mybir.dt.int32),
            ("iota", [128, self.NTB], mybir.dt.int32),
            ("onesF", [128, 1], F32), ("onesFr", [1, 128], F32),
        ]:
            setattr(self, "c_" + nm, al(nc_.sbuf_tensor("c_" + nm, shape, dt)))

        self.p_const()
        if self.upto >= 1:
            self.p_ln(self.inp["x"], src_sem="p1x", dbg="xn0")
        if self.upto >= 2:
            self.p_proj_qk("q0", self.inp["L0_attn_q_w"], self.c_qb0[:, 0:8],
                           inp["cosq"], inp["sinq"], self.c_R0T, self.qsc)
            self.p_proj_qk("k0", self.inp["L0_attn_k_w"], self.c_kb0[:, 0:8],
                           inp["cosk"], inp["sink"], self.c_R0T, self.ksc,
                           tm_out=self.ksctm)
            with self.group("projtm_qgamma"):
                self.p_proj_tm("v0", self.inp["L0_attn_v_w"], DV0,
                               self.inp["vb0_row"], self.vsc, act=None)
                self.p_proj_tm("g0", self.inp["L0_attn_g_w"], DV0,
                               self.inp["gb0_row"], self.gsc, act=AF.Silu)
                if self.upto >= 3:
                    self.p_quant_gamma()
        if self.upto >= 5:
            with self.group("attn0_qapply"):
                self.p_attn0()
                if self.upto >= 3:
                    self.p_quant_apply()
        elif self.upto >= 3:
            self.p_quant_apply()
        if self.upto >= 4:
            self.p_ffn0()
        if self.upto >= 6:
            self.p_oproj_resid0()
        if self.upto >= 8 and self.upto >= 10:
            with self.group("qkv1_router"):
                self.p_qkv1()
                self.p_router()
        else:
            if self.upto >= 8:
                self.p_qkv1()
            if self.upto >= 10:
                self.p_router()
        self.xn_ctx.close()
        if self.upto >= 9 and self.upto >= 11:
            with self.group("moe_attn1"):
                self.p_attn1()
                self.p_moe()
        elif self.upto >= 9:
            self.p_attn1()
        if self.upto >= 12:
            self.p_resid1()
        self.finalize()
        self.ctx.close()
        return self.nc

    def p_const(self):
        inp = self.inp
        s_ld = self.sem("s_const")
        s_ldg = self.sem("s_constg")
        with self.phase():
            for src, dst in [
                ("identB", self.c_identB), ("identF", self.c_identF),
                ("onesB", self.c_onesB), ("R0T", self.c_R0T),
                ("R1T", self.c_R1T), ("triu", self.c_triu),
                ("qb0_pp", self.c_qb0), ("kb0_pp", self.c_kb0),
                ("b1ff_pp", self.c_b1ff), ("qkb1_pp", self.c_qkb1),
                ("b1moe_pp", self.c_b1moe), ("br_pp", self.c_br),
                ("eCm1", self.c_eCm1), ("iota_tb", self.c_iota),
            ]:
                self.dma("sync", dst[:, :], inp[src], s_ld)
            for src, dst in [
                ("ob0_row", self.c_ob0), ("b2ff_row", self.c_b2ff),
                ("vb1_row", self.c_vb1), ("b2moe", self.c_b2moe),
            ]:
                self.dma("gpsimd", dst[:, :], inp[src], s_ldg)
            self.op("vector", "memset", self.c_eps[:, 0:1], EPS_LN)
            self.op("vector", "memset", self.c_eps[:, 1:2], RND)
            self.op("vector", "memset", self.c_eps[:, 2:3], -RND)
            self.op("vector", "memset", self.c_eps[:, 3:4], 0.0)
            self.op("vector", "memset", self.c_onesF[:, :], 1.0)
            self.op("vector", "memset", self.c_onesFr[:, :], 1.0)

    def p_ln(self, x_ap, src_sem="p1x", dbg=None, resid=None):
        nc = self.nc
        NTB = self.NTB
        GRP = 4 if resid is None else 2
        dbg_ap = (self.scratch("dbg_" + dbg, [self.NT, D], BF16)
                  if (self.debug and dbg) else None)
        with self.phase() as ctx:
            al = ctx.enter_context
            pre = src_sem
            xt = [al(nc.sbuf_tensor(f"{pre}ln_x{i}", [128, GRP * D], F32))
                  for i in range(2)]
            if resid is not None:
                st_b = [al(nc.sbuf_tensor(f"{pre}ln_s{i}", [128, GRP * D], F32))
                        for i in range(2)]
                gt_b = [al(nc.sbuf_tensor(f"{pre}ln_g{i}", [128, GRP * D], F32))
                        for i in range(2)]
                x1t = al(nc.sbuf_tensor(pre + "ln_x1t", [128, D], F32))
                s_s = [self.psem(2), self.psem(3)]
                s_g = [self.psem(4), self.psem(5)]
                s_x1 = self.psem(6)
            xn_tm = al(nc.sbuf_tensor(pre + "ln_xn", [128, D], BF16))
            st = al(nc.sbuf_tensor(pre + "ln_st", [128, 8], F32))
            sq = al(nc.sbuf_tensor(pre + "ln_sq", [128, D], F32))
            pt = al(nc.psum_tensor(pre + "ln_pt", [128, 512], BF16))
            s_x = [self.psem(0), self.psem(1)]
            s_dbg = self.psem(7) if dbg_ap is not None else None
            src_ap = x_ap if resid is None else resid[0]
            x_r = src_ap.rearrange("(g a p) d -> g p a d", a=GRP, p=128)
            if resid is not None:
                s_r = resid[1].rearrange("(g a p) d -> g p a d", a=GRP, p=128)
                g_r = resid[2].rearrange("(g a p) d -> g p a d", a=GRP, p=128)
            free_tok = [(), ()]
            free_sg = [(), ()]
            xn_free = ()
            pt_free = None
            t_sq = None
            t_rs_prev = None
            for g in range(NTB // GRP):
                buf = xt[g % 2]
                ld = self.dma("sync",
                              buf[:, :].rearrange("p (a d) -> p a d", a=GRP),
                              x_r[g], s_x[g % 2], wait=free_tok[g % 2])
                if resid is not None:
                    lds = self.dma("sync",
                                   st_b[g % 2][:, :].rearrange(
                                       "p (a d) -> p a d", a=GRP),
                                   s_r[g], s_s[g % 2], wait=free_sg[g % 2])
                    ldg = self.dma("sync",
                                   gt_b[g % 2][:, :].rearrange(
                                       "p (a d) -> p a d", a=GRP),
                                   g_r[g], s_g[g % 2])
                for a in range(GRP):
                    tb = g * GRP + a
                    xa = buf[:, a * D:(a + 1) * D]
                    if resid is not None:
                        sl = slice(a * D, (a + 1) * D)
                        tmu = self.op("vector", "tensor_mul", sq[:, :],
                                      st_b[g % 2][:, sl], gt_b[g % 2][:, sl],
                                      wait=(lds, ldg) if a == 0 else ())
                        tx1 = self.op(
                            "vector", "tensor_add", x1t[:, :], xa, sq[:, :],
                            wait=((ld,) if a == 0 else ()) +
                                 ((t_sq,) if t_sq else ()) +
                                 ((s_x1, s_x1.n),))
                        self.dma("sync", x_ap[tb * 128:(tb + 1) * 128, :],
                                 x1t[:, :], s_x1, wait=(tx1,))
                        free_sg[g % 2] = (tmu,)
                        xa = x1t[:, :]
                    self.op("vector", "reduce_sum", st[:, 0:1], xa, axis=AX,
                            wait=(ld,) if (a == 0 and resid is None) else ())
                    self.op("vector", "tensor_scalar_mul", st[:, 1:2],
                            st[:, 0:1], -1.0 / D)
                    t_nm = self.op("vector", "tensor_scalar_mul", st[:, 2:3],
                                   st[:, 0:1], 1.0 / D, mark=True)
                    t_sq = self.op("scalar", "activation", sq[:, :], xa,
                                   AF.Square, bias=st[:, 1:2], scale=1.0,
                                   wait=(t_nm,))
                    t_ss = self.op("vector", "reduce_sum", st[:, 3:4],
                                   sq[:, :], axis=AX, wait=(t_sq,))
                    t_sd = self.op("scalar", "activation", st[:, 4:5],
                                   st[:, 3:4], AF.Sqrt,
                                   bias=self.c_eps[:, 0:1],
                                   scale=1.0 / D, wait=(t_ss,))
                    t_rc = self.op("vector", "reciprocal", st[:, 5:6],
                                   st[:, 4:5], wait=(t_sd,), mark=True)
                    t_xn = self.op("vector", "tensor_scalar", xn_tm[:, :], xa,
                                   st[:, 2:3], st[:, 5:6], ALU.subtract,
                                   ALU.mult, wait=xn_free, mark=True)
                    xn_free = ()
                    if resid is not None:
                        xtk = self.dma("sync",
                                       self.xn1tm[tb * 128:(tb + 1) * 128, :],
                                       xn_tm[:, :], s_x1, wait=(t_xn,))
                        xn_free = xn_free + (xtk,)
                    if dbg_ap is not None:
                        dtk = self.dma("sync",
                                       dbg_ap[tb * 128:(tb + 1) * 128, :],
                                       xn_tm[:, :], s_dbg, wait=(t_xn,))
                        xn_free = xn_free + (dtk,)
                    for half in range(2):
                        toks = []
                        for j in range(4):
                            c = half * 4 + j
                            pw = ()
                            if j == 0:
                                pw = (t_xn,) if half == 0 else ()
                                if pt_free is not None:
                                    pw = pw + (pt_free,)
                            tt = self.op(
                                "tensor", "transpose",
                                pt[:, j * 128:(j + 1) * 128],
                                xn_tm[:, c * 128:(c + 1) * 128],
                                self.c_identB[:, :],
                                wait=pw, mark=(j == 3))
                            if tt:
                                toks.append(tt)
                        pt_free = self.op(
                            "vector", "tensor_copy",
                            self.xnv[:, half * 4:(half + 1) * 4,
                                     tb * 128:(tb + 1) * 128],
                            pt[:, :].rearrange("p (j t) -> p j t", j=4),
                            wait=tuple(toks), mark=True)
                    xn_free = xn_free + (tt,)
                free_tok[g % 2] = (t_sq, t_xn)

    def p_proj_qk(self, name, w_ap, bias_col8, cos_ap, sin_ap, rotT, out_ap,
                  tm_out=None):
        nc = self.nc
        NT, T = self.NT, self.T
        NTB = self.NTB
        NC = NT // 512
        with self.phase() as ctx:
            al = ctx.enter_context
            if tm_out is not None:
                ktm = al(nc.sbuf_tensor(name + "_ktm", [128, NT], BF16))
                psT = al(nc.psum_tensor(name + "_psT", [128, 512], BF16))
                s_tm = self.psem(4)
            w_sb = al(nc.sbuf_tensor(name + "_w", [128, 8 * D], BF16))
            wv = w_sb[:, :].rearrange("p (c f) -> p c f", c=8)
            tab = [al(nc.sbuf_tensor(f"{name}_tab{i}", [128, 2 * T], F32))
                   for i in range(2)]
            qtmp = al(nc.sbuf_tensor(name + "_tmp", [128, NT], BF16))
            t1 = al(nc.sbuf_tensor(name + "_t1", [128, 512], F32))
            t2 = al(nc.sbuf_tensor(name + "_t2", [128, 512], F32))
            qout = al(nc.sbuf_tensor(name + "_out", [128, NT], BF16))
            pq = [al(nc.psum_tensor(f"{name}_pq{i}", [128, 512], F32))
                  for i in range(2)]
            pr = [al(nc.psum_tensor(f"{name}_pr{i}", [128, 512], F32))
                  for i in range(2)]
            s_w = self.psem(0, swdge=True)
            s_tab = [self.psem(1), self.psem(2)]
            s_out = self.psem(3)
            wr_ = w_ap.rearrange("(c p) f -> p c f", p=128)
            ldw_h = self.dma("gpsimd", wv[:, :, 0:128], wr_[:, :, 0:128],
                             s_w)
            ldw = self.dma("gpsimd", wv[:, :, 128:D], wr_[:, :, 128:D],
                           s_w)

            def load_tab(fb, wait=()):
                tbuf = tab[fb % 2]
                self.dma("sync", tbuf[:, 0:T],
                         cos_ap[fb * 128:(fb + 1) * 128, :], s_tab[fb % 2],
                         wait=wait)
                lt2 = self.dma("sync", tbuf[:, T:2 * T],
                               sin_ap[fb * 128:(fb + 1) * 128, :],
                               s_tab[fb % 2])
                return (lt2, lt2)

            tab_tok = {0: load_tab(0), 1: load_tab(1)}
            out_tok = None
            tm_tr = None
            tm_ev = None
            evac_toks = {}
            rot_toks = {}
            add_toks = {}
            last_add = None

            def flush(fbc, nc_, t_evc, tcolc, ltc, lt2c, tbufc):
                nonlocal last_add
                rbank = pr[nc_ % 2]
                rw = [t_evc]
                prev_t2 = add_toks.get((fbc, nc_ - 2) if nc_ >= 2
                                       else (fbc - 1, NC - 2 + nc_))
                if prev_t2:
                    rw.append(prev_t2)
                t_rot = self.op("tensor", "matmul", rbank[:, :],
                                rotT[:, :],
                                qtmp[:, nc_ * 512:(nc_ + 1) * 512],
                                start=True, stop=True,
                                wait=tuple(rw), mark=True)
                rot_toks[(fbc, nc_)] = t_rot
                w1 = [t_evc, (ltc if nc_ == 0 else None),
                      (lt2c if nc_ == 0 else None)]
                if out_tok is not None and nc_ == 0:
                    w1.append(out_tok)
                    if tm_out is not None and tm_tr is not None:
                        w1.append(tm_tr)
                self.op("vector", "tensor_mul", t1[:, :],
                        qtmp[:, nc_ * 512:(nc_ + 1) * 512],
                        tbufc[:, tcolc:tcolc + 512],
                        wait=tuple(t for t in w1 if t))
                self.op("vector", "tensor_mul", t2[:, :], rbank[:, :],
                        tbufc[:, T + tcolc:T + tcolc + 512],
                        wait=(t_rot,))
                last_add = self.op("vector", "tensor_add",
                                   qout[:, nc_ * 512:(nc_ + 1) * 512],
                                   t1[:, :], t2[:, :], mark=True)
                add_toks[(fbc, nc_)] = last_add

            for fb in range(8):
                lt, lt2 = tab_tok[fb]
                tbuf = tab[fb % 2]
                pend = None
                for n in range(NC):
                    s = (n * 512) // T
                    tcol = n * 512 - s * T
                    bank = pq[n % 2]
                    for c in range(8):
                        pw = []
                        if c == 0:
                            if fb == 0 and n < 2:
                                pw.append(ldw_h)
                            if fb == 1 and n < 2:
                                pw.append(ldw)
                            prev = evac_toks.get((fb, n - 2) if n >= 2
                                                 else (fb - 1, NC - 2 + n))
                            if prev:
                                pw.append(prev)
                        self.op("tensor", "matmul", bank[:, :],
                                wv[:, c, fb * 128:(fb + 1) * 128],
                                self.xnv[:, c, n * 512:(n + 1) * 512],
                                start=(c == 0), stop=(c == 7),
                                wait=tuple(pw), mark=(c == 7))
                    t_mm = (self.ch["tensor"], self.ch["tensor"].n)
                    ev_w = [t_mm]
                    prev_rot = rot_toks.get((fb - 1, n))
                    if prev_rot:
                        ev_w.append(prev_rot)
                    prev_t1 = add_toks.get((fb - 1, n))
                    if prev_t1:
                        ev_w.append(prev_t1)
                    t_ev = self.op("scalar", "add",
                                   qtmp[:, n * 512:(n + 1) * 512], bank[:, :],
                                   bias_col8[:, fb:fb + 1],
                                   wait=tuple(ev_w), mark=True)
                    evac_toks[(fb, n)] = t_ev
                    if pend is not None:
                        flush(*pend)
                    pend = (fb, n, t_ev, tcol, lt, lt2, tbuf)
                flush(*pend)
                if fb + 2 < 8:
                    tab_tok[fb + 2] = load_tab(fb + 2, wait=(last_add,))
                out_tok = self.dma("sync", out_ap[fb * 128:(fb + 1) * 128, :],
                                   qout[:, :], s_out, wait=(last_add,))
                if tm_out is not None:
                    ev = None
                    for grp in range(NTB // 4):
                        for j in range(4):
                            tbk = grp * 4 + j
                            pw = ()
                            if j == 0:
                                pw = (last_add,) if grp == 0 else ()
                                if tm_tr is not None:
                                    pw = pw + (tm_ev,)
                            tm_tr = self.op(
                                "tensor", "transpose",
                                psT[:, j * 128:(j + 1) * 128],
                                qout[:, tbk * 128:(tbk + 1) * 128],
                                self.c_identB[:, :], wait=pw, mark=(j == 3))
                        tm_ev = self.op(
                            "scalar", "copy",
                            ktm[:, grp * 512:(grp + 1) * 512], psT[:, :],
                            wait=(tm_tr,) +
                                 (((s_tm, s_tm.n),) if grp == 0 else ()))
                    self.dma("sync",
                             tm_out.rearrange("(i p) f -> p i f", p=128)
                             [:, :, fb * 128:(fb + 1) * 128],
                             ktm[:, :].rearrange("p (i f) -> p i f", f=128),
                             s_tm, wait=(tm_ev,))
                    out_tok = (out_tok[0], out_tok[1])

    def p_proj_tm(self, name, w_ap, FW, bias_ap, out_ap, act=None):
        nc = self.nc
        NTB = self.NTB
        NF = FW // 512
        with self.phase() as ctx:
            al = ctx.enter_context
            w_sb = al(nc.sbuf_tensor(name + "_w", [128, 8 * FW], BF16))
            wv = w_sb[:, :].rearrange("p (c f) -> p c f", c=8)
            bias_row = al(nc.sbuf_tensor(name + "_b", [1, FW], BF16))
            vout = al(nc.sbuf_tensor(name + "_out", [128, 2 * FW], BF16))
            pb = [al(nc.psum_tensor(f"{name}_p{i}", [128, 512], F32))
                  for i in range(2)]
            s_w = self.psem(0, swdge=True)
            s_out = self.psem(3)
            wr_ = w_ap.rearrange("(c p) f -> p c f", p=128)
            ldb = self.dma("gpsimd", bias_row[:, :], bias_ap, s_w)
            ldw_h = self.dma("gpsimd", wv[:, :, 0:512], wr_[:, :, 0:512],
                             s_w)
            ldw = self.dma("gpsimd", wv[:, :, 512:FW], wr_[:, :, 512:FW],
                           s_w)
            ev_tok = {}
            out_dma = [None, None]
            k = 0
            for tb in range(NTB):
                obuf = vout[:, (tb % 2) * FW:(tb % 2 + 1) * FW]
                for fc in range(NF):
                    bank = pb[k % 2]
                    for c in range(8):
                        pw = []
                        if c == 0:
                            if k < 2:
                                pw.append(ldw_h if k == 0 else ldw)
                            prev = ev_tok.get(k - 2)
                            if prev:
                                pw.append(prev)
                        self.op("tensor", "matmul", bank[:, :],
                                self.xnv[:, c, tb * 128:(tb + 1) * 128],
                                wv[:, c, fc * 512:(fc + 1) * 512],
                                start=(c == 0), stop=False,
                                wait=tuple(pw))
                    self.op("tensor", "matmul", bank[:, :],
                            self.c_onesB[0:1, :],
                            bias_row[0:1, fc * 512:(fc + 1) * 512],
                            start=False, stop=True, mark=True)
                    t_mm = (self.ch["tensor"], self.ch["tensor"].n)
                    ew = [t_mm]
                    if fc == 0 and out_dma[tb % 2] is not None:
                        ew.append((s_out, s_out.n))
                    if act is None:
                        t_ev = self.op("vector", "tensor_copy",
                                       obuf[:, fc * 512:(fc + 1) * 512],
                                       bank[:, :], wait=tuple(ew), mark=True)
                    else:
                        t_ev = self.op("scalar", "activation",
                                       obuf[:, fc * 512:(fc + 1) * 512],
                                       bank[:, :], act, bias=0.0, scale=1.0,
                                       wait=tuple(ew), mark=True)
                    ev_tok[k] = t_ev
                    k += 1
                out_dma[tb % 2] = self.dma(
                    "sync", out_ap[tb * 128:(tb + 1) * 128, :], obuf,
                    s_out, wait=(ev_tok[k - 1], ev_tok[k - NF]))

    def p_quant_gamma(self):
        nc = self.nc
        W1, W2 = self.inp["L0_ffn_W1"], self.inp["L0_ffn_W2"]
        with self.phase() as ctx:
            al = ctx.enter_context
            wt = [al(nc.sbuf_tensor(f"qg_w{i}", [128, 2048], F32))
                  for i in range(2)]
            parts = al(nc.sbuf_tensor("qg_parts", [128, 48], F32))
            strow = al(nc.sbuf_tensor("qg_st", [1, 56], F32))
            ptot = al(nc.psum_tensor("qg_ptot", [1, 48], F32))
            pbc = al(nc.psum_tensor("qg_pbc", [128, 2], F32))
            s_w = [self.psem(0), self.psem(1)]
            k = 0
            free = [(), ()]
            col = 0
            for (w_ap, ntile, width) in ((W1, 32, D), (W2, 8, DFF)):
                for tb in range(ntile):
                    for hw in range(width // 2048 if width > 2048 else 1):
                        wslice = w_ap[tb * 128:(tb + 1) * 128,
                                      hw * 2048:hw * 2048 + min(2048, width)]
                        buf = wt[k % 2]
                        ld = self.dma("scalar", buf[:, 0:min(2048, width)],
                                      wslice, s_w[k % 2], wait=free[k % 2])
                        free[k % 2] = (self.op(
                            "vector", "reduce_sum", parts[:, col:col + 1],
                            buf[:, 0:min(2048, width)], axis=AX,
                            apply_absolute_value=True, wait=(ld,)),)
                        k += 1
                        col += 1
            assert col == 48
            t_mm = self.op("tensor", "matmul", ptot[:, 0:48],
                           self.c_onesF[:, 0:1], parts[:, 0:48],
                           start=True, stop=True,
                           wait=free[0] + free[1], mark=True)
            self.op("vector", "tensor_copy", strow[0:1, 0:48], ptot[0:1, :],
                    wait=(t_mm,))
            self.op("vector", "reduce_sum", strow[0:1, 48:49],
                    strow[0:1, 0:32], axis=AX)
            self.op("vector", "reduce_sum", strow[0:1, 49:50],
                    strow[0:1, 32:48], axis=AX)
            self.op("vector", "tensor_scalar", strow[0:1, 50:51],
                    strow[0:1, 48:49], 1.0 / (DFF * D), EPS_LN,
                    ALU.mult, ALU.add)
            self.op("vector", "tensor_scalar", strow[0:1, 51:52],
                    strow[0:1, 49:50], 1.0 / (DFF * D), EPS_LN,
                    ALU.mult, ALU.add)
            t_rc = self.op("vector", "reciprocal", strow[0:1, 52:54],
                           strow[0:1, 50:52])
            t_bc = self.op("tensor", "matmul", pbc[:, :],
                           self.c_onesFr[0:1, :],
                           strow[0:1, 52:54], start=True, stop=True,
                           wait=(t_rc,), mark=True)
            self.op("vector", "tensor_copy", self.c_rs[:, :], pbc[:, :],
                    wait=(t_bc,))

    def p_quant_apply(self):
        nc = self.nc
        W1, W2 = self.inp["L0_ffn_W1"], self.inp["L0_ffn_W2"]
        with self.phase() as ctx:
            al = ctx.enter_context
            wt = [al(nc.sbuf_tensor(f"qa_w{i}", [128, 1024], F32))
                  for i in range(2)]
            qtmp = al(nc.sbuf_tensor("qa_qtmp", [128, 1024], F32))
            qbf = [al(nc.sbuf_tensor(f"qa_qbf{i}", [128, 1024], BF16))
                   for i in range(2)]
            tbuf = [al(nc.sbuf_tensor(f"qa_tbuf{i}", [128, 1024], BF16))
                    for i in range(2)]
            ptr = al(nc.psum_tensor("qa_ptr", [128, 512], BF16))
            s_w = [self.psem(0, swdge=True), self.psem(1, swdge=True)]
            s_out = self.psem(2, swdge=True)
            k = 0
            free = [(), ()]
            ptr_free = None
            tq_free = {}
            for (w_ap, out_ap, ntile, nch, col) in (
                    (W1, self.w1qt, 32, 1, 0), (W2, self.w2qt, 8, 4, 1)):
                for tb in range(ntile):
                    for hw in range(nch):
                        buf = wt[k % 2]
                        ld = self.dma("gpsimd", buf[:, :],
                                      w_ap[tb * 128:(tb + 1) * 128,
                                           hw * 1024:(hw + 1) * 1024],
                                      s_w[k % 2], wait=free[k % 2])
                        qb = qbf[k % 2]
                        tb_ = tbuf[k % 2]
                        t_q1 = self.op("vector", "tensor_scalar",
                                       qtmp[:, :], buf[:, :],
                                       self.c_rs[:, col:col + 1],
                                       self.c_eps[:, 1:2],
                                       ALU.mult, ALU.add, wait=(ld,))
                        self.op("vector", "tensor_scalar", qtmp[:, :],
                                qtmp[:, :], self.c_eps[:, 1:2], 1.0,
                                ALU.subtract, ALU.min)
                        t_q = self.op("vector", "tensor_scalar_max",
                                      qb[:, :], qtmp[:, :], -1.0,
                                      wait=(tq_free.get(k % 2),))
                        free[k % 2] = (t_q1,)
                        ev = []
                        for grp in range(2):
                            for j in range(4):
                                c = grp * 4 + j
                                pw = ()
                                if j == 0:
                                    pw = (t_q,)
                                    if ptr_free is not None:
                                        pw = pw + (ptr_free,)
                                tt = self.op("tensor", "transpose",
                                             ptr[:, j * 128:(j + 1) * 128],
                                             qb[:, c * 128:(c + 1) * 128],
                                             self.c_identB[:, :],
                                             wait=pw, mark=(j == 3))
                            evw = [tt]
                            if grp == 0:
                                evw.append((s_out, s_out.n))
                            ptr_free = self.op(
                                "scalar", "copy",
                                tb_[:, grp * 512:(grp + 1) * 512],
                                ptr[:, :], wait=tuple(evw))
                            ev.append(ptr_free)
                        tq_free[k % 2] = tt
                        self.dma("gpsimd",
                                 out_ap.rearrange("(c p) f -> p c f", p=128)
                                 [:, hw * 8:(hw + 1) * 8,
                                  tb * 128:(tb + 1) * 128],
                                 tb_[:, :].rearrange("p (c f) -> p c f",
                                                     c=8),
                                 s_out, wait=(ev[-1],))
                        k += 1

    def p_ffn0(self):
        nc = self.nc
        NT = self.NT
        NCH = NT // 512
        with self.phase() as ctx:
            al = ctx.enter_context
            w1t = al(nc.sbuf_tensor("f0_w1t", [128, 8 * DFF], BF16))
            w1v = w1t[:, :].rearrange("p (c f) -> p c f", c=8)
            w2q = [al(nc.sbuf_tensor(f"f0_w2q{i}", [128, 4 * D], BF16))
                   for i in range(2)]
            h_sb = al(nc.sbuf_tensor("f0_h", [128, 32 * 512], BF16))
            hv = h_sb[:, :].rearrange("p (f t) -> p f t", f=32)
            g2t = [al(nc.sbuf_tensor(f"f0_g2_{i}", [128, 512], F32))
                   for i in range(2)]
            pp = [al(nc.psum_tensor(f"f0_p{i}", [128, 512], F32))
                  for i in range(8)]
            s_w1 = self.psem(0)
            s_w2 = [self.psem(1), self.psem(2)]
            s_g2 = [self.psem(3), self.psem(4)]
            ld1 = self.dma("sync", w1v,
                           self.w1qt.rearrange("(c p) f -> p c f", p=128),
                           s_w1)
            w2r = self.w2qt.rearrange("(c p) f -> p c f", p=128)

            def load_w2(g, wait=()):
                q = g % 8
                return self.dma("sync",
                                w2q[g % 2][:, :].rearrange(
                                    "p (c f) -> p c f", c=4),
                                w2r[:, q * 4:(q + 1) * 4, :], s_w2[g % 2],
                                wait=wait)

            w2tok = {0: load_w2(0), 1: load_w2(1)}
            g2free = {}
            kg = 0
            for tc in range(NCH):
                relu_toks = []
                for fb in range(32):
                    bank = pp[fb % 2]
                    for c in range(8):
                        pw = []
                        if c == 0:
                            if fb < 2:
                                if tc == 0 and fb == 0:
                                    pw.append(ld1)
                                gf = g2free.get(fb)
                                if gf:
                                    pw.append(gf)
                            else:
                                pw.append(relu_toks[fb - 2])
                        self.op("tensor", "matmul", bank[:, :],
                                w1v[:, c, fb * 128:(fb + 1) * 128],
                                self.xnv[:, c, tc * 512:(tc + 1) * 512],
                                start=(c == 0), stop=(c == 7),
                                wait=tuple(pw), mark=(c == 7))
                    t_mm = (self.ch["tensor"], self.ch["tensor"].n)
                    relu_toks.append(self.op(
                        "scalar", "activation", hv[:, fb, :], bank[:, :],
                        AF.Relu, bias=self.c_b1ff[:, fb:fb + 1], scale=1.0,
                        wait=(t_mm,)))
                for q in range(8):
                    g = tc * 8 + q
                    for tb in range(4):
                        for dc in range(2):
                            bank = pp[tb * 2 + dc]
                            for j in range(4):
                                fc = q * 4 + j
                                pw = []
                                if j == 0:
                                    if tb == 0 and dc == 0:
                                        pw.append(w2tok[g])
                                    if q == 0:
                                        if tb == 0 and dc == 0:
                                            pw.append(relu_toks[-1])
                                        gf = g2free.get(tb * 2 + dc)
                                        if gf:
                                            pw.append(gf)
                                pe_tok = self.op(
                                    "tensor", "matmul", bank[:, :],
                                    hv[:, fc, tb * 128:(tb + 1) * 128],
                                    w2q[g % 2][:, :].rearrange(
                                        "p (c f) -> p c f", c=4)
                                    [:, j, dc * 512:(dc + 1) * 512],
                                    start=(fc == 0), stop=False,
                                    wait=tuple(pw),
                                    mark=(tb == 3 and dc == 1 and j == 3))
                    if g + 2 < NCH * 8:
                        w2tok[g + 2] = load_w2(g + 2, wait=(pe_tok,))
                for tb in range(4):
                    for dc in range(2):
                        bank = pp[tb * 2 + dc]
                        t_b = self.op("tensor", "matmul", bank[:, :],
                                      self.c_onesB[0:1, :],
                                      self.c_b2ff[0:1, dc * 512:(dc + 1) * 512],
                                      start=False, stop=True, mark=True)
                        gb = g2t[kg % 2]
                        t_g = self.op("scalar", "activation", gb[:, :],
                                      bank[:, :], AF.Silu, bias=0.0,
                                      scale=1.0,
                                      wait=(t_b,
                                            (s_g2[kg % 2], s_g2[kg % 2].n)))
                        self.dma(
                            "sync",
                            self.g20[(tc * 4 + tb) * 128:
                                     (tc * 4 + tb + 1) * 128,
                                     dc * 512:(dc + 1) * 512],
                            gb[:, :], s_g2[kg % 2], wait=(t_g,))
                        g2free[tb * 2 + dc] = t_g
                        kg += 1
        return

    def p_attn0(self):
        nc = self.nc
        B, T, TB = self.B, self.T, self.TB
        with self.phase() as ctx:
            al = ctx.enter_context
            qk = [al(nc.sbuf_tensor(f"a0_qk{i}", [128, 2 * T], BF16))
                  for i in range(2)]
            vx = [al(nc.sbuf_tensor(f"a0_vx{i}", [128, TB * 132], BF16))
                  for i in range(2)]
            sg = [al(nc.sbuf_tensor(f"a0_sg{i}", [128, TB * 128], BF16))
                  for i in range(2)]
            og = [al(nc.sbuf_tensor(f"a0_og{i}", [128, T], BF16))
                  for i in range(2)]
            oraw = [al(nc.sbuf_tensor(f"a0_or{i}", [128, TB * 132], F32))
                    for i in range(2)]
            onb = al(nc.sbuf_tensor("a0_onb", [128, TB * 128], F32))
            sqb = al(nc.sbuf_tensor("a0_sqb", [128, TB * 128], F32))
            nrmb = al(nc.sbuf_tensor("a0_nrmb", [128, TB * 128], F32))
            ogb = [al(nc.sbuf_tensor(f"a0_ogb{i}", [128, TB * 128], BF16))
                   for i in range(2)]
            S_sb = [al(nc.sbuf_tensor(f"a0_S{i}", [128, 132], BF16))
                    for i in range(4)]
            ktm = [al(nc.sbuf_tensor(f"a0_ktm{i}", [128, TB * 64], BF16))
                   for i in range(2)]
            dTs = [al(nc.sbuf_tensor(f"a0_dT{i}", [128, 128], BF16))
                   for i in range(2)]
            dNs = [al(nc.sbuf_tensor(f"a0_dN{i}", [128, 128], BF16))
                   for i in range(2)]
            st = al(nc.sbuf_tensor("a0_st", [128, 8 * TB], F32))
            stv = st[:, :].rearrange("p (k i) -> p k i", k=8)
            psS = [al(nc.psum_tensor(f"a0_psS{i}", [64, 132], F32))
                   for i in range(2)]
            psD = [al(nc.psum_tensor(f"a0_psD{i}", [128, 256], F32))
                   for i in range(2)]
            psO = [al(nc.psum_tensor(f"a0_psO{i}", [128, 132], F32))
                   for i in range(2)]
            psT2 = al(nc.psum_tensor("a0_psT2", [128, 512], BF16))
            s_qk = [self.psem(0), self.psem(1)]
            s_kt = [self.psem(2), self.psem(3)]
            s_vx = [self.psem(4), self.psem(5)]
            s_sg = [self.psem(6), self.psem(7)]
            s_og = [self.psem(8), self.psem(9)]
            mset = []
            for i in range(2):
                mset.append(self.op(
                    "vector", "memset",
                    vx[i][:, :].rearrange("p (i c) -> p i c", c=132)
                    [:, :, 128:132], 1.0))
            vsc_r = self.vsc.rearrange("(b i p) c -> b p i c", i=TB, p=128)
            ktm_r = self.ksctm.rearrange("(b i p) c -> b p i c", i=TB, p=128)
            gsc_r = self.gsc.rearrange("(b i p) c -> b p i c", i=TB, p=128)
            tkD = {}
            tkO = {}
            last_snap = {}
            t_av = {}
            b_on_last = None
            b_sq_last = None
            b_ssq = None
            b_og = None
            b_abs = None
            b_nrm_last = None
            b_pmn = None
            b_sd = None
            pe_group = {}
            pe_head = {}
            kp = kh = 0
            prev_c = None
            pend_b = None
            cstate = {"tkT2": None, "c_tr_last": None}

            def emit_c(kh_, s_, h_, pv_, b_og_):
                ev_last = None
                for grp in range(TB // 4):
                    for j in range(4):
                        i = grp * 4 + j
                        pw = ()
                        if j == 0:
                            pw = ((b_og_, cstate["tkT2"]) if grp == 0
                                  else (cstate["tkT2"],))
                        cstate["c_tr_last"] = self.op(
                            "tensor", "transpose",
                            psT2[:, j * 128:(j + 1) * 128],
                            ogb[kh_ % 2][:, i * 128:(i + 1) * 128],
                            self.c_identB[:, :], wait=pw,
                            mark=(j == 3))
                    ev_last = self.op(
                        "scalar", "copy",
                        og[pv_][:, grp * 512:(grp + 1) * 512],
                        psT2[:, :],
                        wait=(cstate["c_tr_last"],
                              (s_og[pv_], s_og[pv_].n)
                              if grp == 0 else None))
                    cstate["tkT2"] = ev_last
                self.dma("sync",
                         self.ogf[128 * h_:128 * (h_ + 1),
                                  s_ * T:(s_ + 1) * T],
                         og[pv_][:, :], s_og[pv_], wait=(ev_last,))

            def emit_b(kh_, s_, h_, pv_, l_s_, a_or_last_, a_or_prev_):
                nonlocal b_abs, b_on_last, b_sq_last, b_ssq, b_og, \
                    b_nrm_last, b_pmn, b_sd
                orv_ = oraw[pv_][:, :].rearrange("p (i c) -> p i c", c=132)
                b_abs = self.op("scalar", "activation",
                                stv[:, 0, :], orv_[:, :, 128],
                                AF.Abs, bias=0.0, scale=1.0,
                                wait=(a_or_last_, a_or_prev_,
                                      b_nrm_last))
                self.op("vector", "tensor_scalar_max",
                        stv[:, 1, :], stv[:, 0, :], 1.0,
                        wait=(b_abs,))
                b_rr = self.op("vector", "reciprocal",
                               stv[:, 2, :], stv[:, 1, :])
                prev_sq = b_sq_last
                for i in range(TB):
                    b_on_last = self.op(
                        "vector", "tensor_scalar_mul",
                        onb[:, i * 128:(i + 1) * 128],
                        orv_[:, i, 0:128], stv[:, 2, i:i + 1],
                        wait=(prev_sq,) if i == 0 else ())
                b_mn = self.op("vector", "reduce_sum", stv[:, 3, :],
                               onb[:, :].rearrange(
                                   "p (i c) -> p i c", c=128),
                               axis=AX, wait=(b_pmn,))
                b_nmn = self.op("vector", "tensor_scalar_mul",
                                stv[:, 4, :], stv[:, 3, :],
                                -1.0 / 128, wait=(b_mn, b_sq_last))
                b_pmn = self.op("vector", "tensor_scalar_mul",
                                stv[:, 5, :], stv[:, 3, :],
                                1.0 / 128)
                prev_ssq = b_ssq
                for i in range(TB):
                    b_sq_last = self.op(
                        "scalar", "activation",
                        sqb[:, i * 128:(i + 1) * 128],
                        onb[:, i * 128:(i + 1) * 128], AF.Square,
                        bias=stv[:, 4, i:i + 1], scale=1.0,
                        wait=(b_nmn, b_on_last, prev_ssq)
                        if i == 0 else ())
                b_ssq = self.op("vector", "reduce_sum", stv[:, 6, :],
                                sqb[:, :].rearrange(
                                    "p (i c) -> p i c", c=128),
                                axis=AX, wait=(b_sq_last, b_sd))
                b_sd = self.op("scalar", "activation", stv[:, 7, :],
                               stv[:, 6, :], AF.Sqrt,
                               bias=self.c_eps[:, 0:1],
                               scale=1.0 / 128, wait=(b_ssq,))
                b_rstd = self.op("vector", "reciprocal",
                                 stv[:, 0, :], stv[:, 7, :],
                                 wait=(b_sd,))
                prev_og = b_og
                for i in range(TB):
                    b_nrm_last = self.op(
                        "vector", "tensor_scalar",
                        nrmb[:, i * 128:(i + 1) * 128],
                        onb[:, i * 128:(i + 1) * 128],
                        stv[:, 5, i:i + 1], stv[:, 0, i:i + 1],
                        ALU.subtract, ALU.mult,
                        wait=(b_rstd, prev_og) if i == 0 else ())
                b_og = self.op("gpsimd", "tensor_mul",
                               ogb[kh_ % 2][:, :],
                               nrmb[:, :], sg[pv_][:, :],
                               wait=(l_s_, cstate["c_tr_last"],
                                     b_nrm_last))
                return (kh_, s_, h_, pv_, b_og)

            for s in range(B):
                for hp in range(8):
                    pq = kp % 2
                    self.dma("sync", qk[pq][:, 0:T],
                             self.qsc[hp * 128:(hp + 1) * 128,
                                      s * T:(s + 1) * T], s_qk[pq],
                             wait=(pe_group.get(pq),))
                    l_k = self.dma("sync", qk[pq][:, T:2 * T],
                                   self.ksc[hp * 128:(hp + 1) * 128,
                                            s * T:(s + 1) * T], s_qk[pq])
                    for hh in range(2):
                        h = 2 * hp + hh
                        rb = 64 * hh
                        pv = kh % 2
                        wv = (pe_head.get(pv), mset[pv] if kh < 2 else None)
                        l_v = self.dma(
                            "sync",
                            vx[pv][:, :].rearrange("p (i c) -> p i c", c=132)
                            [:, :, 0:128],
                            vsc_r[s][:, :, 128 * h:128 * h + 128],
                            s_vx[pv], wait=wv)
                        l_s = self.dma(
                            "sync",
                            sg[pv][:, :].rearrange("p (i c) -> p i c", c=128),
                            gsc_r[s][:, :, 128 * h:128 * h + 128],
                            s_sg[pv], wait=(b_og,))
                        l_kt = self.dma(
                            "sync",
                            ktm[pv][:, :].rearrange("p (i c) -> p i c", c=64),
                            ktm_r[s][:, :, 64 * h:64 * h + 64],
                            s_kt[pv], wait=(pe_head.get(pv),))
                        qs = qk[pq][rb:rb + 64, 0:T]
                        ks = qk[pq][rb:rb + 64, T:2 * T]
                        orv = oraw[pv][:, :].rearrange("p (i c) -> p i c",
                                                       c=132)
                        ktv = ktm[pv][:, :].rearrange("p (i c) -> p i c",
                                                      c=64)
                        vxv = vx[pv][:, :].rearrange("p (i c) -> p i c",
                                                    c=132)
                        snap = {-2: last_snap.get(0),
                                -1: last_snap.get(1)}
                        buf = lambda j: 2 * (j % 2) + ((j // 2) % 2)
                        dT_tok = {}

                        def emit_scores(j, first):
                            pj = j % 2
                            jb = slice(j * 128, (j + 1) * 128)
                            jp = slice((j - 1) * 128, j * 128)
                            t3 = self.op("tensor", "matmul",
                                         psD[pj][:, 0:128],
                                         ks[:, jb], qs[:, jb],
                                         start=True, stop=True,
                                         wait=(tkD.get(pj),
                                               l_k if first else None),
                                         mark=(j == 0))
                            if j > 0:
                                t3 = self.op("tensor", "matmul",
                                             psD[pj][:, 128:256],
                                             ks[:, jp], qs[:, jb],
                                             start=True, stop=True,
                                             mark=True)
                            t4 = self.op("vector", "tensor_mul",
                                         dTs[pj][:, :], psD[pj][:, 0:128],
                                         self.c_triu[:, :],
                                         wait=(t3, t_av.get(pj)))
                            t4n = None
                            if j > 0:
                                t4n = self.op("vector", "tensor_copy",
                                              dNs[pj][:, :],
                                              psD[pj][:, 128:256],
                                              wait=(t3,))
                            tkD[pj] = (t4, t4n) if j > 0 else (t4,)
                            dT_tok[j] = (t4, t4n)

                        emit_scores(0, hh == 0)
                        for i in range(TB):
                            pi = i % 2
                            ib = slice(i * 128, (i + 1) * 128)
                            vxi = vxv[:, i, :]
                            if i + 1 < TB:
                                emit_scores(i + 1, False)
                            t6 = self.op("tensor", "matmul",
                                         psS[pi][:, :],
                                         ktv[:, i, :], vxi,
                                         start=(i < 2), stop=(i >= TB - 2),
                                         skip_group_check=True,
                                         wait=(snap.get(i - 2),
                                               l_kt if i == 0 else None,
                                               l_v if i == 0 else None),
                                         mark=True)
                            t4, t4n = dT_tok[i]
                            t5 = self.op("tensor", "matmul", psO[pi][:, :],
                                         dTs[pi][:, :], vxi,
                                         start=True, stop=(i == 0),
                                         wait=(t4, tkO.get(pi)),
                                         mark=(i == 0))
                            if i > 0:
                                t5 = self.op("tensor", "matmul",
                                             psO[pi][:, :],
                                             dNs[pi][:, :], vxv[:, i - 1, :],
                                             start=False, stop=(i == 1),
                                             wait=(t4n,), mark=(i == 1))
                            for j in (i - 2, i - 3):
                                if j >= 0:
                                    t5 = self.op(
                                        "tensor", "matmul", psO[pi][:, :],
                                        qs[:, ib],
                                        S_sb[buf(j)][rb:rb + 64, :],
                                        start=False, stop=(j == min(
                                            x for x in (i - 2, i - 3)
                                            if x >= 0)),
                                        wait=(snap[j],), mark=True)
                            snap[i] = self.op("scalar", "copy",
                                              S_sb[buf(i)][rb:rb + 64, :],
                                              psS[pi][:, :], wait=(t6, t5))
                            tkO[pi] = self.op("scalar", "copy",
                                              orv[:, i, :], psO[pi][:, :],
                                              wait=(t5, b_on_last,
                                                    (b_abs if i == 0
                                                     else None)))
                            t_av[pi] = t5
                        last_snap = {0: snap[TB - 2], 1: snap[TB - 1]}
                        a_or_last = tkO[(TB - 1) % 2]
                        a_or_prev = tkO[TB % 2]
                        pe_head[pv] = t5
                        if pend_b is not None:
                            bret = emit_b(*pend_b)
                            if prev_c is not None:
                                emit_c(*prev_c)
                            prev_c = bret
                        pend_b = (kh, s, h, pv, l_s,
                                  a_or_last, a_or_prev)
                        kh += 1
                    pe_group[pq] = t5
                    kp += 1
            if pend_b is not None:
                bret = emit_b(*pend_b)
                if prev_c is not None:
                    emit_c(*prev_c)
                emit_c(*bret)

    def p_oproj(self):
        nc = self.nc
        NTB = self.NTB
        with self.phase() as ctx:
            al = ctx.enter_context
            ow = al(nc.sbuf_tensor("op_ow", [128, 16 * D], BF16))
            owv = ow[:, :].rearrange("p (c f) -> p c f", c=16)
            ogb = [al(nc.sbuf_tensor(f"op_og{i}", [128, 16 * 128], BF16))
                   for i in range(2)]
            s0t = [al(nc.sbuf_tensor(f"op_s0{i}", [128, D], F32))
                   for i in range(2)]
            pb = [al(nc.psum_tensor(f"op_p{i}", [128, 512], F32))
                  for i in range(2)]
            s_w = self.psem(0, swdge=True)
            s_og = [self.psem(1), self.psem(2)]
            s_out = [self.psem(3), self.psem(4)]
            ldw = self.dma("gpsimd", owv,
                           self.inp["L0_attn_o_w"].rearrange(
                               "(c p) f -> p c f", p=128), s_w)
            ogr = self.ogf.rearrange("(c p) t -> p c t", p=128)

            def load_og(tb, wait=()):
                return self.dma("sync",
                                ogb[tb % 2][:, :].rearrange(
                                    "p (c t) -> p c t", c=16),
                                ogr[:, :, tb * 128:(tb + 1) * 128],
                                s_og[tb % 2], wait=wait)

            og_tok = {0: load_og(0), 1: load_og(1)}
            ev_tok = {}
            pe_done = {}
            for tb in range(NTB):
                for fc in range(2):
                    bank = pb[fc]
                    for c in range(16):
                        pw = []
                        if c == 0:
                            if fc == 0:
                                pw.append(og_tok[tb])
                                if tb == 0:
                                    pw.append(ldw)
                            prev = ev_tok.get((tb - 1, fc))
                            if prev:
                                pw.append(prev)
                        self.op("tensor", "matmul", bank[:, :],
                                ogb[tb % 2][:, :].rearrange(
                                    "p (c t) -> p c t", c=16)[:, c, :],
                                owv[:, c, fc * 512:(fc + 1) * 512],
                                start=(c == 0), stop=False, wait=tuple(pw))
                    t_b = self.op("tensor", "matmul", bank[:, :],
                                  self.c_onesB[0:1, :],
                                  self.c_ob0[0:1, fc * 512:(fc + 1) * 512],
                                  start=False, stop=True,
                                  mark=True)
                    ev_tok[(tb, fc)] = self.op(
                        "vector", "tensor_copy",
                        s0t[tb % 2][:, fc * 512:(fc + 1) * 512], bank[:, :],
                        wait=(t_b, (s_out[tb % 2], s_out[tb % 2].n))
                        if fc == 0 else (t_b,))
                    if fc == 1:
                        pe_done[tb] = t_b
                if tb + 2 < NTB:
                    og_tok[tb + 2] = load_og(tb + 2, wait=(pe_done[tb],))
                self.dma("sync", self.s0[tb * 128:(tb + 1) * 128, :],
                         s0t[tb % 2][:, :], s_out[tb % 2],
                         wait=(ev_tok[(tb, 1)],))

    def p_oproj_resid0(self):
        nc = self.nc
        NTB = self.NTB
        with self.phase() as ctx:
            al = ctx.enter_context
            ow = al(nc.sbuf_tensor("op_ow", [128, 16 * D], BF16))
            owv = ow[:, :].rearrange("p (c f) -> p c f", c=16)
            ogb = [al(nc.sbuf_tensor(f"op_og{i}", [128, 16 * 128], BF16))
                   for i in range(2)]
            s0t = [al(nc.sbuf_tensor(f"op_s0{i}", [128, D], F32))
                   for i in range(2)]
            xg = [al(nc.sbuf_tensor(f"op_xg{i}", [128, 2 * D], F32))
                  for i in range(2)]
            x1t = [al(nc.sbuf_tensor(f"op_x1{i}", [128, D], F32))
                   for i in range(2)]
            xn_tm = [al(nc.sbuf_tensor(f"op_xn{i}", [128, D], BF16))
                     for i in range(2)]
            sq = [al(nc.sbuf_tensor(f"op_sq{i}", [128, D], F32))
                  for i in range(2)]
            st = al(nc.sbuf_tensor("op_st", [128, 16], F32))
            pb = [al(nc.psum_tensor(f"op_p{i}", [128, 512], F32))
                  for i in range(4)]
            pt = al(nc.psum_tensor("op_pt", [128, 512], BF16))
            s_w = self.psem(0, swdge=True)
            s_og = [self.psem(1), self.psem(2)]
            s_xg = [self.psem(3), self.psem(4)]
            s_x1 = self.psem(5)
            owr = self.inp["L0_attn_o_w"].rearrange("(c p) f -> p c f",
                                                    p=128)
            ldw = self.dma("gpsimd", owv[:, 0:8, :], owr[:, 0:8, :], s_w)
            ldw_b = self.dma("gpsimd", owv[:, 8:16, :], owr[:, 8:16, :],
                             s_w)
            ogr = self.ogf.rearrange("(c p) t -> p c t", p=128)
            x_ap = self.inp["x"]

            def load_og(tb, wait=()):
                return self.dma("sync",
                                ogb[tb % 2][:, :].rearrange(
                                    "p (c t) -> p c t", c=16),
                                ogr[:, :, tb * 128:(tb + 1) * 128],
                                s_og[tb % 2], wait=wait)

            def load_xg(tb, wait=()):
                par = tb % 2
                self.dma("scalar", xg[par][:, 0:D],
                         x_ap[tb * 128:(tb + 1) * 128, :], s_xg[par],
                         wait=wait)
                return self.dma("scalar", xg[par][:, D:2 * D],
                                self.g20[tb * 128:(tb + 1) * 128, :],
                                s_xg[par])

            og_tok = {0: load_og(0), 1: load_og(1)}
            xg_tok = {0: load_xg(0), 1: load_xg(1)}
            ev_tok = {}
            pe_done = {}
            mu_tok = {}
            sq_w = {}
            x1_free = {}
            xn_free = {}
            ptst = {"free": None}

            def emit_ln(tb):
                par = tb % 2
                c0 = par * 8
                lxg = xg_tok[tb]
                t_mu = self.op("vector", "tensor_mul", sq[par][:, :],
                               s0t[par][:, :], xg[par][:, D:2 * D],
                               wait=(ev_tok[(tb, 1)], lxg))
                mu_tok[par] = t_mu
                t_x1 = self.op("vector", "tensor_add", x1t[par][:, :],
                               xg[par][:, 0:D], sq[par][:, :],
                               wait=x1_free.get(par, ()))
                xdma1 = self.dma("sync",
                                 self.x1s[tb * 128:(tb + 1) * 128, :],
                                 x1t[par][:, :], s_x1, wait=(t_x1,))
                self.op("vector", "reduce_sum", st[:, c0:c0 + 1],
                        x1t[par][:, :], axis=AX)
                self.op("vector", "tensor_scalar_mul", st[:, c0 + 1:c0 + 2],
                        st[:, c0:c0 + 1], -1.0 / D,
                        wait=(sq_w.get(par),))
                t_nm = self.op("vector", "tensor_scalar_mul",
                               st[:, c0 + 2:c0 + 3], st[:, c0:c0 + 1],
                               1.0 / D, mark=True)
                t_sq = self.op("scalar", "activation", sq[par][:, :],
                               x1t[par][:, :], AF.Square,
                               bias=st[:, c0 + 1:c0 + 2], scale=1.0,
                               wait=(t_nm,))
                sq_w[par] = t_sq
                t_ss = self.op("vector", "reduce_sum", st[:, c0 + 3:c0 + 4],
                               sq[par][:, :], axis=AX, wait=(t_sq,))
                t_sd = self.op("scalar", "activation", st[:, c0 + 4:c0 + 5],
                               st[:, c0 + 3:c0 + 4], AF.Sqrt,
                               bias=self.c_eps[:, 0:1],
                               scale=1.0 / D, wait=(t_ss,))
                t_rc = self.op("vector", "reciprocal", st[:, c0 + 5:c0 + 6],
                               st[:, c0 + 4:c0 + 5], wait=(t_sd,),
                               mark=True)
                t_xn = self.op("vector", "tensor_scalar", xn_tm[par][:, :],
                               x1t[par][:, :], st[:, c0 + 2:c0 + 3],
                               st[:, c0 + 5:c0 + 6],
                               ALU.subtract, ALU.mult,
                               wait=xn_free.get(par, ()), mark=True)
                x1_free[par] = (t_sq, xdma1)
                xdma2 = self.dma("sync",
                                 self.xn1tm[tb * 128:(tb + 1) * 128, :],
                                 xn_tm[par][:, :], s_x1, wait=(t_xn,))
                for half in range(2):
                    toks = []
                    for j in range(4):
                        c = half * 4 + j
                        pw = ()
                        if j == 0:
                            pw = ((t_xn,) if half == 0 else ()) + \
                                 ((ptst["free"],) if ptst["free"] else ())
                        tt = self.op(
                            "tensor", "transpose",
                            pt[:, j * 128:(j + 1) * 128],
                            xn_tm[par][:, c * 128:(c + 1) * 128],
                            self.c_identB[:, :],
                            wait=pw, mark=(j == 3))
                        if tt:
                            toks.append(tt)
                    ptst["free"] = self.op(
                        "vector", "tensor_copy",
                        self.xnv[:, half * 4:(half + 1) * 4,
                                 tb * 128:(tb + 1) * 128],
                        pt[:, :].rearrange("p (j t) -> p j t", j=4),
                        wait=tuple(toks), mark=True)
                xn_free[par] = (tt, xdma2)
                if tb + 2 < NTB:
                    xg_tok[tb + 2] = load_xg(tb + 2, wait=(t_x1,))

            pend_ln = None
            for tb in range(NTB):
                par = tb % 2
                for fc in range(2):
                    bank = pb[par * 2 + fc]
                    for c in range(16):
                        pw = []
                        if c == 0:
                            if fc == 0:
                                pw.append(og_tok[tb])
                                if tb == 0:
                                    pw.append(ldw)
                            prev = ev_tok.get((tb - 2, fc))
                            if prev:
                                pw.append(prev)
                        elif c == 8 and tb == 0 and fc == 0:
                            pw.append(ldw_b)
                        self.op("tensor", "matmul", bank[:, :],
                                ogb[tb % 2][:, :].rearrange(
                                    "p (c t) -> p c t", c=16)[:, c, :],
                                owv[:, c, fc * 512:(fc + 1) * 512],
                                start=(c == 0), stop=False, wait=tuple(pw))
                    t_b = self.op("tensor", "matmul", bank[:, :],
                                  self.c_onesB[0:1, :],
                                  self.c_ob0[0:1, fc * 512:(fc + 1) * 512],
                                  start=False, stop=True,
                                  mark=True)
                    ev_w = [t_b]
                    if fc == 0 and mu_tok.get(par):
                        ev_w.append(mu_tok[par])
                    ev_tok[(tb, fc)] = self.op(
                        "scalar", "copy",
                        s0t[par][:, fc * 512:(fc + 1) * 512], bank[:, :],
                        wait=tuple(ev_w))
                    if fc == 1:
                        pe_done[tb] = t_b
                if tb + 2 < NTB:
                    og_tok[tb + 2] = load_og(tb + 2, wait=(pe_done[tb],))
                if pend_ln is not None:
                    emit_ln(pend_ln)
                pend_ln = tb
            emit_ln(NTB - 1)

    def p_qkv1(self):
        nc = self.nc
        NT, T, NTB = self.NT, self.T, self.NTB
        NC = NT // 512
        with self.phase() as ctx:
            al = ctx.enter_context
            wq = al(nc.sbuf_tensor("q1_wq", [128, 8 * 3 * D], BF16))
            wqv = wq[:, :].rearrange("p (c f) -> p c f", c=8)
            wraw = al(nc.sbuf_tensor("q1_wraw", [128, 4 * D], BF16))
            tabc = al(nc.sbuf_tensor("q1_tab", [128, 2 * T], F32))
            qtmp = al(nc.sbuf_tensor("q1_tmp", [128, NT], BF16))
            t1 = al(nc.sbuf_tensor("q1_t1", [128, 512], F32))
            t2 = al(nc.sbuf_tensor("q1_t2", [128, 512], F32))
            qout = al(nc.sbuf_tensor("q1_out", [128, NT], BF16))
            vout = [al(nc.sbuf_tensor(f"q1_v{i}", [128, D], BF16))
                    for i in range(2)]
            ktm1 = al(nc.sbuf_tensor("q1_ktm", [128, NT], BF16))
            psT = al(nc.psum_tensor("q1_psT", [128, 512], BF16))
            pq = [al(nc.psum_tensor(f"q1_pq{i}", [128, 512], F32))
                  for i in range(2)]
            pr = [al(nc.psum_tensor(f"q1_pr{i}", [128, 512], F32))
                  for i in range(2)]
            s_w = self.psem(0, swdge=True)
            s_tab = self.psem(1)
            s_out = self.psem(2)
            s_vout = [self.psem(4), self.psem(5)]
            s_tm1 = self.psem(6)
            self.dma("sync", tabc[:, 0:T], self.inp["cos1"], s_tab)
            ltab = self.dma("sync", tabc[:, T:2 * T], self.inp["sin1"], s_tab)
            wr = self.inp["L1_attn_W_qkv"].rearrange("(r p) d -> p r d", p=128)
            tp_free = None
            for grp6 in range(6):
                ldw = self.dma("gpsimd",
                               wraw[:, :].rearrange("p (r d) -> p r d", r=4),
                               wr[:, grp6 * 4:(grp6 + 1) * 4, :], s_w,
                               wait=(tp_free,) if grp6 else ())
                for c in range(8):
                    for j in range(4):
                        r = j
                        pw = ()
                        if j == 0:
                            pw = (ldw,) if (c == 0) else ()
                            if tp_free is not None:
                                pw = pw + (tp_free,)
                        tt = self.op(
                            "tensor", "transpose",
                            psT[:, j * 128:(j + 1) * 128],
                            wraw[:, r * D + c * 128:
                                 r * D + (c + 1) * 128],
                            self.c_identB[:, :], wait=pw,
                            mark=(j == 3))
                    tp_free = self.op(
                        "vector", "tensor_copy",
                        wqv[:, c, (grp6 * 4) * 128:
                            (grp6 * 4 + 4) * 128],
                        psT[:, :], wait=(tt,))
            t_wq = tp_free
            evac_toks = {}
            rot_toks = {}
            add_toks = {}
            out_tok = None
            tm1_tr = None
            tm1_ev = None
            last_add = None

            def flush1(fbgc, nc_, t_evc, tcolc):
                nonlocal last_add
                rbank = pr[nc_ % 2]
                rw = [t_evc]
                prev_t2 = add_toks.get((fbgc, nc_ - 2) if nc_ >= 2
                                       else (fbgc - 1, NC - 2 + nc_))
                if prev_t2:
                    rw.append(prev_t2)
                t_rot = self.op("tensor", "matmul", rbank[:, :],
                                self.c_R1T[:, :],
                                qtmp[:, nc_ * 512:(nc_ + 1) * 512],
                                start=True, stop=True,
                                wait=tuple(rw), mark=True)
                rot_toks[(fbgc, nc_)] = t_rot
                w1 = [t_evc, ltab if fbgc == 0 and nc_ == 0 else None]
                if out_tok is not None and nc_ == 0:
                    w1.append(out_tok)
                    if tm1_tr is not None:
                        w1.append(tm1_tr)
                self.op("vector", "tensor_mul", t1[:, :],
                        qtmp[:, nc_ * 512:(nc_ + 1) * 512],
                        tabc[:, tcolc:tcolc + 512],
                        wait=tuple(t for t in w1 if t))
                self.op("vector", "tensor_mul", t2[:, :], rbank[:, :],
                        tabc[:, T + tcolc:T + tcolc + 512],
                        wait=(t_rot,))
                last_add = self.op("vector", "tensor_add",
                                   qout[:, nc_ * 512:(nc_ + 1) * 512],
                                   t1[:, :], t2[:, :], mark=True)
                add_toks[(fbgc, nc_)] = last_add

            for part, out_ap in ((0, self.qr1), (1, self.kr1)):
                for fb in range(8):
                    fbg = part * 8 + fb
                    pend = None
                    for n in range(NC):
                        sseq = (n * 512) // T
                        tcol = n * 512 - sseq * T
                        bank = pq[n % 2]
                        for c in range(8):
                            pw = []
                            if c == 0:
                                if fbg == 0 and n < 2:
                                    pw.append(t_wq)
                                prev = evac_toks.get((fbg, n - 2) if n >= 2
                                                     else (fbg - 1, NC - 2 + n))
                                if prev:
                                    pw.append(prev)
                            self.op("tensor", "matmul", bank[:, :],
                                    wqv[:, c, fbg * 128:(fbg + 1) * 128],
                                    self.xnv[:, c, n * 512:(n + 1) * 512],
                                    start=(c == 0), stop=(c == 7),
                                    wait=tuple(pw), mark=(c == 7))
                        t_mm = (self.ch["tensor"], self.ch["tensor"].n)
                        ev_w = [t_mm]
                        prev_rot = rot_toks.get((fbg - 1, n))
                        if prev_rot:
                            ev_w.append(prev_rot)
                        prev_t1 = add_toks.get((fbg - 1, n))
                        if prev_t1:
                            ev_w.append(prev_t1)
                        t_ev = self.op("scalar", "add",
                                       qtmp[:, n * 512:(n + 1) * 512],
                                       bank[:, :],
                                       self.c_qkb1[:, fbg:fbg + 1],
                                       wait=tuple(ev_w), mark=True)
                        evac_toks[(fbg, n)] = t_ev
                        if pend is not None:
                            flush1(*pend)
                        pend = (fbg, n, t_ev, tcol)
                    flush1(*pend)
                    out_tok = self.dma("sync",
                                       out_ap[fb * 128:(fb + 1) * 128, :],
                                       qout[:, :], s_out, wait=(last_add,))
                    if part == 1:
                        for grp in range(NTB // 4):
                            for j in range(4):
                                tbk = grp * 4 + j
                                pw = ()
                                if j == 0:
                                    pw = ((last_add,) if grp == 0 else ()) + \
                                         ((tm1_ev,) if tm1_tr
                                          else (t_wq,))
                                tm1_tr = self.op(
                                    "tensor", "transpose",
                                    psT[:, j * 128:(j + 1) * 128],
                                    qout[:, tbk * 128:(tbk + 1) * 128],
                                    self.c_identB[:, :], wait=pw,
                                    mark=(j == 3))
                            tm1_ev = self.op(
                                "scalar", "copy",
                                ktm1[:, grp * 512:(grp + 1) * 512],
                                psT[:, :],
                                wait=(tm1_tr,) +
                                     (((s_tm1, s_tm1.n),) if grp == 0
                                      else ()))
                        self.dma("sync",
                                 self.kr1tm.rearrange("(i p) f -> p i f",
                                                      p=128)
                                 [:, :, fb * 128:(fb + 1) * 128],
                                 ktm1[:, :].rearrange("p (i f) -> p i f",
                                                      f=128),
                                 s_tm1, wait=(tm1_ev,))
            ev_tok = {}
            vo_dma = {}
            k = 0
            for tb in range(NTB):
                obuf = vout[tb % 2]
                for fc in range(2):
                    bank = pq[k % 2]
                    for c in range(8):
                        pw = []
                        if c == 0:
                            prev = ev_tok.get(k - 2)
                            if prev:
                                pw.append(prev)
                            elif k < 2:
                                pw.append(evac_toks[(15, NC - 2 + k)])
                        self.op("tensor", "matmul", bank[:, :],
                                self.xnv[:, c, tb * 128:(tb + 1) * 128],
                                wqv[:, c, 2 * D + fc * 512:
                                    2 * D + (fc + 1) * 512],
                                start=(c == 0), stop=False, wait=tuple(pw))
                    self.op("tensor", "matmul", bank[:, :],
                            self.c_onesB[0:1, :],
                            self.c_vb1[0:1, fc * 512:(fc + 1) * 512],
                            start=False, stop=True, mark=True)
                    t_mm = (self.ch["tensor"], self.ch["tensor"].n)
                    ew = [t_mm]
                    if fc == 0 and tb >= 2:
                        ew.append((s_vout[tb % 2], s_vout[tb % 2].n))
                    ev_tok[k] = self.op("vector", "tensor_copy",
                                        obuf[:, fc * 512:(fc + 1) * 512],
                                        bank[:, :], wait=tuple(ew))
                    k += 1
                vo_dma[tb] = self.dma("sync",
                                      self.v1s[tb * 128:(tb + 1) * 128, :],
                                      obuf[:, :], s_vout[tb % 2],
                                      wait=(ev_tok[k - 1],))

    def p_attn1(self):
        nc = self.nc
        B, T, TB = self.B, self.T, self.TB
        with self.phase() as ctx:
            al = ctx.enter_context
            qk = [al(nc.sbuf_tensor(f"a1_qk{i}", [128, 2 * T], BF16))
                  for i in range(2)]
            vx = [al(nc.sbuf_tensor(f"a1_vx{i}", [128, TB * 128], BF16))
                  for i in range(2)]
            kt1 = [al(nc.sbuf_tensor(f"a1_kt{i}", [128, TB * 128], BF16))
                   for i in range(2)]
            sh1 = [al(nc.sbuf_tensor(f"a1_sh{i}", [128, TB * 64], F32))
                   for i in range(2)]
            S_sb = [al(nc.sbuf_tensor(f"a1_S{i}", [128, 64], BF16))
                    for i in range(4)]
            dTs = [al(nc.sbuf_tensor(f"a1_dT{i}", [128, 128], BF16))
                   for i in range(2)]
            dNs = [al(nc.sbuf_tensor(f"a1_dN{i}", [128, 128], BF16))
                   for i in range(2)]
            psS = [al(nc.psum_tensor(f"a1_psS{i}", [64, 64], F32))
                   for i in range(2)]
            psD = [al(nc.psum_tensor(f"a1_psD{i}", [128, 256], F32))
                   for i in range(2)]
            psO = al(nc.psum_tensor("a1_psO", [128, 256], F32))
            s_qk = [self.psem(0), self.psem(1)]
            s_vx = [self.psem(2), self.psem(3)]
            s_sh = [self.psem(4), self.psem(5)]
            v1r = self.v1s.rearrange("(b i p) c -> b p i c", i=TB, p=128)
            kt1r = self.kr1tm.rearrange("(b i p) c -> b p i c", i=TB, p=128)
            s1r = self.s1.rearrange("(b i p) c -> b p i c", i=TB, p=128)
            tkD = {}
            tkO = {}
            last_snap = {}
            t_av = {}
            pe_group = {}
            pe_pair = {}
            kp = 0
            for s in range(B):
                for hp in range(8):
                    pq_ = kp % 2
                    self.dma("sync", qk[pq_][:, 0:T],
                             self.qr1[hp * 128:(hp + 1) * 128,
                                      s * T:(s + 1) * T], s_qk[pq_],
                             wait=(pe_group.get(pq_),))
                    l_k = self.dma("sync", qk[pq_][:, T:2 * T],
                                   self.kr1[hp * 128:(hp + 1) * 128,
                                            s * T:(s + 1) * T], s_qk[pq_])
                    l_v = self.dma("sync",
                                   vx[pq_][:, :].rearrange(
                                       "p (i c) -> p i c", c=128),
                                   v1r[s][:, :, 128 * hp:128 * hp + 128],
                                   s_vx[pq_], wait=(pe_pair.get(pq_),))
                    l_kt = self.dma("sync",
                                    kt1[pq_][:, :].rearrange(
                                        "p (i c) -> p i c", c=128),
                                    kt1r[s][:, :, 128 * hp:128 * hp + 128],
                                    s_vx[pq_])
                    for hh in range(2):
                        h = 2 * hp + hh
                        rb = 64 * hh
                        qs = qk[pq_][rb:rb + 64, 0:T]
                        ks = qk[pq_][rb:rb + 64, T:2 * T]
                        ktv = kt1[pq_][:, :].rearrange(
                            "p (i c) -> p i c", c=128)
                        ev4 = None
                        snap = {-2: last_snap.get(0),
                                -1: last_snap.get(1)}
                        buf = lambda j: 2 * (j % 2) + ((j // 2) % 2)
                        vxv1 = vx[pq_][:, :].rearrange(
                            "p (i c) -> p i c", c=128)
                        dT_tok = {}

                        def emit_scores1(j, first):
                            pj = j % 2
                            jb = slice(j * 128, (j + 1) * 128)
                            jp = slice((j - 1) * 128, j * 128)
                            t3 = self.op("tensor", "matmul",
                                         psD[pj][:, 0:128],
                                         ks[:, jb], qs[:, jb],
                                         start=True, stop=True,
                                         wait=(tkD.get(pj),
                                               l_k if first else None),
                                         mark=(j == 0))
                            if j > 0:
                                t3 = self.op("tensor", "matmul",
                                             psD[pj][:, 128:256],
                                             ks[:, jp], qs[:, jb],
                                             start=True, stop=True,
                                             mark=True)
                            t4 = self.op("vector", "tensor_mul",
                                         dTs[pj][:, :], psD[pj][:, 0:128],
                                         self.c_triu[:, :],
                                         wait=(t3, t_av.get(pj)))
                            t4n = None
                            if j > 0:
                                t4n = self.op("vector", "tensor_copy",
                                              dNs[pj][:, :],
                                              psD[pj][:, 128:256],
                                              wait=(t3,))
                            tkD[pj] = (t4, t4n) if j > 0 else (t4,)
                            dT_tok[j] = (t4, t4n)

                        emit_scores1(0, hh == 0)
                        for i in range(TB):
                            ib = slice(i * 128, (i + 1) * 128)
                            pi = i % 2
                            vxi = vxv1[:, i, rb:rb + 64]
                            oseg = psO[:, (i % 4) * 64:(i % 4) * 64 + 64]
                            if i + 1 < TB:
                                emit_scores1(i + 1, False)
                            t6 = self.op("tensor", "matmul", psS[pi][:, :],
                                         ktv[:, i, rb:rb + 64], vxi,
                                         start=(i < 2), stop=(i >= TB - 2),
                                         skip_group_check=True,
                                         wait=(snap.get(i - 2),
                                               l_kt if (i == 0 and hh == 0)
                                               else None),
                                         mark=True)
                            t4, t4n = dT_tok[i]
                            t5 = self.op("tensor", "matmul", oseg,
                                         dTs[pi][:, :], vxi,
                                         start=True, stop=(i == 0),
                                         wait=(t4,
                                               tkO.get(0)
                                               if i % 4 == 0 else None),
                                         mark=(i == 0))
                            if i > 0:
                                t5 = self.op("tensor", "matmul", oseg,
                                             dNs[pi][:, :],
                                             vxv1[:, i - 1, rb:rb + 64],
                                             start=False, stop=(i == 1),
                                             wait=(t4n,), mark=(i == 1))
                            for j in (i - 2, i - 3):
                                if j >= 0:
                                    t5 = self.op(
                                        "tensor", "matmul", oseg,
                                        qs[:, ib],
                                        S_sb[buf(j)][rb:rb + 64, :],
                                        start=False, stop=(j == min(
                                            x for x in (i - 2, i - 3)
                                            if x >= 0)),
                                        wait=(snap[j],), mark=True)
                            snap[i] = self.op("scalar", "copy",
                                              S_sb[buf(i)][rb:rb + 64, :],
                                              psS[pi][:, :], wait=(t6, t5))
                            t_av[pi] = t5
                            if i % 4 == 3:
                                ev4 = self.op(
                                    "vector", "tensor_copy",
                                    sh1[hh][:, (i - 3) * 64:(i + 1) * 64],
                                    psO[:, :],
                                    wait=(t5,) +
                                         (((s_sh[hh], s_sh[hh].n),)
                                          if i == 3 else ()))
                                tkO[0] = ev4
                        last_snap = {0: snap[TB - 2], 1: snap[TB - 1]}
                        self.dma("sync",
                                 s1r[s][:, :, 64 * h:64 * h + 64],
                                 sh1[hh][:, :].rearrange(
                                     "p (i c) -> p i c", c=64),
                                 s_sh[hh], wait=(ev4,))
                    pe_pair[pq_] = t5
                    pe_group[pq_] = t5
                    kp += 1

    def p_router(self):
        nc = self.nc
        NT, NTB = self.NT, self.NTB
        NC = NT // 512
        with self.phase() as ctx:
            al = ctx.enter_context
            wr_sb = al(nc.sbuf_tensor("rt_wr", [128, 8 * E], BF16))
            logb = al(nc.sbuf_tensor("rt_log", [8, NT], F32))
            lt = al(nc.sbuf_tensor("rt_lt", [128, E], F32))
            ct = al(nc.sbuf_tensor("rt_ct", [128, E], F32))
            ovt = al(nc.sbuf_tensor("rt_ovt", [128, E], F32))
            exb = al(nc.sbuf_tensor("rt_ex", [128, E], F32))
            stt = al(nc.sbuf_tensor("rt_st", [128, 4], F32))
            offs = al(nc.sbuf_tensor("rt_offs", [1, E], F32))
            xtile = [al(nc.sbuf_tensor(f"rt_xt{i}", [128, D], BF16))
                     for i in range(2)]
            psL = al(nc.psum_tensor("rt_psL", [8, 512], F32))
            psTr = al(nc.psum_tensor("rt_psTr", [128, E], F32))
            psC = al(nc.psum_tensor("rt_psC", [128, 2 * E], F32))
            s_w = self.psem(0, swdge=True)
            s_xt = [self.psem(1), self.psem(2)]
            s_sc = self.psem(3, swdge=True)
            ldw = self.dma("gpsimd",
                           wr_sb[:, :].rearrange("p (c e) -> p c e", c=8),
                           self.inp["L1_ffn_Wr"].rearrange(
                               "(c p) e -> p c e", p=128), s_w)
            t_off = self.op("vector", "memset", offs[:, :], 0.0)
            ev_prev = None
            for n in range(NC):
                for c in range(8):
                    self.op("tensor", "matmul", psL[:, :],
                            wr_sb[:, :].rearrange("p (c e) -> p c e",
                                                  c=8)[:, c, :],
                            self.xnv[:, c, n * 512:(n + 1) * 512],
                            start=(c == 0), stop=(c == 7),
                            wait=((ldw,) if (n == 0 and c == 0) else ()) +
                                 ((ev_prev,) if (c == 0 and ev_prev)
                                  else ()),
                            mark=(c == 7))
                t_mm = (self.ch["tensor"], self.ch["tensor"].n)
                ev_prev = self.op("vector", "tensor_scalar_add",
                                  logb[:, n * 512:(n + 1) * 512], psL[:, :],
                                  self.c_br[:, 0:1], wait=(t_mm,))
            CAP = float(self.CAP)
            tr_free = None
            tc_free = None
            sc_prev = None
            for tb in range(NTB):
                par = tb % 2
                lx = self.dma("sync", xtile[par][:, :],
                              self.xn1tm[tb * 128:(tb + 1) * 128, :],
                              s_xt[par], wait=(sc_prev,))
                t_tr = self.op("tensor", "transpose", psTr[:, :],
                               logb[:, tb * 128:(tb + 1) * 128],
                               self.c_identF[0:8, 0:8],
                               wait=(ev_prev if tb == 0 else None,
                                     tr_free), mark=True)
                tr_free = self.op("vector", "tensor_copy", lt[:, :],
                                  psTr[:, :], wait=(t_tr,))
                self.op("vector", "reduce_max", stt[:, 0:1], lt[:, :],
                        axis=AX)
                t_nm = self.op("vector", "tensor_scalar_mul", stt[:, 1:2],
                               stt[:, 0:1], -1.0)
                t_ex = self.op("scalar", "activation", exb[:, :], lt[:, :],
                               AF.Exp, bias=stt[:, 1:2], scale=1.0,
                               wait=(t_nm,))
                t_se = self.op("vector", "reduce_sum", stt[:, 2:3],
                               exb[:, :], axis=AX, wait=(t_ex,))
                self.op("vector", "reciprocal", self.c_mp[:, tb:tb + 1],
                        stt[:, 2:3])
                t_ind = self.op("vector", "tensor_single_scalar",
                                self.c_ind[:, tb * E:(tb + 1) * E], lt[:, :],
                                stt[:, 0:1], ALU.is_ge)
                indsl = self.c_ind[:, tb * E:(tb + 1) * E]
                self.op("tensor", "matmul", psC[:, 0:E],
                        self.c_triu[:, :], indsl,
                        start=True, stop=False, wait=(t_ind, tc_free))
                self.op("tensor", "matmul", psC[0:1, E:2 * E],
                        self.c_onesF[:, 0:1], indsl,
                        start=True, stop=True, wait=(t_off,))
                t_c2 = self.op("tensor", "matmul", psC[:, 0:E],
                               self.c_onesFr[0:1, :], offs[0:1, :],
                               start=False, stop=True, mark=True)
                tc_free = self.op("vector", "tensor_copy", ct[:, :],
                                  psC[:, 0:E], wait=(t_c2,))
                t_off = self.op("vector", "tensor_add", offs[0:1, :],
                                offs[0:1, :], psC[0:1, E:2 * E])
                self.op("vector", "tensor_single_scalar", ovt[:, :],
                        ct[:, :], CAP, ALU.is_gt)
                self.op("vector", "tensor_mul", ovt[:, :], ovt[:, :], indsl)
                self.op("vector", "reduce_sum", stt[:, 2:3], ovt[:, :],
                        axis=AX)
                self.op("vector", "tensor_scalar_mul", stt[:, 3:4],
                        stt[:, 2:3], 1.0e8)
                self.op("vector", "tensor_add", ct[:, :], ct[:, :],
                        self.c_eCm1[:, :])
                self.op("vector", "tensor_mul", ct[:, :], ct[:, :], indsl)
                self.op("vector", "reduce_sum", stt[:, 1:2], ct[:, :],
                        axis=AX)
                self.op("vector", "tensor_add", stt[:, 1:2], stt[:, 1:2],
                        stt[:, 3:4])
                t_sl = self.op("vector", "tensor_copy",
                               self.c_slots[:, tb:tb + 1], stt[:, 1:2])
                sc_prev = self.dma("gpsimd", self.xnsort,
                                   xtile[par][:, :], s_sc,
                                   wait=(lx, t_sl),
                                   indirect_out=self.c_slots[:, tb:tb + 1])

    def p_moe(self):
        nc = self.nc
        CAP = self.CAP
        NCH = CAP // 512
        with self.phase() as ctx:
            al = ctx.enter_context
            w1e = al(nc.sbuf_tensor("mo_w1", [128, 8 * HMOE], BF16))
            w1v = w1e[:, :].rearrange("p (c f) -> p c f", c=8)
            w2e = al(nc.sbuf_tensor("mo_w2", [128, 16 * D], BF16))
            w2v = w2e[:, :].rearrange("p (c f) -> p c f", c=16)
            xs = [al(nc.sbuf_tensor(f"mo_xs{i}", [128, 8 * 512], BF16))
                  for i in range(2)]
            h1 = al(nc.sbuf_tensor("mo_h", [128, 16 * 512], BF16))
            hv = h1[:, :].rearrange("p (f t) -> p f t", f=16)
            yev = [al(nc.sbuf_tensor(f"mo_ev{i}", [128, 512], F32))
                   for i in range(4)]
            pp = [al(nc.psum_tensor(f"mo_pp{i}", [128, 512], F32))
                  for i in range(3)]
            s_w1 = self.psem(0, swdge=True)
            s_w2 = self.psem(1, swdge=True)
            s_xs = [self.psem(2), self.psem(3)]
            s_y = [self.psem(4 + i) for i in range(4)]
            W1, W2 = self.inp["L1_ffn_W1"], self.inp["L1_ffn_W2"]

            def load_w1(e, wait=()):
                return self.dma("gpsimd", w1v,
                                W1[e].rearrange("(c p) f -> p c f", p=128),
                                s_w1, wait=wait)

            def load_w2(e, wait=()):
                return self.dma("gpsimd", w2v,
                                W2[e].rearrange("(c p) f -> p c f", p=128),
                                s_w2, wait=wait)

            def load_xs(g, wait=()):
                e, tcs = divmod(g, NCH)
                par = g % 2
                r0 = e * CAP + tcs * 512
                tok = None
                for c in range(8):
                    tok = self.dma("sync",
                                   xs[par][:, c * 512:(c + 1) * 512],
                                   self.xnsort[r0:r0 + 512,
                                               c * 128:(c + 1) * 128],
                                   s_xs[par],
                                   wait=wait if c == 0 else (),
                                   transpose=True)
                return tok

            w1tok = load_w1(0)
            w2tok = load_w2(0)
            xstok = {0: load_xs(0), 1: load_xs(1)}
            evac_tok = {}
            kq = 0
            ky = 0
            y_prev = None
            for e in range(E):
                for tcs in range(NCH):
                    g = e * NCH + tcs
                    par = g % 2
                    xsv = xs[par][:, :].rearrange("p (c t) -> p c t", c=8)
                    h_last = None
                    for fb in range(16):
                        bank = pp[kq % 3]
                        for c in range(8):
                            pw = []
                            if c == 0:
                                if fb == 0:
                                    pw.append(xstok[g])
                                    if tcs == 0:
                                        pw.append(w1tok)
                                        if e > 0:
                                            pw.append(y_prev)
                                prev = evac_tok.get(kq % 3)
                                if prev:
                                    pw.append(prev)
                            self.op("tensor", "matmul", bank[:, :],
                                    w1v[:, c, fb * 128:(fb + 1) * 128],
                                    xsv[:, c, :],
                                    start=(c == 0), stop=(c == 7),
                                    wait=tuple(pw), mark=(c == 7))
                        h_last = (self.ch["tensor"], self.ch["tensor"].n)
                        evac_tok[kq % 3] = self.op(
                            "scalar", "activation", hv[:, fb, :], bank[:, :],
                            AF.Relu,
                            bias=self.c_b1moe[:, e * 16 + fb:e * 16 + fb + 1],
                            scale=1.0, wait=(h_last,))
                        kq += 1
                    relu_last = evac_tok[(kq - 1) % 3]
                    if g + 2 < E * NCH:
                        xstok[g + 2] = load_xs(g + 2, wait=(h_last,))
                    if tcs == NCH - 1 and e + 1 < E:
                        w1tok = load_w1(e + 1, wait=(h_last,))
                    for tt in range(4):
                        for dc in range(2):
                            bank = pp[kq % 3]
                            for j in range(16):
                                pw = []
                                if j == 0:
                                    if tt == 0 and dc == 0:
                                        if tcs == 0:
                                            pw.append(w2tok)
                                        pw.append(relu_last)
                                    prev = evac_tok.get(kq % 3)
                                    if prev:
                                        pw.append(prev)
                                y_last = self.op(
                                    "tensor", "matmul", bank[:, :],
                                    hv[:, j, tt * 128:(tt + 1) * 128],
                                    w2v[:, j, dc * 512:(dc + 1) * 512],
                                    start=(j == 0), stop=(j == 15),
                                    wait=tuple(pw), mark=(j == 15))
                            bb = ky % 4
                            t_ev = self.op(
                                "vector", "tensor_copy", yev[bb][:, :],
                                bank[:, :],
                                wait=(y_last, (s_y[bb], s_y[bb].n)))
                            evac_tok[kq % 3] = t_ev
                            kq += 1
                            ky += 1
                            r0 = e * CAP + tcs * 512 + tt * 128
                            self.dma("sync",
                                     self.ysort[r0:r0 + 128,
                                                dc * 512:(dc + 1) * 512],
                                     yev[bb][:, :], s_y[bb], wait=(t_ev,))
                if e + 1 < E:
                    w2tok = load_w2(e + 1, wait=(y_last,))
                y_prev = y_last

    def p_resid1(self):
        nc = self.nc
        NTB = self.NTB
        GRP = 2
        with self.phase() as ctx:
            al = ctx.enter_context
            xb = [al(nc.sbuf_tensor(f"r1_x{i}", [128, GRP * D], F32))
                  for i in range(2)]
            sb = [al(nc.sbuf_tensor(f"r1_s{i}", [128, GRP * D], F32))
                  for i in range(2)]
            mb = [al(nc.sbuf_tensor(f"r1_m{i}", [128, D], F32))
                  for i in range(4)]
            pre = [al(nc.sbuf_tensor(f"r1_pre{i}", [128, D], F32))
                   for i in range(2)]
            indb = [al(nc.sbuf_tensor(f"r1_indb{i}", [128, E], BF16))
                    for i in range(2)]
            indTall = al(nc.sbuf_tensor("r1_indT", [E, self.NT], BF16))
            psTi = al(nc.psum_tensor("r1_psTi", [E, 128], BF16))
            gat = [al(nc.sbuf_tensor(f"r1_gat{i}", [128, D], F32))
                   for i in range(2)]
            outt = [al(nc.sbuf_tensor(f"r1_out{i}", [128, D], F32))
                    for i in range(2)]
            pb2 = [al(nc.psum_tensor(f"r1_pb{i}", [128, 512], F32))
                   for i in range(2)]
            s_x = [self.psem(0), self.psem(1)]
            s_s = [self.psem(2), self.psem(3)]
            s_m = [self.psem(4 + i, swdge=True) for i in range(4)]
            s_y = self.psem(8)
            x_r = self.x1s.rearrange("(g a p) d -> g p a d", a=GRP, p=128)
            s_r = self.s1.rearrange("(g a p) d -> g p a d", a=GRP, p=128)
            free = [(), ()]
            t_sil = {}
            t_iT = None
            t_icp = None
            t_pre = {}
            t_mb_read = {}
            t_m_prev = {}
            out_dma = {}
            ind_tok = {}
            ldm_tok = {}
            ldxs = {}
            t_out_tok = {}

            def prefetch(tb):
                q = tb % 4
                t_ms = self.op("vector", "memset", mb[q][:, :], 0.0,
                               wait=(t_mb_read.get(q),))
                ldm_tok[tb] = self.dma("gpsimd", mb[q][:, :],
                                       self.ysort, s_m[q],
                                       wait=(t_ms,),
                                       indirect_in=self.c_slots[:, tb:tb + 1])

            for tb in range(NTB):
                ipar = tb % 2
                t_ib = self.op("vector", "tensor_copy", indb[ipar][:, :],
                               self.c_ind[:, tb * E:(tb + 1) * E],
                               wait=(ind_tok.get(tb - 2),))
                t_tr = self.op("tensor", "transpose", psTi[:, :],
                               indb[ipar][:, :], self.c_identB[:, :],
                               wait=(t_ib, t_icp), mark=True)
                t_icp = self.op("vector", "tensor_copy",
                                indTall[:, tb * 128:(tb + 1) * 128],
                                psTi[:, :], wait=(t_tr,))
                ind_tok[tb] = t_icp
                if tb < 4:
                    prefetch(tb)

            def emit_out(tb):
                par = (tb // GRP) % 2
                opar = tb % 2
                a = tb % GRP
                sl = slice(a * D, (a + 1) * D)
                t_m = self.op("vector", "tensor_mul", outt[opar][:, :],
                              sb[par][:, sl], gat[opar][:, :],
                              wait=(t_sil[tb],
                                    ldxs[tb // GRP][1] if a == 0 else None,
                                    out_dma.get(opar)))
                t_m_prev[opar] = t_m
                t_out = self.op("vector", "tensor_add", outt[opar][:, :],
                                xb[par][:, sl], outt[opar][:, :],
                                wait=(ldxs[tb // GRP][0],) if a == 0 else ())
                t_out_tok[tb] = t_out
                out_dma[opar] = self.dma(
                    "sync", self.y_out[tb * 128:(tb + 1) * 128, :],
                    outt[opar][:, :], s_y, wait=(t_out,))

            for g in range(NTB // GRP):
                par = g % 2
                ldx = self.dma("sync",
                               xb[par][:, :].rearrange("p (a d) -> p a d",
                                                       a=GRP),
                               x_r[g], s_x[par], wait=free[par])
                lds = self.dma("sync",
                               sb[par][:, :].rearrange("p (a d) -> p a d",
                                                       a=GRP),
                               s_r[g], s_s[par])
                ldxs[g] = (ldx, lds)
                for a in range(GRP):
                    tb = g * GRP + a
                    q = tb % 4
                    opar = tb % 2
                    for dc in range(2):
                        t_b2 = self.op(
                            "tensor", "matmul", pb2[dc][:, :],
                            indTall[:, tb * 128:(tb + 1) * 128],
                            self.c_b2moe[:, dc * 512:(dc + 1) * 512],
                            start=True, stop=True,
                            wait=(t_pre.get(dc), ind_tok[tb]), mark=True)
                        t_pre[dc] = self.op(
                            "vector", "tensor_add",
                            pre[opar][:, dc * 512:(dc + 1) * 512],
                            mb[q][:, dc * 512:(dc + 1) * 512],
                            pb2[dc][:, :],
                            wait=(t_b2, t_sil.get(tb - 2), ldm_tok[tb]))
                    t_mb_read[q] = t_pre[1]
                    t_sil[tb] = self.op("scalar", "activation",
                                        gat[opar][:, :],
                                        pre[opar][:, :], AF.Silu, bias=0.0,
                                        scale=self.c_mp[:, tb:tb + 1],
                                        wait=(t_pre[0], t_pre[1],
                                              t_m_prev.get(opar)))
                    if tb + 4 < NTB:
                        prefetch(tb + 4)
                    if tb > 0:
                        emit_out(tb - 1)
                if g >= 2:
                    free[par] = (t_out_tok[(g - 2) * GRP + 1],)
            emit_out(NTB - 1)

    def finalize(self):
        with self.phase():
            pass



_CACHED = {}


def get_nc(B=2, T=2048, debug=False, upto=99):
    key = (B, T, debug, upto)
    if key not in _CACHED:
        _CACHED[key] = MK(B, T, debug, upto).build()
    return _CACHED[key]


def kernel(**inputs):
    nc = get_nc()
    in_maps = [host_inputs(c, inputs) for c in range(NCORES)]
    res = run_bass_kernel_spmd(nc, in_maps, core_ids=list(range(NCORES)))
    out = np.concatenate([r["y"] for r in res.results], axis=0)
    return out.reshape(16, 2048, D)
